# version 01
# baseline (speedup 1.0000x reference)
import numpy as np
import concourse.bass as bass
import concourse.mybir as mybir
from concourse import bass_utils as _bu

F32 = mybir.dt.float32
AF = mybir.ActivationFunctionType
OP = mybir.AluOpType
AX = mybir.AxisListType

D = 1024
T = 4096
TH = 2048
NEG = -1.0e30
FMIN = -3.0e38
GROUPS = [[0, 1], [2, 3], [4, 5], [6, 7]]

STAGE = 4


def build_nc(stage=None):
    if stage is None:
        stage = STAGE
    nc = bass.Bass("TRN2", target_bir_lowering=False)

    def din(name, shape):
        return nc.dram_tensor(name, shape, F32, kind="ExternalInput").ap()

    x_in = din("x", [8, 128, TH])
    l0_wqk = din("l0_wqk", [128, 128])
    l0_wv = din("l0_wv", [128, 64])
    l0_bqk = din("l0_bqk", [128, 1])
    l0_sqk = din("l0_sqk", [128, 1])
    l0_bv = din("l0_bv", [128, 1024])
    l0_wr = din("l0_wr", [4, 128, D])
    l0_w1 = din("l0_w1", [32, 128, D])
    l0_b1 = din("l0_b1", [128, 32])
    l0_w2 = din("l0_w2", [8, 128, 4096])
    l1_wqkv = din("l1_wqkv", [8, 128, 3072])
    l1_bv = din("l1_bv", [128, 1024])
    l1_w1p = din("l1_w1p", [8, 128, 64])
    l1_w2p = din("l1_w2p", [8, 128, 64])
    l1_ps96 = din("l1_ps96", [96, 1])
    l1_wout = din("l1_wout", [4, 128, D])
    m_gatew = din("m_gatew", [8, 128, 16])
    m_gateb = din("m_gateb", [128, 16])
    m_we1 = din("m_we1", [16, 8, 128, 512])
    m_be1 = din("m_be1", [16, 128, 4])
    m_we3 = din("m_we3", [16, 8, 128, 512])
    m_be3 = din("m_be3", [16, 128, 4])
    m_we2 = din("m_we2", [16, 4, 128, D])
    m_be2 = din("m_be2", [16, D])
    m_ws1 = din("m_ws1", [8, 128, 1024])
    m_ws3 = din("m_ws3", [8, 128, 1024])
    m_ws2 = din("m_ws2", [8, 128, D])
    b_pack = din("b_pack", [128, 64])
    c_ident = din("c_ident", [128, 128])
    c_tril = din("c_tril", [128, 128])
    c_triuneg = din("c_triuneg", [128, 128])
    c_spi = din("c_spi", [64, 96])
    c_spj = din("c_spj", [64, 96])
    c_g96 = din("c_g96", [96, 16])
    c_gt96 = din("c_gt96", [16, 96])
    c_bj = din("c_bj", [96, 96])
    c_ones1 = din("c_ones1", [1, 128])
    c_onesd = din("c_onesd", [128, 1])
    c_u32 = din("c_u32", [32, 32])
    c_eps = din("c_eps", [1, 2])
    U32 = mybir.dt.uint32
    c_qi0 = nc.dram_tensor("c_qi0", [64, 32], U32, kind="ExternalInput").ap()
    c_ki0 = nc.dram_tensor("c_ki0", [64, 32], U32, kind="ExternalInput").ap()
    c_qi1 = nc.dram_tensor("c_qi1", [70, 32], U32, kind="ExternalInput").ap()
    c_ki1 = nc.dram_tensor("c_ki1", [70, 32], U32, kind="ExternalInput").ap()
    c_vi = nc.dram_tensor("c_vi", [128, 512], U32, kind="ExternalInput").ap()
    c_sel16 = din("c_sel16", [16, 2048])

    out = nc.dram_tensor("out", [8, 128, TH], F32, kind="ExternalOutput").ap()

    aq0 = nc.dram_tensor("aq0", [2048, TH], F32).ap()
    aq0g = nc.dram_tensor("aq0g", [2, 2048, TH], F32).ap()
    aq1 = nc.dram_tensor("aq1", [2240, TH], F32).ap()
    aq1g = nc.dram_tensor("aq1g", [2, 2240, TH], F32).ap()
    av = nc.dram_tensor("av", [16, 128, 1024], F32).ap()
    avg = nc.dram_tensor("avg", [2, 16, 128, 1024], F32).ap()
    om = nc.dram_tensor("om", [512, T], F32).ap()
    rs_s = nc.dram_tensor("rs_s", [2, 8, 128, TH], F32).ap()
    rs_d = nc.dram_tensor("rs_d", [8, 128, TH], F32).ap()
    bw1t = nc.dram_tensor("bw1t", [32, 128, 1024], F32).ap()
    bw2t = nc.dram_tensor("bw2t", [32, 128, 1024], F32).ap()

    ctx = []

    def S(name, shape):
        cm = nc.sbuf_tensor(name, shape, F32)
        t = cm.__enter__()
        ctx.append(cm)
        return t

    x_sb = S("x_sb", [128, 8 * TH])
    rega = S("rega", [128, 8 * TH])
    regb = S("regb", [128, 12 * 1024])
    ident = S("ident", [128, 128])
    tril = S("tril", [128, 128])
    triun = S("triun", [128, 128])
    spi = S("spi", [64, 96])
    spj = S("spj", [64, 96])
    g96 = S("g96", [96, 16])
    gt96 = S("gt96", [16, 96])
    bjc = S("bjc", [96, 96])
    ones1 = S("ones1", [1, 128])
    onesd = S("onesd", [128, 1])
    u32 = S("u32", [32, 32])
    ps96 = S("ps96", [96, 1])
    wqk0 = S("wqk0", [128, 128])
    wv0 = S("wv0", [128, 64])
    bqk0 = S("bqk0", [128, 1])
    sqk0 = S("sqk0", [128, 1])
    bqk0s = S("bqk0s", [128, 1])
    gateb = S("gateb", [128, 16])
    bpk = S("bpk", [128, 64])
    b1sb = S("b1sb", [128, 32])
    be1sb = S("be1sb", [128, 4])
    be3sb = S("be3sb", [128, 4])
    stat = S("stat", [128, 24])
    srow = S("srow", [1, 2048])
    gcf = S("gcf", [16, TH])
    ceps = S("ceps", [1, 2])
    _u32ctx = []
    def SU(name, shape):
        cm = nc.sbuf_tensor(name, shape, mybir.dt.uint32)
        t = cm.__enter__()
        ctx.append(cm)
        return t
    qi0s = SU("qi0s", [64, 32])
    ki0s = SU("ki0s", [64, 32])
    qi1s = SU("qi1s", [70, 32])
    ki1s = SU("ki1s", [70, 32])
    vis = SU("vis", [128, 512])
    sel16 = S("sel16", [16, 2048])
    glt = regb[:, 8192:9216]
    gw_sb = regb[:, 9216:9344]
    bsum = regb[0:64, 8448:8480]
    bsumT = regb[0:32, 8480:8544]
    vsufT = regb[0:32, 8544:8608]

    pbcm = [nc.psum_tensor(f"pb{i}", [128, 512], F32) for i in range(8)]
    pbt = [p.__enter__() for p in pbcm]
    ctx.extend(pbcm)

    sems = {}
    for sname in ["ld", "st", "tt", "vv", "ss", "gg", "oo", "sq0", "sq1",
                  "sv0", "sv1", "sv2", "sv3", "lv0", "lv1", "lw2", "lw3",
                  "sto", "lg"]:
        cm = nc.semaphore(sname)
        sems[sname] = [cm.__enter__(), 0]
        ctx.append(cm)

    def inc(obj, s, k=1):
        obj.then_inc(sems[s][0], k)
        sems[s][1] += k
        return sems[s][1]

    def cnt(s):
        return sems[s][1]

    blkcm = nc.Block()
    block = blkcm.__enter__()

    ENGS = ["sync", "tensor", "vector", "scalar", "gpsimd"]

    def barrier(engs=ENGS, on=("ld", "st", "tt", "vv", "ss", "gg", "sq0",
                               "sq1", "sv0", "sv1", "sv2", "sv3", "lv0",
                               "lv1", "lw2", "lw3", "sto", "lg")):
        snap = [(s, sems[s][1]) for s in on if sems[s][1] > 0]
        for e in engs:
            dec = getattr(block, e)

            @dec
            def _(eng):
                for s, v in snap:
                    eng.wait_ge(sems[s][0], v)

    def bcol(i):
        return bpk[:, i:i + 1]

    def x_ap(c, t0, w=512):
        return x_sb[:, c * TH + t0: c * TH + t0 + w]

    def xln_ap(c, t0, w=512):
        return rega[:, c * TH + t0: c * TH + t0 + w]

    @block.sync
    def _(sync):
        for c in range(8):
            inc(sync.dma_start(out=x_sb[:, c * TH:(c + 1) * TH], in_=x_in[c]), "ld", 16)
        for dst, src in [(ident, c_ident), (tril, c_tril), (triun, c_triuneg),
                         (spi, c_spi), (spj, c_spj), (g96, c_g96),
                         (gt96, c_gt96), (bjc, c_bj), (ones1, c_ones1),
                         (onesd, c_onesd), (u32, c_u32), (ps96, l1_ps96),
                         (wqk0, l0_wqk), (wv0, l0_wv), (bqk0, l0_bqk),
                         (sqk0, l0_sqk), (gateb, m_gateb), (bpk, b_pack),
                         (b1sb, l0_b1), (ceps, c_eps), (sel16, c_sel16),
                         (qi0s, c_qi0), (ki0s, c_ki0), (qi1s, c_qi1),
                         (ki1s, c_ki1), (vis, c_vi)]:
            inc(sync.dma_start(out=dst[:], in_=src), "ld", 16)
    LD0 = cnt("ld")

    @block.vector
    def _(vector):
        vector.wait_ge(sems["ld"][0], LD0)
        vector.tensor_tensor(out=bqk0s[:], in0=bqk0[:], in1=sqk0[:], op=OP.mult)
        inc(vector.tensor_scalar(out=bpk[:, 32:40], in0=bpk[:, 32:40],
                                 scalar1=0.125, scalar2=None, op0=OP.mult), "vv")
    barrier()

    def emit_ln(srcf, dstf, resf, tokw, sq_off, tmp_off):
        tprev = cnt("tt")

        @block.vector
        def _(vector):
            vector.wait_ge(sems["tt"][0], tprev)
            vector.drain()
            for c in range(8):
                s = srcf(c)
                inc(vector.tensor_tensor(
                    out=regb[:, sq_off + c * tokw: sq_off + (c + 1) * tokw],
                    in0=s, in1=s, op=OP.mult), "vv")
        vsq = cnt("vv")

        @block.tensor
        def _(tensor):
            tensor.wait_ge(sems["vv"][0], vsq)
            for c in range(8):
                inc(tensor.matmul(pbt[0][0:1, 0:tokw], onesd[:], srcf(c),
                                  start=(c == 0), stop=(c == 7)), "tt")
            for c in range(8):
                inc(tensor.matmul(
                    pbt[1][0:1, 0:tokw], onesd[:],
                    regb[:, sq_off + c * tokw: sq_off + (c + 1) * tokw],
                    start=(c == 0), stop=(c == 7)), "tt")
        tstat = cnt("tt")

        @block.vector
        def _(vector):
            vector.wait_ge(sems["tt"][0], tstat)
            vector.tensor_scalar(out=srow[0:1, 0:tokw], in0=pbt[0][0:1, 0:tokw],
                                 scalar1=1.0 / 1024.0, scalar2=None, op0=OP.mult)
            vector.drain()
            vector.tensor_tensor(out=srow[0:1, 512:512 + tokw], in0=srow[0:1, 0:tokw],
                                 in1=srow[0:1, 0:tokw], op=OP.mult)
            vector.drain()
            inc(vector.scalar_tensor_tensor(
                out=srow[0:1, 512:512 + tokw], in0=pbt[1][0:1, 0:tokw],
                scalar=1.0 / 1024.0, in1=srow[0:1, 512:512 + tokw],
                op0=OP.mult, op1=OP.subtract), "vv")
        vstat = cnt("vv")

        @block.scalar
        def _(scalar):
            scalar.wait_ge(sems["vv"][0], vstat)
            inc(scalar.activation(out=srow[0:1, 1024:1024 + tokw],
                                  in_=srow[0:1, 512:512 + tokw],
                                  func=AF.Sqrt, bias=ceps[0:1, 1:2]), "ss")
        sstat = cnt("ss")

        @block.vector
        def _(vector):
            vector.wait_ge(sems["ss"][0], sstat)
            inc(vector.reciprocal(out=srow[0:1, 1536:1536 + tokw],
                                  in_=srow[0:1, 1024:1024 + tokw]), "vv")
        vrec = cnt("vv")

        @block.tensor
        def _(tensor):
            tensor.wait_ge(sems["vv"][0], vrec)
            inc(tensor.matmul(pbt[2][:, 0:tokw], ones1[:], srow[0:1, 0:tokw],
                              start=True, stop=True), "tt")
            inc(tensor.matmul(pbt[3][:, 0:tokw], ones1[:], srow[0:1, 1536:1536 + tokw],
                              start=True, stop=True), "tt")
        tbc = cnt("tt")

        @block.vector
        def _(vector):
            vector.wait_ge(sems["tt"][0], tbc)
            for c in range(8):
                tmp = regb[:, tmp_off: tmp_off + tokw]
                if c > 0:
                    vector.drain()
                vector.tensor_tensor(out=tmp, in0=srcf(c), in1=pbt[2][:, 0:tokw],
                                     op=OP.subtract)
                vector.drain()
                d = dstf(c) if dstf is not None else None
                if d is not None:
                    mm = vector.tensor_tensor(out=d, in0=tmp, in1=pbt[3][:, 0:tokw],
                                              op=OP.mult)
                    lnres = d
                else:
                    mm = vector.tensor_tensor(out=tmp, in0=tmp, in1=pbt[3][:, 0:tokw],
                                              op=OP.mult)
                    lnres = tmp
                if resf is not None:
                    vector.drain()
                    r = resf(c)
                    mm = vector.tensor_tensor(out=r, in0=r, in1=lnres, op=OP.add)
                inc(mm, "vv")
        vapply = cnt("vv")

        @block.tensor
        def _(tensor):
            tensor.wait_ge(sems["vv"][0], vapply)

    def emit_preln(t0_list, tokw=512):
        for t0 in t0_list:
            emit_ln(lambda c, t0=t0: x_ap(c, t0, tokw),
                    lambda c, t0=t0: xln_ap(c, t0, tokw),
                    None, tokw, 0, 8 * tokw)

    def emit_qkv_l0():
        evac_ms = {}
        dma_ms = {}
        for idx in range(64):
            j, n = idx // 4, idx % 4
            base = 64 * (j % 2)
            ch = j // 2
            t0 = n * 512

            @block.tensor
            def _(tensor, idx=idx, j=j, n=n, base=base, ch=ch, t0=t0):
                if idx >= 2:
                    tensor.wait_ge(sems["ss"][0], evac_ms[idx - 2])
                inc(tensor.matmul(pbt[idx % 2][:, :],
                                  wqk0[base:base + 64, 0:128],
                                  xln_ap(ch, t0)[base:base + 64, :],
                                  start=True, stop=True), "tt")
            tqk = cnt("tt")

            @block.scalar
            def _(scalar, idx=idx, j=j, t0=t0, tqk=tqk):
                scalar.wait_ge(sems["tt"][0], tqk)
                sq = "sq%d" % (idx % 2)
                if idx >= 2:
                    scalar.wait_ge(sems[sq][0], dma_ms[idx - 2])
                stg = regb[:, 8192 + (idx % 2) * 512: 8192 + (idx % 2) * 512 + 512]
                ev = scalar.activation(out=stg, in_=pbt[idx % 2][:, :],
                                       func=AF.Identity, bias=bqk0s[:],
                                       scale=sqk0[:])
                evac_ms[idx] = inc(ev, "ss")
                scalar.drain()
                inc(scalar.dma_start(out=aq0[64 * j:64 * j + 64, t0:t0 + 512],
                                     in_=stg[0:64, :]), sq, 16)
                dma_ms[idx] = inc(scalar.dma_start(
                    out=aq0[1024 + 64 * j:1024 + 64 * j + 64, t0:t0 + 512],
                    in_=stg[64:128, :]), sq, 16)
        emit_v_common(wv0, l0_bv)
        barrier()

    def emit_v_common(wv_tile, bv_src):
        @block.sync
        def _(sync):
            inc(sync.dma_start(out=regb[:, 11264:12288], in_=bv_src), "ld", 16)
        lbv = cnt("ld")
        bv_tile = regb[:, 11264:12288]
        for tg in range(4):
            vprev = cnt("vv")
            sprev = cnt("ss")

            @block.tensor
            def _(tensor, tg=tg, vprev=vprev, sprev=sprev):
                tensor.wait_ge(sems["vv"][0], vprev)
                if tg == 0:
                    tensor.wait_ge(sems["ss"][0], sprev)
                for j in range(16):
                    base = 64 * (j % 2)
                    ch = j // 2
                    for tt in range(4):
                        a = tg * 4 + tt
                        mm = tensor.matmul(
                            pbt[tt * 2 + (j // 8)][:, (j % 8) * 64:(j % 8) * 64 + 64],
                            xln_ap(ch, a * 128, 128)[base:base + 64, :],
                            wv_tile[base:base + 64, :],
                            start=True, stop=True)
                inc(mm, "tt")
            tv = cnt("tt")
            for tt in range(4):
                a = tg * 4 + tt

                @block.vector
                def _(vector, tt=tt, tv=tv, tg=tg):
                    vector.wait_ge(sems["tt"][0], tv)
                    vector.wait_ge(sems["ld"][0], lbv)
                    if tg > 0:
                        vector.wait_ge(sems["sv%d" % tt][0],
                                       emit_v_common.dma_ms.get((tg - 1, tt), 0))
                    stg = regb[:, tt * 1024: tt * 1024 + 1024]
                    inc(vector.tensor_tensor(out=stg[:, 0:512], in0=pbt[tt * 2][:, :],
                                             in1=bv_tile[:, 0:512], op=OP.add), "vv")
                    inc(vector.tensor_tensor(out=stg[:, 512:1024],
                                             in0=pbt[tt * 2 + 1][:, :],
                                             in1=bv_tile[:, 512:1024], op=OP.add), "vv")
                vva = cnt("vv")

                @block.sync
                def _(sync, tt=tt, a=a, vva=vva):
                    sync.wait_ge(sems["vv"][0], vva)
                    emit_v_common.dma_ms[(tg, tt)] = inc(
                        sync.dma_start(out=av[a],
                                       in_=regb[:, tt * 1024: tt * 1024 + 1024]),
                        "sv%d" % tt, 16)
    emit_v_common.dma_ms = {}

    def emit_qkv_l1():
        for mg in range(4):
            tprev = cnt("tt")

            @block.sync
            def _(sync, mg=mg, tprev=tprev):
                sync.wait_ge(sems["tt"][0], tprev)
                for c in range(8):
                    inc(sync.dma_start(
                        out=regb[:, (mg % 2) * 4096 + c * 512:
                                 (mg % 2) * 4096 + c * 512 + 512],
                        in_=l1_wqkv[c][:, mg * 512: mg * 512 + 512]), "ld", 16)
            lw = cnt("ld")
            for n in range(4):
                t0 = n * 512
                sprev = cnt("ss")

                @block.tensor
                def _(tensor, mg=mg, n=n, t0=t0, lw=lw, sprev=sprev):
                    if n == 0:
                        tensor.wait_ge(sems["ld"][0], lw)
                    if mg * 4 + n >= 1:
                        tensor.wait_ge(sems["ss"][0], emit_qkv_l1.ev.get(mg * 4 + n - 1, 0))
                    for mt in range(4):
                        for c in range(8):
                            mm = tensor.matmul(
                                pbt[4 + mt][:, :],
                                regb[:, (mg % 2) * 4096 + c * 512 + mt * 128:
                                     (mg % 2) * 4096 + c * 512 + mt * 128 + 128],
                                xln_ap(c, t0), start=(c == 0), stop=(c == 7))
                    inc(mm, "tt")
                tqk = cnt("tt")

                @block.scalar
                def _(scalar, mg=mg, n=n, t0=t0, tqk=tqk):
                    scalar.wait_ge(sems["tt"][0], tqk)
                    idx = mg * 4 + n
                    sq = "sq%d" % (idx % 2)
                    if idx >= 2:
                        scalar.wait_ge(sems[sq][0], emit_qkv_l1.dm.get(idx - 2, 0))
                    for mt in range(4):
                        gm = mg * 4 + mt
                        stg = regb[:, 8192 + ((idx % 2) * 4 + mt) * 512:
                                   8192 + ((idx % 2) * 4 + mt) * 512 + 512]
                        sc = 0.125 if gm < 8 else 1.0
                        inc(scalar.activation(out=stg, in_=pbt[4 + mt][:, :],
                                              func=AF.Identity, bias=bcol(32 + gm),
                                              scale=sc), "ss")
                    scalar.drain()
                    for mt in range(4):
                        gm = mg * 4 + mt
                        stg = regb[:, 8192 + ((idx % 2) * 4 + mt) * 512:
                                   8192 + ((idx % 2) * 4 + mt) * 512 + 512]
                        dm = scalar.dma_start(
                            out=aq1[128 * gm:128 * gm + 128, t0:t0 + 512], in_=stg)
                        inc(dm, sq, 16)
                    emit_qkv_l1.ev[idx] = cnt("ss")
                    emit_qkv_l1.dm[idx] = cnt(sq)
        barrier()
        @block.sync
        def _(sync):
            inc(sync.dma_start(out=regb[:, 11264:12288], in_=l1_bv), "ld", 16)
        lbv1 = cnt("ld")
        bv1t = regb[:, 11264:12288]
        for tg in range(4):
            for c in range(8):
                tprev = cnt("tt")

                @block.sync
                def _(sync, tg=tg, c=c, tprev=tprev):
                    if tg * 8 + c >= 2:
                        sync.wait_ge(sems["tt"][0], emit_qkv_l1.vt.get(tg * 8 + c - 2, 0))
                    inc(sync.dma_start(out=regb[:, 9216 + (c % 2) * 1024:
                                                9216 + (c % 2) * 1024 + 1024],
                                       in_=l1_wqkv[c][:, 2048:3072]), "lv%d" % (c % 2), 16)
                lv = cnt("lv%d" % (c % 2))

                @block.tensor
                def _(tensor, tg=tg, c=c, lv=lv):
                    tensor.wait_ge(sems["lv%d" % (c % 2)][0], lv)
                    if c == 0:
                        tensor.wait_ge(sems["vv"][0], cnt("vv"))
                    for tt in range(4):
                        a = tg * 4 + tt
                        for hh in range(2):
                            mm = tensor.matmul(
                                pbt[tt * 2 + hh][:, :],
                                xln_ap(c, a * 128, 128),
                                regb[:, 9216 + (c % 2) * 1024 + hh * 512:
                                     9216 + (c % 2) * 1024 + hh * 512 + 512],
                                start=(c == 0), stop=(c == 7))
                    emit_qkv_l1.vt[tg * 8 + c] = inc(mm, "tt")
            tv = cnt("tt")
            for tt in range(4):
                a = tg * 4 + tt

                @block.vector
                def _(vector, tt=tt, tv=tv, tg=tg):
                    vector.wait_ge(sems["tt"][0], tv)
                    vector.wait_ge(sems["ld"][0], lbv1)
                    if tg > 0:
                        vector.wait_ge(sems["sv%d" % tt][0],
                                       emit_qkv_l1.vdm.get((tg - 1, tt), 0))
                    stg = regb[:, tt * 1024: tt * 1024 + 1024]
                    inc(vector.tensor_tensor(out=stg[:, 0:512], in0=pbt[tt * 2][:, :],
                                             in1=bv1t[:, 0:512], op=OP.add), "vv")
                    inc(vector.tensor_tensor(out=stg[:, 512:1024],
                                             in0=pbt[tt * 2 + 1][:, :],
                                             in1=bv1t[:, 512:1024], op=OP.add), "vv")
                vva = cnt("vv")

                @block.sync
                def _(sync, tt=tt, a=a, vva=vva, tg=tg):
                    sync.wait_ge(sems["vv"][0], vva)
                    emit_qkv_l1.vdm[(tg, tt)] = inc(
                        sync.dma_start(out=av[a],
                                       in_=regb[:, tt * 1024: tt * 1024 + 1024]),
                        "sv%d" % tt, 16)
        barrier()
        emit_plucker()
        barrier()
    emit_qkv_l1.ev = {}
    emit_qkv_l1.dm = {}
    emit_qkv_l1.vt = {}
    emit_qkv_l1.vdm = {}

    def emit_plucker():
        w1p_sb = regb[:, 4096:4608]
        w2p_sb = regb[:, 4608:5120]

        @block.sync
        def _(sync):
            for c in range(8):
                inc(sync.dma_start(out=w1p_sb[:, c * 64:(c + 1) * 64], in_=l1_w1p[c]),
                    "ld", 16)
                inc(sync.dma_start(out=w2p_sb[:, c * 64:(c + 1) * 64], in_=l1_w2p[c]),
                    "ld", 16)
        lwp = cnt("ld")
        pev = {}
        for n in range(4):
            t0 = n * 512

            @block.tensor
            def _(tensor, n=n, t0=t0):
                if n == 0:
                    tensor.wait_ge(sems["ld"][0], lwp)
                else:
                    tensor.wait_ge(sems["ss"][0], pev[n - 1])
                for c in range(8):
                    tensor.matmul(pbt[0][0:64, :], w1p_sb[:, c * 64:(c + 1) * 64],
                                  xln_ap(c, t0), start=(c == 0), stop=(c == 7))
                    mm = tensor.matmul(pbt[1][0:64, :], w2p_sb[:, c * 64:(c + 1) * 64],
                                       xln_ap(c, t0), start=(c == 0), stop=(c == 7))
                inc(mm, "tt")
            tp = cnt("tt")

            @block.scalar
            def _(scalar, n=n, t0=t0, tp=tp):
                scalar.wait_ge(sems["tt"][0], tp)
                inc(scalar.activation(out=regb[0:64, t0:t0 + 512], in_=pbt[0][0:64, :],
                                      func=AF.Copy), "ss")
                inc(scalar.activation(out=regb[0:64, 2048 + t0:2048 + t0 + 512],
                                      in_=pbt[1][0:64, :], func=AF.Copy), "ss")
                pev[n] = cnt("ss")
        sp = cnt("ss")

        shev = {}
        for n in range(4):
            t0 = n * 512

            @block.tensor
            def _(tensor, n=n, t0=t0, sp=sp):
                tensor.wait_ge(sems["ss"][0], sp if n == 0 else shev[n - 1])
                tensor.matmul(pbt[2][0:96, :], spi[:], regb[0:64, t0:t0 + 512],
                              start=True, stop=True)
                tensor.matmul(pbt[3][0:96, :], spj[:],
                              regb[0:64, 2048 + t0:2048 + t0 + 512],
                              start=True, stop=True)
                tensor.matmul(pbt[4][0:96, :], spj[:], regb[0:64, t0:t0 + 512],
                              start=True, stop=True)
                inc(tensor.matmul(pbt[5][0:96, :], spi[:],
                                  regb[0:64, 2048 + t0:2048 + t0 + 512],
                                  start=True, stop=True), "tt")
            ts = cnt("tt")

            @block.scalar
            def _(scalar, n=n, t0=t0, ts=ts):
                scalar.wait_ge(sems["tt"][0], ts)
                inc(scalar.activation(out=regb[0:96, 4096 + t0:4096 + t0 + 512],
                                      in_=pbt[2][0:96, :], func=AF.Copy), "ss")
                inc(scalar.activation(out=regb[0:96, 6144 + t0:6144 + t0 + 512],
                                      in_=pbt[3][0:96, :], func=AF.Copy), "ss")
                inc(scalar.activation(out=regb[0:96, 8192 + t0:8192 + t0 + 512],
                                      in_=pbt[4][0:96, :], func=AF.Copy), "ss")
                inc(scalar.activation(out=regb[0:96, 10240 + t0:10240 + t0 + 512],
                                      in_=pbt[5][0:96, :], func=AF.Copy), "ss")
                shev[n] = cnt("ss")
        ssh = cnt("ss")

        @block.vector
        def _(vector):
            vector.wait_ge(sems["ss"][0], ssh)
            A1 = regb[0:96, 4096:6144]
            B2 = regb[0:96, 6144:8192]
            A2 = regb[0:96, 8192:10240]
            B1 = regb[0:96, 10240:12288]
            inc(vector.tensor_tensor(out=A1, in0=A1, in1=B2, op=OP.mult), "vv")
            inc(vector.tensor_tensor(out=A2, in0=A2, in1=B1, op=OP.mult), "vv")
            vector.drain()
            inc(vector.tensor_tensor(out=A1, in0=A1, in1=A2, op=OP.subtract), "vv")
            vector.drain()
            inc(vector.tensor_tensor(out=B2, in0=A1, in1=A1, op=OP.mult), "vv")
        vl = cnt("vv")

        nev = {}
        for n in range(4):
            t0 = n * 512

            @block.tensor
            def _(tensor, n=n, t0=t0, vl=vl):
                tensor.wait_ge(sems["vv"][0], vl)
                if n > 0:
                    tensor.wait_ge(sems["ss"][0], nev[n - 1])
                inc(tensor.matmul(pbt[0][0:16, :], g96[:],
                                  regb[0:96, 6144 + t0:6144 + t0 + 512],
                                  start=True, stop=True), "tt")
            tn = cnt("tt")

            @block.scalar
            def _(scalar, n=n, t0=t0, tn=tn):
                scalar.wait_ge(sems["tt"][0], tn)
                inc(scalar.activation(out=regb[0:16, 8192 + t0:8192 + t0 + 512],
                                      in_=pbt[0][0:16, :], func=AF.Sqrt), "ss")
                nev[n] = cnt("ss")
        sn = cnt("ss")

        @block.vector
        def _(vector):
            vector.wait_ge(sems["ss"][0], sn)
            nr = regb[0:16, 8192:10240]
            vector.tensor_scalar(out=nr, in0=nr, scalar1=1e-12, scalar2=None,
                                 op0=OP.max)
            vector.drain()
            inc(vector.reciprocal(out=regb[0:16, 10240:12288], in_=nr), "vv")
        vrn = cnt("vv")

        rnv = {}
        jlv = {}
        for n in range(4):
            t0 = n * 512

            @block.tensor
            def _(tensor, n=n, t0=t0, vrn=vrn):
                tensor.wait_ge(sems["vv"][0], vrn if n == 0 else rnv[n - 1])
                inc(tensor.matmul(pbt[1][0:96, :], gt96[:],
                                  regb[0:16, 10240 + t0:10240 + t0 + 512],
                                  start=True, stop=True), "tt")
            tb = cnt("tt")

            @block.vector
            def _(vector, n=n, t0=t0, tb=tb):
                vector.wait_ge(sems["tt"][0], tb)
                inc(vector.tensor_tensor(out=regb[0:96, 4096 + t0:4096 + t0 + 512],
                                         in0=regb[0:96, 4096 + t0:4096 + t0 + 512],
                                         in1=pbt[1][0:96, :], op=OP.mult), "vv")
            vno = cnt("vv")
            rnv[n] = vno

            @block.tensor
            def _(tensor, n=n, t0=t0, vno=vno):
                tensor.wait_ge(sems["vv"][0], vno)
                if n > 0:
                    tensor.wait_ge(sems["ss"][0], jlv[n - 1])
                inc(tensor.matmul(pbt[2][0:96, :], bjc[:],
                                  regb[0:96, 4096 + t0:4096 + t0 + 512],
                                  start=True, stop=True), "tt")
            tj = cnt("tt")

            @block.scalar
            def _(scalar, n=n, t0=t0, tj=tj):
                scalar.wait_ge(sems["tt"][0], tj)
                inc(scalar.activation(out=regb[0:96, 6144 + t0:6144 + t0 + 512],
                                      in_=pbt[2][0:96, :], func=AF.Copy), "ss")
                jlv[n] = cnt("ss")

            @block.vector
            def _(vector, n=n, t0=t0):
                vector.drain()
                inc(vector.tensor_scalar(out=regb[0:96, 8192 + t0:8192 + t0 + 512],
                                         in0=regb[0:96, 4096 + t0:4096 + t0 + 512],
                                         scalar1=ps96[:], scalar2=None,
                                         op0=OP.mult), "vv")
        sJ = cnt("ss")
        vL = cnt("vv")

        @block.sync
        def _(sync):
            sync.wait_ge(sems["ss"][0], sJ)
            sync.wait_ge(sems["vv"][0], vL)
            inc(sync.dma_start(out=aq1[2144:2240, :], in_=regb[0:96, 6144:8192]),
                "st", 16)
            inc(sync.dma_start(out=aq1[2048:2144, :], in_=regb[0:96, 8192:10240]),
                "st", 16)

    def emit_gather(layer):
        aq_src = aq0 if layer == 0 else aq1
        aq_dst = aq0g if layer == 0 else aq1g

        @block.gpsimd
        def _(gpsimd):
            for s in ("st", "sq0", "sq1", "sv0", "sv1", "sv2", "sv3"):
                if cnt(s):
                    gpsimd.wait_ge(sems[s][0], cnt(s))
            inc(gpsimd.collective_compute(
                "AllGather", OP.bypass, replica_groups=GROUPS,
                ins=[aq_src.opt()], outs=[aq_dst.opt()]), "gg")
            inc(gpsimd.collective_compute(
                "AllGather", OP.bypass, replica_groups=GROUPS,
                ins=[av.opt()], outs=[avg.opt()]), "gg")
        barrier()

    def emit_attention(layer):
        aqg = aq0g if layer == 0 else aq1g
        KR = 64 if layer == 0 else 70
        qis = qi0s if layer == 0 else qi1s
        kis = ki0s if layer == 0 else ki1s
        aqg_flat = aqg.rearrange("s r c -> (s r) c")
        avg_flat = avg.rearrange("s a p (u c) -> (s a p u) c", c=64)
        lgs = regb[:, 4096:8192]
        for j in range(8):
            tprev = cnt("tt")
            sprev = cnt("ss")

            @block.gpsimd
            def _(gpsimd, j=j, tprev=tprev, sprev=sprev, qis=qis, kis=kis,
                  aqg_flat=aqg_flat, avg_flat=avg_flat, KR=KR):
                gpsimd.wait_ge(sems["tt"][0], tprev)
                gpsimd.wait_ge(sems["ss"][0], sprev)
                for s in range(2):
                    inc(gpsimd.indirect_dma_start(
                        out=rega[0:KR, s * 2048:(s + 1) * 2048], out_offset=None,
                        in_=aqg_flat,
                        in_offset=bass.IndirectOffsetOnAxis(
                            ap=qis[:, 2 * j + s:2 * j + s + 1], axis=0)), "lg", 16)
                    inc(gpsimd.indirect_dma_start(
                        out=rega[0:KR, 4096 + s * 2048:4096 + (s + 1) * 2048],
                        out_offset=None, in_=aqg_flat,
                        in_offset=bass.IndirectOffsetOnAxis(
                            ap=kis[:, 2 * j + s:2 * j + s + 1], axis=0)), "lg", 16)
                    for a in range(16):
                        col = j * 32 + s * 16 + a
                        inc(gpsimd.indirect_dma_start(
                            out=rega[:, 8192 + (s * 16 + a) * 64:
                                     8192 + (s * 16 + a) * 64 + 64],
                            out_offset=None, in_=avg_flat,
                            in_offset=bass.IndirectOffsetOnAxis(
                                ap=vis[:, col:col + 1], axis=0)), "lg", 16)
            lh = cnt("lg")

            if layer == 0:

                @block.tensor
                def _(tensor, lh=lh):
                    tensor.wait_ge(sems["lg"][0], lh)
                    for g in range(32):
                        mm = tensor.matmul(pbt[6][0:64, g:g + 1],
                                           rega[:, 8192 + g * 64:8192 + g * 64 + 64],
                                           onesd[:, 0:1], start=True, stop=True)
                    inc(mm, "tt")
                tbs = cnt("tt")

                @block.scalar
                def _(scalar, tbs=tbs):
                    scalar.wait_ge(sems["tt"][0], tbs)
                    inc(scalar.activation(out=bsum[:], in_=pbt[6][0:64, 0:32],
                                          func=AF.Copy), "ss")
                sbs = cnt("ss")

                @block.tensor
                def _(tensor, sbs=sbs):
                    tensor.wait_ge(sems["ss"][0], sbs)
                    inc(tensor.transpose(pbt[5][0:32, 0:64], bsum[:],
                                         ident[0:64, 0:64]), "tt")
                tbt = cnt("tt")

                @block.scalar
                def _(scalar, tbt=tbt):
                    scalar.wait_ge(sems["tt"][0], tbt)
                    inc(scalar.activation(out=bsumT[:], in_=pbt[5][0:32, 0:64],
                                          func=AF.Copy), "ss")
                sbt = cnt("ss")

                @block.tensor
                def _(tensor, sbt=sbt):
                    tensor.wait_ge(sems["ss"][0], sbt)
                    inc(tensor.matmul(pbt[7][0:32, 0:64], u32[:], bsumT[:],
                                      start=True, stop=True), "tt")
                tvs = cnt("tt")

                @block.scalar
                def _(scalar, tvs=tvs):
                    scalar.wait_ge(sems["tt"][0], tvs)
                    inc(scalar.activation(out=vsufT[:], in_=pbt[7][0:32, 0:64],
                                          func=AF.Copy), "ss")
                svs = cnt("ss")

                @block.sync
                def _(sync, svs=svs):
                    sync.wait_ge(sems["ss"][0], svs)
                    inc(sync.dma_start(out=rega[0:1, 14336:16384], in_=vsufT[:]),
                        "ld", 16)
                vsuf_ld = cnt("ld")
            else:
                vsuf_ld = 0

            evpt_ms = {}
            exp_ms = {}
            nrm_ms = {}
            oev_ms = {}
            for i in range(32):
                K_i = 128 * (i + 1)
                ncH = (K_i + 511) // 512
                nsuf = T - K_i

                for kc in range(ncH):
                    w = min(512, K_i - kc * 512)
                    gidx = emit_attention.qk_idx
                    emit_attention.qk_idx += 1
                    bank = gidx % 2

                    @block.tensor
                    def _(tensor, i=i, kc=kc, w=w, j=j, lh=lh, gidx=gidx, bank=bank):
                        if i == 0 and kc == 0:
                            tensor.wait_ge(sems["lg"][0], lh)
                        if i >= 1 and kc == 0:
                            tensor.wait_ge(sems["ss"][0], exp_ms[i - 1])
                        prev = emit_attention.qk_hist.get(gidx - 2)
                        if prev is not None:
                            tensor.wait_ge(sems["vv"][0], prev)
                        inc(tensor.matmul(pbt[bank][:, 0:w],
                                          rega[0:KR, 128 * i:128 * i + 128],
                                          rega[0:KR, 4096 + kc * 512:4096 + kc * 512 + w],
                                          start=True, stop=True), "tt")
                    tq = cnt("tt")

                    @block.vector
                    def _(vector, i=i, kc=kc, w=w, tq=tq, bank=bank, gidx=gidx):
                        vector.wait_ge(sems["tt"][0], tq)
                        mm = vector.tensor_scalar(
                            out=lgs[:, kc * 512:kc * 512 + w],
                            in0=pbt[bank][:, 0:w],
                            scalar1=0.0, scalar2=FMIN, op0=OP.add, op1=OP.max,
                            accum_out=stat[:, 8 + kc:9 + kc])
                        emit_attention.qk_hist[gidx] = inc(mm, "vv")

                @block.vector
                def _(vector, i=i, ncH=ncH, layer=layer):
                    vector.drain()
                    if layer == 0:
                        vector.tensor_tensor(out=lgs[:, 128 * i:128 * i + 128],
                                             in0=lgs[:, 128 * i:128 * i + 128],
                                             in1=tril[:], op=OP.mult)
                    else:
                        vector.tensor_tensor(out=lgs[:, 128 * i:128 * i + 128],
                                             in0=lgs[:, 128 * i:128 * i + 128],
                                             in1=triun[:], op=OP.add)
                    vector.reduce_max(out=stat[:, 16:17], in_=stat[:, 8:8 + ncH],
                                      axis=AX.X)
                    vector.drain()
                    if layer == 0:
                        inc(vector.tensor_scalar(out=stat[:, 17:18], in0=stat[:, 16:17],
                                                 scalar1=0.0, scalar2=-1.0,
                                                 op0=OP.max, op1=OP.mult), "vv")
                    else:
                        inc(vector.tensor_scalar(out=stat[:, 17:18], in0=stat[:, 16:17],
                                                 scalar1=-1.0, scalar2=None, op0=OP.mult),
                            "vv")
                vmask = cnt("vv")

                @block.scalar
                def _(scalar, i=i, K_i=K_i, vmask=vmask, layer=layer):
                    scalar.wait_ge(sems["vv"][0], vmask)
                    scalar.wait_ge(sems["tt"][0], emit_attention.last_pused)
                    ex = scalar.activation(out=regb[:, 0:K_i], in_=lgs[:, 0:K_i],
                                           func=AF.Exp, bias=stat[:, 17:18],
                                           accum_out=stat[:, 18:19])
                    if layer == 0:
                        scalar.drain()
                        ex = scalar.activation(out=stat[:, 19:20], in_=stat[:, 17:18],
                                               func=AF.Exp)
                    exp_ms[i] = inc(ex, "ss")

                @block.vector
                def _(vector, i=i, K_i=K_i, nsuf=nsuf, layer=layer):
                    vector.wait_ge(sems["ss"][0], exp_ms[i])
                    if layer == 0 and nsuf > 0:
                        vector.scalar_tensor_tensor(out=stat[:, 18:19], in0=stat[:, 19:20],
                                                    scalar=float(nsuf), in1=stat[:, 18:19],
                                                    op0=OP.mult, op1=OP.add)
                        vector.drain()
                    vector.reciprocal(out=stat[:, 20:21], in_=stat[:, 18:19])
                    vector.drain()
                    inc(vector.tensor_scalar(out=regb[:, 0:K_i], in0=regb[:, 0:K_i],
                                             scalar1=stat[:, 20:21], scalar2=None,
                                             op0=OP.mult), "vv")
                    if layer == 0:
                        inc(vector.tensor_tensor(out=stat[:, 21:22], in0=stat[:, 19:20],
                                                 in1=stat[:, 20:21], op=OP.mult), "vv")
                    nrm_ms[i] = cnt("vv")

                if layer == 0 and nsuf > 0:

                    @block.tensor
                    def _(tensor, i=i):
                        tensor.wait_ge(sems["vv"][0], nrm_ms[i])
                        inc(tensor.transpose(pbt[5][0:1, 0:128], stat[:, 21:22],
                                             ident[:]), "tt")
                    tem = cnt("tt")

                    @block.scalar
                    def _(scalar, i=i, tem=tem):
                        scalar.wait_ge(sems["tt"][0], tem)
                        inc(scalar.activation(out=srow[0:1, 0:128],
                                              in_=pbt[5][0:1, 0:128],
                                              func=AF.Copy), "ss")
                    emit_attention.emn_ms[i] = cnt("ss")

                for g in range(i + 1):
                    ptg = emit_attention.pt_idx
                    emit_attention.pt_idx += 1

                    @block.tensor
                    def _(tensor, i=i, g=g, ptg=ptg):
                        if g == 0:
                            tensor.wait_ge(sems["vv"][0], nrm_ms[i])
                        prev = emit_attention.pt_hist.get(ptg - 2)
                        if prev is not None:
                            tensor.wait_ge(sems["ss"][0], prev)
                        inc(tensor.transpose(pbt[2 + ptg % 2][:, 0:128],
                                             regb[:, 128 * g:128 * g + 128],
                                             ident[:]), "tt")
                    tpt = cnt("tt")

                    @block.scalar
                    def _(scalar, i=i, g=g, tpt=tpt, ptg=ptg):
                        scalar.wait_ge(sems["tt"][0], tpt)
                        emit_attention.pt_hist[ptg] = inc(scalar.activation(
                            out=regb[:, 8192 + (ptg % 2) * 128:8192 + (ptg % 2) * 128 + 128],
                            in_=pbt[2 + ptg % 2][:, 0:128], func=AF.Copy), "ss")

                    @block.tensor
                    def _(tensor, i=i, g=g, layer=layer, nsuf=nsuf, ptg=ptg):
                        tensor.wait_ge(sems["ss"][0], emit_attention.pt_hist[ptg])
                        if g == 0 and i >= 1:
                            tensor.wait_ge(sems["ss"][0], oev_ms[i - 1])
                        last = (g == i) and not (layer == 0 and nsuf > 0)
                        mm = tensor.matmul(pbt[4][0:64, 0:128],
                                           rega[:, 8192 + g * 64:8192 + g * 64 + 64],
                                           regb[:, 8192 + (ptg % 2) * 128:8192 + (ptg % 2) * 128 + 128],
                                           start=(g == 0), stop=last)
                        if g == i and not (layer == 0 and nsuf > 0):
                            inc(mm, "tt")

                if layer == 0 and nsuf > 0:

                    @block.tensor
                    def _(tensor, i=i, vsuf_ld=vsuf_ld):
                        tensor.wait_ge(sems["ss"][0], emit_attention.emn_ms[i])
                        tensor.wait_ge(sems["ld"][0], vsuf_ld)
                        inc(tensor.matmul(pbt[4][0:64, 0:128],
                                          rega[0:1, 14336 + 64 * i:14336 + 64 * i + 64],
                                          srow[0:1, 0:128],
                                          start=False, stop=True), "tt")
                tav = cnt("tt")
                emit_attention.last_pused = tav

                @block.scalar
                def _(scalar, i=i, tav=tav):
                    scalar.wait_ge(sems["tt"][0], tav)
                    if i == 0:
                        scalar.wait_ge(sems["sto"][0], cnt("sto"))
                    oev_ms[i] = inc(scalar.activation(
                        out=rega[0:64, 10240 + 128 * i:10240 + 128 * i + 128],
                        in_=pbt[4][0:64, 0:128], func=AF.Copy), "ss")

            @block.sync
            def _(sync, j=j):
                sync.wait_ge(sems["ss"][0], oev_ms[31])
                inc(sync.dma_start(out=om[64 * j:64 * j + 64, :],
                                   in_=rega[0:64, 10240:14336]), "sto", 16)
        barrier()
    emit_attention.qk_hist = {}
    emit_attention.qk_idx = 0
    emit_attention.pt_hist = {}
    emit_attention.pt_idx = 0
    emit_attention.last_pused = 0
    emit_attention.emn_ms = {}

    def emit_oproj(layer):
        wsrc = l0_wr if layer == 0 else l1_wout

        @block.sync
        def _(sync):
            for c in range(4):
                inc(sync.dma_start(out=regb[:, 1024 + c * D:1024 + (c + 1) * D],
                                   in_=wsrc[c]), "ld", 16)
        lwr = cnt("ld")
        for tk in range(8):
            for c in range(4):
                idx = tk * 4 + c

                @block.sync
                def _(sync, tk=tk, c=c, idx=idx):
                    if idx >= 2:
                        sync.wait_ge(sems["tt"][0], emit_oproj.tms.get(idx - 2, 0))
                    inc(sync.dma_start(
                        out=regb[:, (idx % 2) * 512:(idx % 2) * 512 + 512],
                        in_=om[128 * c:128 * c + 128, 512 * tk:512 * tk + 512]),
                        "lv%d" % (idx % 2), 16)
                lom = cnt("lv%d" % (idx % 2))

                @block.tensor
                def _(tensor, tk=tk, c=c, idx=idx, lom=lom, lwr=lwr):
                    tensor.wait_ge(sems["lv%d" % (idx % 2)][0], lom)
                    tensor.wait_ge(sems["ld"][0], lwr)
                    if c == 0:
                        tensor.wait_ge(sems["ss"][0], emit_oproj.sms.get(tk - 1, 0))
                    for m in range(8):
                        mm = tensor.matmul(
                            pbt[m][:, :],
                            regb[:, 1024 + c * D + 128 * m:1024 + c * D + 128 * m + 128],
                            regb[:, (idx % 2) * 512:(idx % 2) * 512 + 512],
                            start=(c == 0), stop=(c == 3))
                    emit_oproj.tms[idx] = inc(mm, "tt")
            tpr = cnt("tt")

            @block.scalar
            def _(scalar, tk=tk, tpr=tpr):
                scalar.wait_ge(sems["tt"][0], tpr)
                for m in range(8):
                    su = emit_oproj.slot_use
                    emit_oproj.slot_use += 1
                    prev = emit_oproj.slot_dm.get(su - 4)
                    if prev is not None:
                        scalar.wait_ge(sems["sv%d" % (su % 4)][0], prev)
                    stg = regb[:, 5120 + (su % 4) * 512:5120 + (su % 4) * 512 + 512]
                    inc(scalar.activation(out=stg, in_=pbt[m][:, :], func=AF.Copy), "ss")
                    scalar.drain()
                    dm = scalar.dma_start(
                        out=rs_s[tk // 4, m, :, (tk % 4) * 512:(tk % 4) * 512 + 512],
                        in_=stg)
                    emit_oproj.slot_dm[su] = inc(dm, "sv%d" % (su % 4), 16)
                emit_oproj.sms[tk] = cnt("ss")

        @block.gpsimd
        def _(gpsimd):
            for s in ("st", "sv0", "sv1", "sv2", "sv3"):
                if cnt(s):
                    gpsimd.wait_ge(sems[s][0], cnt(s))
            inc(gpsimd.collective_compute(
                "ReduceScatter", OP.add, replica_groups=GROUPS,
                ins=[rs_s.opt()], outs=[rs_d.opt()]), "gg")
        barrier()
    emit_oproj.tms = {}
    emit_oproj.sms = {}
    emit_oproj.slot_use = 0
    emit_oproj.slot_dm = {}

    def emit_postln_attn(layer):
        bb = 0 if layer == 0 else 16
        for tc in range(4):
            t0 = tc * 512
            vprev = cnt("vv")

            @block.sync
            def _(sync, tc=tc, t0=t0, vprev=vprev):
                sync.wait_ge(sems["vv"][0], vprev)
                for c in range(8):
                    inc(sync.dma_start(out=regb[:, c * 512:(c + 1) * 512],
                                       in_=rs_d[c][:, t0:t0 + 512]), "ld", 16)
            lrs = cnt("ld")

            @block.scalar
            def _(scalar, tc=tc, lrs=lrs, bb=bb):
                scalar.wait_ge(sems["ld"][0], lrs)
                for c in range(8):
                    inc(scalar.activation(out=regb[:, 4096 + c * 512:4096 + (c + 1) * 512],
                                          in_=regb[:, c * 512:(c + 1) * 512],
                                          func=AF.Identity, bias=bcol(bb + c)), "ss")
            sbi = cnt("ss")

            @block.vector
            def _(vector, sbi=sbi):
                vector.wait_ge(sems["ss"][0], sbi)

            emit_ln(lambda c: regb[:, 4096 + c * 512:4096 + (c + 1) * 512],
                    None,
                    lambda c, t0=t0: x_ap(c, t0),
                    512, 8192, 8192 + 4096 - 512)

    def emit_binarize():
        for H in range(32):
            tprev = cnt("tt")

            @block.sync
            def _(sync, H=H, tprev=tprev):
                sync.wait_ge(sems["tt"][0], tprev)
                inc(sync.dma_start(out=regb[:, (H % 2) * 1024:(H % 2) * 1024 + 1024],
                                   in_=l0_w1[H]), "lv%d" % (H % 2), 16)
            lw = cnt("lv%d" % (H % 2))

            @block.scalar
            def _(scalar, H=H, lw=lw):
                scalar.wait_ge(sems["lv%d" % (H % 2)][0], lw)
                inc(scalar.activation(out=regb[:, 2048:3072],
                                      in_=regb[:, (H % 2) * 1024:(H % 2) * 1024 + 1024],
                                      func=AF.Abs), "ss")
                inc(scalar.activation(out=regb[:, 3072:4096],
                                      in_=regb[:, (H % 2) * 1024:(H % 2) * 1024 + 1024],
                                      func=AF.Sign), "ss")
            sab = cnt("ss")

            @block.vector
            def _(vector, H=H, sab=sab):
                vector.wait_ge(sems["ss"][0], sab)
                vector.reduce_sum(out=stat[:, 8:9], in_=regb[:, 2048:3072], axis=AX.X)
                vector.drain()
                inc(vector.tensor_scalar(out=regb[:, 4096:5120], in0=regb[:, 3072:4096],
                                         scalar1=stat[:, 8:9], scalar2=1.0 / 1024.0,
                                         op0=OP.mult, op1=OP.mult), "vv")
            vbw = cnt("vv")

            for c in range(8):

                @block.tensor
                def _(tensor, H=H, c=c, vbw=vbw):
                    if c == 0:
                        tensor.wait_ge(sems["vv"][0], vbw)
                    prevev = emit_binarize.ev.get(c % 2)
                    if prevev:
                        tensor.wait_ge(sems["ss"][0], prevev)
                    inc(tensor.transpose(pbt[c % 2][:, 0:128],
                                         regb[:, 4096 + 128 * c:4096 + 128 * c + 128],
                                         ident[:]), "tt")
                ttr = cnt("tt")

                @block.scalar
                def _(scalar, H=H, c=c, ttr=ttr):
                    scalar.wait_ge(sems["tt"][0], ttr)
                    prevdm = emit_binarize.dm.get(c % 4)
                    if prevdm:
                        scalar.wait_ge(sems["sv%d" % (c % 4)][0], prevdm)
                    stg = regb[:, 5120 + (c % 4) * 128:5120 + (c % 4) * 128 + 128]
                    emit_binarize.ev[c % 2] = inc(
                        scalar.activation(out=stg, in_=pbt[c % 2][:, 0:128],
                                          func=AF.Copy), "ss")
                    scalar.drain()
                    emit_binarize.dm[c % 4] = inc(
                        scalar.dma_start(out=bw1t[H][:, 128 * c:128 * c + 128], in_=stg),
                        "sv%d" % (c % 4), 16)
        barrier()
        for m in range(8):
            tprev = cnt("tt")

            @block.sync
            def _(sync, m=m, tprev=tprev):
                sync.wait_ge(sems["tt"][0], tprev)
                inc(sync.dma_start(out=regb[:, 0:4096], in_=l0_w2[m]), "ld", 16)
            lw = cnt("ld")

            @block.scalar
            def _(scalar, m=m, lw=lw):
                scalar.wait_ge(sems["ld"][0], lw)
                inc(scalar.activation(out=regb[:, 4096:8192], in_=regb[:, 0:4096],
                                      func=AF.Abs), "ss")
                scalar.drain()
                inc(scalar.activation(out=regb[:, 0:4096], in_=regb[:, 0:4096],
                                      func=AF.Sign), "ss")
            sab = cnt("ss")

            @block.vector
            def _(vector, m=m, sab=sab):
                vector.wait_ge(sems["ss"][0], sab)
                vector.reduce_sum(out=stat[:, 8:9], in_=regb[:, 4096:8192], axis=AX.X)
                vector.drain()
                inc(vector.tensor_scalar(out=regb[:, 4096:8192], in0=regb[:, 0:4096],
                                         scalar1=stat[:, 8:9], scalar2=1.0 / 4096.0,
                                         op0=OP.mult, op1=OP.mult), "vv")
            vbw = cnt("vv")

            for c in range(32):

                @block.tensor
                def _(tensor, m=m, c=c, vbw=vbw):
                    if c == 0:
                        tensor.wait_ge(sems["vv"][0], vbw)
                    prevev = emit_binarize.ev2.get(c % 2)
                    if prevev:
                        tensor.wait_ge(sems["ss"][0], prevev)
                    inc(tensor.transpose(pbt[c % 2][:, 0:128],
                                         regb[:, 4096 + 128 * c:4096 + 128 * c + 128],
                                         ident[:]), "tt")
                ttr = cnt("tt")

                @block.scalar
                def _(scalar, m=m, c=c, ttr=ttr):
                    scalar.wait_ge(sems["tt"][0], ttr)
                    prevdm = emit_binarize.dm2.get(c % 4)
                    if prevdm:
                        scalar.wait_ge(sems["sv%d" % (c % 4)][0], prevdm)
                    stg = regb[:, 8192 + (c % 4) * 128:8192 + (c % 4) * 128 + 128]
                    emit_binarize.ev2[c % 2] = inc(
                        scalar.activation(out=stg, in_=pbt[c % 2][:, 0:128],
                                          func=AF.Copy), "ss")
                    scalar.drain()
                    emit_binarize.dm2[c % 4] = inc(
                        scalar.dma_start(out=bw2t[c][:, 128 * m:128 * m + 128], in_=stg),
                        "sv%d" % (c % 4), 16)
        barrier()
    emit_binarize.ev = {}
    emit_binarize.dm = {}
    emit_binarize.ev2 = {}
    emit_binarize.dm2 = {}

    def emit_ffn0():
        for tc in range(8):
            t0 = tc * 256
            for Ht in range(32):
                idx = tc * 32 + Ht
                tpv = cnt("tt")

                @block.sync
                def _(sync, Ht=Ht, idx=idx):
                    if idx >= 2:
                        sync.wait_ge(sems["tt"][0], emit_ffn0.t1.get(idx - 2, 0))
                    inc(sync.dma_start(out=regb[:, 512 + (idx % 2) * 1024:
                                                512 + (idx % 2) * 1024 + 1024],
                                       in_=bw1t[Ht]), "lv%d" % (idx % 2), 16)
                l1_ = cnt("lv%d" % (idx % 2))

                @block.tensor
                def _(tensor, Ht=Ht, t0=t0, idx=idx, l1_=l1_):
                    tensor.wait_ge(sems["lv%d" % (idx % 2)][0], l1_)
                    if idx >= 2:
                        tensor.wait_ge(sems["ss"][0], emit_ffn0.sg.get(idx - 2, 0))
                    for c in range(8):
                        mm = tensor.matmul(
                            pbt[idx % 2][:, 0:256],
                            regb[:, 512 + (idx % 2) * 1024 + 128 * c:
                                 512 + (idx % 2) * 1024 + 128 * c + 128],
                            xln_ap(c, t0, 256), start=(c == 0), stop=(c == 7))
                    emit_ffn0.t1[idx] = inc(mm, "tt")
                th1 = cnt("tt")

                @block.scalar
                def _(scalar, Ht=Ht, idx=idx, th1=th1):
                    scalar.wait_ge(sems["tt"][0], th1)
                    if idx >= 2:
                        scalar.wait_ge(sems["tt"][0], emit_ffn0.t2.get(idx - 2, 0))
                    emit_ffn0.sg[idx] = inc(scalar.activation(
                        out=regb[:, (idx % 2) * 256:(idx % 2) * 256 + 256],
                        in_=pbt[idx % 2][:, 0:256], func=AF.Gelu,
                        bias=b1sb[:, Ht:Ht + 1]), "ss")

                @block.sync
                def _(sync, Ht=Ht, idx=idx):
                    if idx >= 2:
                        sync.wait_ge(sems["tt"][0], emit_ffn0.t2.get(idx - 2, 0))
                    inc(sync.dma_start(out=regb[:, 2560 + (idx % 2) * 1024:
                                                2560 + (idx % 2) * 1024 + 1024],
                                       in_=bw2t[Ht]), "lw%d" % (2 + idx % 2), 16)
                l2_ = cnt("lw%d" % (2 + idx % 2))

                @block.tensor
                def _(tensor, Ht=Ht, idx=idx, l2_=l2_):
                    tensor.wait_ge(sems["lw%d" % (2 + idx % 2)][0], l2_)
                    tensor.wait_ge(sems["ss"][0], emit_ffn0.sg[idx])
                    if idx >= 1:
                        tensor.wait_ge(sems["vv"][0], emit_ffn0.ya.get(idx - 1, 0))
                    for m in range(8):
                        mm = tensor.matmul(
                            pbt[2 + m // 2][:, (m % 2) * 256:(m % 2) * 256 + 256],
                            regb[:, 2560 + (idx % 2) * 1024 + 128 * m:
                                 2560 + (idx % 2) * 1024 + 128 * m + 128],
                            regb[:, (idx % 2) * 256:(idx % 2) * 256 + 256],
                            start=True, stop=True, skip_group_check=True)
                    emit_ffn0.t2[idx] = inc(mm, "tt")
                th2 = cnt("tt")

                @block.vector
                def _(vector, Ht=Ht, idx=idx, th2=th2):
                    vector.wait_ge(sems["tt"][0], th2)
                    vector.drain()
                    for m in range(8):
                        src = pbt[2 + m // 2][:, (m % 2) * 256:(m % 2) * 256 + 256]
                        ydst = regb[:, 4608 + m * 256:4608 + m * 256 + 256]
                        if Ht == 0:
                            inc(vector.tensor_copy(out=ydst, in_=src), "vv")
                        else:
                            inc(vector.tensor_tensor(out=ydst, in0=ydst, in1=src,
                                                     op=OP.add), "vv")
                    emit_ffn0.ya[idx] = cnt("vv")
            yb = cnt("vv")

            @block.vector
            def _(vector, tc=tc, yb=yb):
                vector.drain()
                for c in range(8):
                    inc(vector.tensor_scalar(
                        out=regb[:, 4608 + c * 256:4608 + c * 256 + 256],
                        in0=regb[:, 4608 + c * 256:4608 + c * 256 + 256],
                        scalar1=bcol(8 + c), scalar2=None, op0=OP.add), "vv")

            emit_ln(lambda c: regb[:, 4608 + c * 256:4608 + c * 256 + 256],
                    None,
                    lambda c, t0=t0: x_ap(c, t0, 256),
                    256, 6656, 8704)
        barrier()
    emit_ffn0.t1 = {}
    emit_ffn0.t2 = {}
    emit_ffn0.sg = {}
    emit_ffn0.ya = {}

    def emit_routing():
        @block.sync
        def _(sync):
            for c in range(8):
                inc(sync.dma_start(out=gw_sb[:, c * 16:(c + 1) * 16], in_=m_gatew[c]),
                    "ld", 16)
        lgw = cnt("ld")
        glev = {}
        for n in range(4):
            t0 = n * 512

            @block.tensor
            def _(tensor, n=n, t0=t0, lgw=lgw):
                if n == 0:
                    tensor.wait_ge(sems["ld"][0], lgw)
                else:
                    tensor.wait_ge(sems["ss"][0], glev[n - 1])
                for c in range(8):
                    mm = tensor.matmul(pbt[0][0:16, :], gw_sb[:, c * 16:(c + 1) * 16],
                                       xln_ap(c, t0), start=(c == 0), stop=(c == 7))
                inc(mm, "tt")
            tg = cnt("tt")

            @block.scalar
            def _(scalar, n=n, t0=t0, tg=tg):
                scalar.wait_ge(sems["tt"][0], tg)
                inc(scalar.activation(out=gcf[:, t0:t0 + 512], in_=pbt[0][0:16, :],
                                      func=AF.Copy), "ss")
                glev[n] = cnt("ss")
        sg = cnt("ss")
        for tt in range(16):

            @block.tensor
            def _(tensor, tt=tt, sg=sg):
                if tt == 0:
                    tensor.wait_ge(sems["ss"][0], sg)
                if tt >= 2:
                    tensor.wait_ge(sems["ss"][0], emit_routing.ev.get(tt - 2, 0))
                inc(tensor.transpose(pbt[tt % 2][0:128, 0:16],
                                     gcf[:, 128 * tt:128 * tt + 128],
                                     ident[0:16, 0:16]), "tt")
            ttr = cnt("tt")

            @block.scalar
            def _(scalar, tt=tt, ttr=ttr):
                scalar.wait_ge(sems["tt"][0], ttr)
                emit_routing.ev[tt] = inc(scalar.activation(
                    out=glt[:, 16 * tt:16 * tt + 16], in_=pbt[tt % 2][0:128, 0:16],
                    func=AF.Copy), "ss")
        sev = cnt("ss")
        for tt in range(16):
            gl = glt[:, 16 * tt:16 * tt + 16]
            sreg = glt[:, 256 + 16 * tt:256 + 16 * tt + 16]
            msk = glt[:, 512 + 16 * tt:512 + 16 * tt + 16]
            cmb = glt[:, 768 + 16 * tt:768 + 16 * tt + 16]

            @block.vector
            def _(vector, tt=tt, gl=gl, sev=sev):
                if tt == 0:
                    vector.wait_ge(sems["ss"][0], sev)
                vector.reduce_max(out=stat[:, 8:9], in_=gl, axis=AX.X)
                vector.drain()
                inc(vector.tensor_scalar(out=stat[:, 9:10], in0=stat[:, 8:9],
                                         scalar1=-1.0, scalar2=None, op0=OP.mult), "vv")
            v1 = cnt("vv")

            @block.scalar
            def _(scalar, tt=tt, gl=gl, sreg=sreg, v1=v1):
                scalar.wait_ge(sems["vv"][0], v1)
                inc(scalar.activation(out=sreg, in_=gl, func=AF.Exp,
                                      bias=stat[:, 9:10], accum_out=stat[:, 10:11]),
                    "ss")
            s1 = cnt("ss")

            @block.vector
            def _(vector, tt=tt, sreg=sreg, msk=msk, cmb=cmb, s1=s1):
                vector.wait_ge(sems["ss"][0], s1)
                vector.reciprocal(out=stat[:, 11:12], in_=stat[:, 10:11])
                vector.drain()
                vector.tensor_scalar(out=sreg, in0=sreg, scalar1=stat[:, 11:12],
                                     scalar2=None, op0=OP.mult)
                vector.drain()
                vector.tensor_tensor(out=msk, in0=sreg, in1=gateb[:], op=OP.add)
                vector.drain()
                for g in range(4):
                    mg = msk[:, 4 * g:4 * g + 4]
                    tmp = cmb[:, 4 * g:4 * g + 4]
                    vector.reduce_max(out=stat[:, 12:13], in_=mg, axis=AX.X)
                    vector.drain()
                    vector.tensor_scalar(out=tmp, in0=mg, scalar1=stat[:, 12:13],
                                         scalar2=None, op0=OP.is_equal)
                    vector.drain()
                    vector.scalar_tensor_tensor(out=tmp, in0=tmp, scalar=NEG,
                                                in1=mg, op0=OP.mult, op1=OP.add)
                    vector.drain()
                    vector.reduce_max(out=stat[:, 13:14], in_=tmp, axis=AX.X)
                    vector.drain()
                    vector.tensor_tensor(out=stat[:, 16 + g:17 + g],
                                         in0=stat[:, 12:13], in1=stat[:, 13:14],
                                         op=OP.add)
                    vector.drain()
                gsc = stat[:, 16:20]
                vector.reduce_max(out=stat[:, 12:13], in_=gsc, axis=AX.X)
                vector.drain()
                vector.tensor_scalar(out=stat[:, 20:24], in0=gsc,
                                     scalar1=stat[:, 12:13], scalar2=None,
                                     op0=OP.is_equal)
                vector.drain()
                vector.scalar_tensor_tensor(out=stat[:, 20:24], in0=stat[:, 20:24],
                                            scalar=NEG, in1=gsc, op0=OP.mult,
                                            op1=OP.add)
                vector.drain()
                vector.reduce_max(out=stat[:, 13:14], in_=stat[:, 20:24], axis=AX.X)
                vector.drain()
                for g in range(4):
                    vector.tensor_scalar(out=stat[:, 20 + g:21 + g],
                                         in0=stat[:, 16 + g:17 + g],
                                         scalar1=stat[:, 13:14], scalar2=1.0,
                                         op0=OP.is_ge, op1=OP.subtract)
                    vector.drain()
                    vector.tensor_scalar(out=stat[:, 20 + g:21 + g],
                                         in0=stat[:, 20 + g:21 + g],
                                         scalar1=-NEG, scalar2=None, op0=OP.mult)
                    vector.drain()
                    vector.tensor_scalar(out=msk[:, 4 * g:4 * g + 4],
                                         in0=msk[:, 4 * g:4 * g + 4],
                                         scalar1=stat[:, 20 + g:21 + g],
                                         scalar2=None, op0=OP.add)
                    vector.drain()
                vector.reduce_max(out=stat[:, 12:13], in_=msk, axis=AX.X)
                vector.drain()
                vector.tensor_scalar(out=cmb, in0=msk, scalar1=stat[:, 12:13],
                                     scalar2=None, op0=OP.is_equal)
                vector.drain()
                vector.scalar_tensor_tensor(out=cmb, in0=cmb, scalar=NEG, in1=msk,
                                            op0=OP.mult, op1=OP.add)
                vector.drain()
                vector.reduce_max(out=stat[:, 13:14], in_=cmb, axis=AX.X)
                vector.drain()
                vector.tensor_scalar(out=cmb, in0=msk, scalar1=stat[:, 13:14],
                                     scalar2=None, op0=OP.is_ge)
                vector.drain()
                inc(vector.tensor_tensor(out=cmb, in0=cmb, in1=sreg, op=OP.mult),
                    "vv")
        vcm = cnt("vv")
        for tt in range(16):

            @block.tensor
            def _(tensor, tt=tt, vcm=vcm):
                if tt == 0:
                    tensor.wait_ge(sems["vv"][0], vcm)
                if tt >= 2:
                    tensor.wait_ge(sems["ss"][0], emit_routing.ev2.get(tt - 2, 0))
                inc(tensor.transpose(pbt[tt % 2][0:16, 0:128],
                                     glt[:, 768 + 16 * tt:768 + 16 * tt + 16],
                                     ident[:]), "tt")
            ttr = cnt("tt")

            @block.scalar
            def _(scalar, tt=tt, ttr=ttr):
                scalar.wait_ge(sems["tt"][0], ttr)
                emit_routing.ev2[tt] = inc(scalar.activation(
                    out=gcf[:, 128 * tt:128 * tt + 128], in_=pbt[tt % 2][0:16, 0:128],
                    func=AF.Copy), "ss")
        barrier()
    emit_routing.ev = {}
    emit_routing.ev2 = {}

    def emit_moe():
        for tc in range(4):
            t0 = tc * 512

            @block.sync
            def _(sync, tc=tc):
                sync.wait_ge(sems["tt"][0], cnt("tt"))
                sync.wait_ge(sems["vv"][0], cnt("vv"))
                inc(sync.dma_start(out=regb[0:16, 4096:5120], in_=m_be2), "ld", 16)
            lbe2 = cnt("ld")

            @block.tensor
            def _(tensor, tc=tc, t0=t0, lbe2=lbe2):
                tensor.wait_ge(sems["ld"][0], lbe2)
                tensor.wait_ge(sems["vv"][0], cnt("vv"))
                for m in range(8):
                    mm = tensor.matmul(pbt[m][:, :],
                                       regb[0:16, 4096 + 128 * m:4096 + 128 * m + 128],
                                       gcf[:, t0:t0 + 512],
                                       start=True, stop=True)
                inc(mm, "tt")
            tbe = cnt("tt")

            @block.scalar
            def _(scalar, tc=tc, tbe=tbe):
                scalar.wait_ge(sems["tt"][0], tbe)
                for m in range(8):
                    inc(scalar.activation(out=regb[:, 8192 + m * 512:8192 + m * 512 + 512],
                                          in_=pbt[m][:, :], func=AF.Copy), "ss")
            barrier(engs=["tensor", "scalar", "vector", "sync"], on=("tt", "ss"))

            def shared_mm(wsrc, rhsf, twait_ss=0):
                shm = {}
                tentry = cnt("tt")
                for c in range(8):

                    @block.sync
                    def _(sync, c=c, wsrc=wsrc, tentry=tentry):
                        if c < 2:
                            sync.wait_ge(sems["tt"][0], tentry)
                        else:
                            sync.wait_ge(sems["tt"][0], shm[c - 2])
                        inc(sync.dma_start(out=regb[:, 4096 + (c % 2) * 1024:
                                                    4096 + (c % 2) * 1024 + 1024],
                                           in_=wsrc[c]), "lv%d" % (c % 2), 16)
                    lw = cnt("lv%d" % (c % 2))

                    @block.tensor
                    def _(tensor, c=c, lw=lw, twait_ss=twait_ss, rhsf=rhsf):
                        tensor.wait_ge(sems["lv%d" % (c % 2)][0], lw)
                        if c == 0 and twait_ss:
                            tensor.wait_ge(sems["ss"][0], twait_ss)
                        for m in range(8):
                            mm = tensor.matmul(
                                pbt[m][:, :],
                                regb[:, 4096 + (c % 2) * 1024 + 128 * m:
                                     4096 + (c % 2) * 1024 + 128 * m + 128],
                                rhsf(c), start=(c == 0), stop=(c == 7))
                        shm[c] = inc(mm, "tt")
                return cnt("tt")

            tsh1 = shared_mm(m_ws1, lambda c: xln_ap(c, t0))

            @block.scalar
            def _(scalar, tsh1=tsh1):
                scalar.wait_ge(sems["tt"][0], tsh1)
                for m in range(8):
                    inc(scalar.activation(
                        out=regb[:, m * 512:m * 512 + 512],
                        in_=pbt[m][:, :], func=AF.Silu, bias=bcol(48 + m)), "ss")
            s1done = cnt("ss")

            tsh3 = shared_mm(m_ws3, lambda c: xln_ap(c, t0), twait_ss=s1done)

            @block.vector
            def _(vector, tsh3=tsh3, s1done=s1done):
                vector.wait_ge(sems["tt"][0], tsh3)
                vector.wait_ge(sems["ss"][0], s1done)
                for m in range(8):
                    inc(vector.scalar_tensor_tensor(
                        out=regb[:, m * 512:m * 512 + 512],
                        in0=pbt[m][:, :], scalar=bcol(56 + m),
                        in1=regb[:, m * 512:m * 512 + 512],
                        op0=OP.add, op1=OP.mult), "vv")
            vh3 = cnt("vv")

            @block.tensor
            def _(tensor, vh3=vh3):
                tensor.wait_ge(sems["vv"][0], vh3)
            tz = shared_mm(m_ws2, lambda c: regb[:, c * 512:c * 512 + 512])

            @block.vector
            def _(vector, tz=tz):
                vector.wait_ge(sems["tt"][0], tz)
                vector.drain()
                for m in range(8):
                    inc(vector.scalar_tensor_tensor(
                        out=regb[:, 8192 + m * 512:8192 + m * 512 + 512],
                        in0=pbt[m][:, :], scalar=bcol(24 + m),
                        in1=regb[:, 8192 + m * 512:8192 + m * 512 + 512],
                        op0=OP.add, op1=OP.add), "vv")
            barrier(engs=["tensor", "scalar", "vector", "sync"], on=("tt", "vv", "ss"))

            for e in range(16):

                @block.sync
                def _(sync, e=e, tc=tc):
                    sync.wait_ge(sems["tt"][0], cnt("tt"))
                    for c in range(8):
                        inc(sync.dma_start(out=regb[:, 4096 + c * 512:4096 + c * 512 + 512],
                                           in_=m_we1[e, c]), "ld", 16)
                    inc(sync.dma_start(out=be1sb[:], in_=m_be1[e]), "ld", 16)
                    inc(sync.dma_start(out=be3sb[:], in_=m_be3[e]), "ld", 16)
                lw1 = cnt("ld")

                @block.tensor
                def _(tensor, e=e, t0=t0, lw1=lw1):
                    tensor.wait_ge(sems["ld"][0], lw1)
                    tensor.wait_ge(sems["vv"][0], cnt("vv"))
                    tensor.matmul(pbt[6][:, :], sel16[:, 128 * e:128 * e + 128],
                                  gcf[0:16, t0:t0 + 512], start=True, stop=True)
                    for mt in range(4):
                        for c in range(8):
                            mm = tensor.matmul(
                                pbt[mt][:, :],
                                regb[:, 4096 + c * 512 + mt * 128:
                                     4096 + c * 512 + mt * 128 + 128],
                                xln_ap(c, t0), start=(c == 0), stop=(c == 7))
                    inc(mm, "tt")
                t1 = cnt("tt")

                @block.scalar
                def _(scalar, e=e, t1=t1):
                    scalar.wait_ge(sems["tt"][0], t1)
                    for mt in range(4):
                        inc(scalar.activation(out=regb[:, mt * 512:mt * 512 + 512],
                                              in_=pbt[mt][:, :], func=AF.Silu,
                                              bias=be1sb[:, mt:mt + 1]), "ss")
                s1_ = cnt("ss")

                @block.sync
                def _(sync, e=e, s1_=s1_):
                    sync.wait_ge(sems["tt"][0], cnt("tt"))
                    for c in range(8):
                        inc(sync.dma_start(out=regb[:, 4096 + c * 512:4096 + c * 512 + 512],
                                           in_=m_we3[e, c]), "ld", 16)
                lw3 = cnt("ld")

                @block.tensor
                def _(tensor, e=e, t0=t0, lw3=lw3, s1_=s1_):
                    tensor.wait_ge(sems["ld"][0], lw3)
                    tensor.wait_ge(sems["ss"][0], s1_)
                    for mt in range(4):
                        for c in range(8):
                            mm = tensor.matmul(
                                pbt[mt][:, :],
                                regb[:, 4096 + c * 512 + mt * 128:
                                     4096 + c * 512 + mt * 128 + 128],
                                xln_ap(c, t0), start=(c == 0), stop=(c == 7))
                    inc(mm, "tt")
                t3 = cnt("tt")

                @block.vector
                def _(vector, e=e, t3=t3):
                    vector.wait_ge(sems["tt"][0], t3)
                    for mt in range(4):
                        inc(vector.scalar_tensor_tensor(
                            out=regb[:, 2048 + mt * 512:2048 + mt * 512 + 512],
                            in0=pbt[mt][:, :], scalar=be3sb[:, mt:mt + 1],
                            in1=regb[:, mt * 512:mt * 512 + 512],
                            op0=OP.add, op1=OP.mult), "vv")
                    vector.drain()
                    for mt in range(4):
                        inc(vector.tensor_tensor(
                            out=regb[:, mt * 512:mt * 512 + 512],
                            in0=regb[:, 2048 + mt * 512:2048 + mt * 512 + 512],
                            in1=pbt[6][:, :], op=OP.mult), "vv")
                vht = cnt("vv")

                @block.sync
                def _(sync, e=e, vht=vht):
                    sync.wait_ge(sems["tt"][0], cnt("tt"))
                    for hc in range(4):
                        inc(sync.dma_start(out=regb[:, 4096 + hc * 1024:4096 + hc * 1024 + 1024],
                                           in_=m_we2[e, hc]), "ld", 16)
                lw2 = cnt("ld")

                @block.tensor
                def _(tensor, e=e, lw2=lw2, vht=vht):
                    tensor.wait_ge(sems["ld"][0], lw2)
                    tensor.wait_ge(sems["vv"][0], vht)
                    for hc in range(4):
                        for m in range(8):
                            mm = tensor.matmul(
                                pbt[m][:, :],
                                regb[:, 4096 + hc * 1024 + 128 * m:
                                     4096 + hc * 1024 + 128 * m + 128],
                                regb[:, hc * 512:hc * 512 + 512],
                                start=(hc == 0), stop=(hc == 3))
                    inc(mm, "tt")
                t2 = cnt("tt")

                @block.vector
                def _(vector, e=e, t2=t2):
                    vector.wait_ge(sems["tt"][0], t2)
                    vector.drain()
                    for m in range(8):
                        inc(vector.tensor_tensor(
                            out=regb[:, 8192 + m * 512:8192 + m * 512 + 512],
                            in0=regb[:, 8192 + m * 512:8192 + m * 512 + 512],
                            in1=pbt[m][:, :], op=OP.add), "vv")

                @block.tensor
                def _(tensor, e=e):
                    tensor.wait_ge(sems["vv"][0], cnt("vv"))

            emit_ln(lambda c: regb[:, 8192 + c * 512:8192 + c * 512 + 512],
                    None,
                    lambda c, t0=t0: x_ap(c, t0),
                    512, 0, 4096)
        barrier()

    emit_preln([0, 512, 1024, 1536])
    barrier()
    emit_qkv_l0()
    emit_gather(0)
    emit_attention(0)
    emit_oproj(0)
    emit_postln_attn(0)
    if stage >= 2:
        emit_preln([0, 512, 1024, 1536])
        barrier()
        emit_binarize()
        emit_ffn0()
    if stage >= 3:
        emit_preln([0, 512, 1024, 1536])
        barrier()
        emit_qkv_l1()
        emit_gather(1)
        emit_attention(1)
        emit_oproj(1)
        emit_postln_attn(1)
    if stage >= 4:
        emit_preln([0, 512, 1024, 1536])
        barrier()
        emit_routing()
        emit_moe()
    barrier()

    @block.sync
    def _(sync):
        for c in range(8):
            inc(sync.dma_start(out=out[c], in_=x_sb[:, c * TH:(c + 1) * TH]), "oo", 16)
    OO = cnt("oo")

    @block.gpsimd
    def _(gpsimd):
        gpsimd.wait_ge(sems["oo"][0], OO)

    blkcm.__exit__(None, None, None)
    for cm in reversed(ctx):
        cm.__exit__(None, None, None)
    return nc


def _prep_inputs(inputs, core):
    b, r = core // 2, core % 2
    slot = list(range(16))
    pi = np.arange(1024)
    myrows = np.concatenate([np.arange(64 * h, 64 * h + 64)
                             for h in range(8 * r, 8 * r + 8)])
    C = np.ascontiguousarray

    d = {}
    x = np.asarray(inputs["L0_attn_x"][b, r * TH:(r + 1) * TH, :])
    d["x"] = C(x.T[pi].reshape(8, 128, TH))

    w0 = np.asarray(inputs["L0_attn_Wqkv"])
    bq0 = np.asarray(inputs["L0_attn_bqkv"])
    d["l0_wqk"] = C(np.concatenate([w0[:, 0:128], w0[:, 0:128]], axis=0))
    d["l0_wv"] = C(np.concatenate([w0[:, 128:192], w0[:, 128:192]], axis=0))
    d["l0_bqk"] = C(bq0[0:128].reshape(128, 1))
    d["l0_sqk"] = C(np.array([0.125] * 64 + [1.0] * 64, np.float32).reshape(128, 1))
    d["l0_bv"] = C(np.broadcast_to(np.tile(bq0[128:192], 16), (128, 1024)))
    wr = np.asarray(inputs["L0_attn_Wr"])
    d["l0_wr"] = C(wr[myrows][:, pi].reshape(4, 128, D))
    w1 = np.asarray(inputs["L0_ffn_w1"])
    d["l0_w1"] = C(w1[:, pi].reshape(32, 128, D))
    d["l0_b1"] = C(np.asarray(inputs["L0_ffn_b1"]).reshape(32, 128).T)
    w2 = np.asarray(inputs["L0_ffn_w2"])
    d["l0_w2"] = C(w2[pi, :].reshape(8, 128, 4096))

    wq1 = np.asarray(inputs["L1_attn_Wqkv"])
    colp = np.concatenate([np.concatenate([np.arange(64 * h, 64 * h + 64) + 1024 * s
                                           for h in slot]) for s in range(3)])
    d["l1_wqkv"] = C(wq1[pi][:, colp].reshape(8, 128, 3072))
    bq1 = np.asarray(inputs["L1_attn_bqkv"])[colp]
    d["l1_bv"] = C(np.broadcast_to(bq1[2048:3072], (128, 1024)))
    w1p = np.asarray(inputs["L1_attn_W1p"])
    w2p = np.asarray(inputs["L1_attn_W2p"])
    colp4 = np.concatenate([np.arange(4 * h, 4 * h + 4) for h in slot])
    d["l1_w1p"] = C(w1p[pi][:, colp4].reshape(8, 128, 64))
    d["l1_w2p"] = C(w2p[pi][:, colp4].reshape(8, 128, 64))
    d["l1_ps96"] = C(np.repeat(np.asarray(inputs["L1_attn_plucker_scale"])[slot], 6)
                     .reshape(96, 1))
    wout = np.asarray(inputs["L1_attn_Wout"])
    d["l1_wout"] = C(wout[myrows][:, pi].reshape(4, 128, D))

    d["m_gatew"] = C(np.asarray(inputs["L1_ffn_gate_w"]).T[pi].reshape(8, 128, 16))
    d["m_gateb"] = C(np.broadcast_to(np.asarray(inputs["L1_ffn_gate_b"]), (128, 16)))
    d["m_we1"] = C(np.asarray(inputs["L1_ffn_We1"])[:, pi, :].reshape(16, 8, 128, 512))
    d["m_be1"] = C(np.asarray(inputs["L1_ffn_be1"]).reshape(16, 4, 128)
                   .transpose(0, 2, 1))
    d["m_we3"] = C(np.asarray(inputs["L1_ffn_We3"])[:, pi, :].reshape(16, 8, 128, 512))
    d["m_be3"] = C(np.asarray(inputs["L1_ffn_be3"]).reshape(16, 4, 128)
                   .transpose(0, 2, 1))
    d["m_we2"] = C(np.asarray(inputs["L1_ffn_We2"])[:, :, pi].reshape(16, 4, 128, D))
    d["m_be2"] = C(np.asarray(inputs["L1_ffn_be2"])[:, pi])
    d["m_ws1"] = C(np.asarray(inputs["L1_ffn_ws1"])[pi].reshape(8, 128, 1024))
    d["m_ws3"] = C(np.asarray(inputs["L1_ffn_ws3"])[pi].reshape(8, 128, 1024))
    d["m_ws2"] = C(np.asarray(inputs["L1_ffn_ws2"])[:, pi].reshape(8, 128, D))

    bp = np.zeros((128, 64), np.float32)
    bp[:, 0:8] = np.asarray(inputs["L0_attn_br"])[pi].reshape(8, 128).T
    bp[:, 8:16] = np.asarray(inputs["L0_ffn_b2"])[pi].reshape(8, 128).T
    bp[:, 16:24] = np.asarray(inputs["L1_attn_bout"])[pi].reshape(8, 128).T
    bp[:, 24:32] = np.asarray(inputs["L1_ffn_bs2"])[pi].reshape(8, 128).T
    bp[:, 32:48] = bq1[0:2048].reshape(16, 128).T
    bp[:, 48:56] = np.asarray(inputs["L1_ffn_bs1"]).reshape(8, 128).T
    bp[:, 56:64] = np.asarray(inputs["L1_ffn_bs3"]).reshape(8, 128).T
    d["b_pack"] = C(bp)

    d["c_ident"] = np.eye(128, dtype=np.float32)
    jj = np.arange(128)
    d["c_tril"] = C((jj[None, :] < jj[:, None]).astype(np.float32))
    d["c_triuneg"] = C(np.where(jj[None, :] > jj[:, None], NEG, 0.0)
                       .astype(np.float32))
    PI_ = np.array([0, 0, 0, 1, 1, 2])
    PJ_ = np.array([1, 2, 3, 2, 3, 3])
    spi_ = np.zeros((64, 96), np.float32)
    spj_ = np.zeros((64, 96), np.float32)
    for h in range(16):
        for mth in range(6):
            spi_[4 * h + PI_[mth], 6 * h + mth] = 1.0
            spj_[4 * h + PJ_[mth], 6 * h + mth] = 1.0
    d["c_spi"] = spi_
    d["c_spj"] = spj_
    g_ = np.zeros((96, 16), np.float32)
    gt_ = np.zeros((16, 96), np.float32)
    for h in range(16):
        g_[6 * h:6 * h + 6, h] = 1.0
        gt_[h, 6 * h:6 * h + 6] = 1.0
    d["c_g96"] = g_
    d["c_gt96"] = gt_
    J6 = np.array([[0, 0, 0, 0, 0, 1], [0, 0, 0, 0, -1, 0], [0, 0, 0, 1, 0, 0],
                   [0, 0, 1, 0, 0, 0], [0, -1, 0, 0, 0, 0], [1, 0, 0, 0, 0, 0]],
                  np.float32)
    bj_ = np.zeros((96, 96), np.float32)
    for h in range(16):
        bj_[6 * h:6 * h + 6, 6 * h:6 * h + 6] = J6.T
    d["c_bj"] = bj_
    d["c_ones1"] = np.ones((1, 128), np.float32)
    d["c_onesd"] = np.ones((128, 1), np.float32)
    jj32 = np.arange(32)
    d["c_u32"] = C((jj32[:, None] > jj32[None, :]).astype(np.float32))
    d["c_eps"] = np.array([[0.0, 1e-5]], np.float32)
    dd = np.arange(64)
    jj8 = np.arange(8)
    qi0 = np.zeros((64, 32), np.uint32)
    ki0 = np.zeros((64, 32), np.uint32)
    qi1 = np.zeros((70, 32), np.uint32)
    ki1 = np.zeros((70, 32), np.uint32)
    for j in range(8):
        h = 8 * r + j
        for s in range(2):
            qi0[:, 2 * j + s] = s * 2048 + 64 * h + dd
            ki0[:, 2 * j + s] = s * 2048 + 1024 + 64 * h + dd
            qi1[0:64, 2 * j + s] = s * 2240 + 64 * h + dd
            qi1[64:70, 2 * j + s] = s * 2240 + 2048 + 6 * h + np.arange(6)
            ki1[0:64, 2 * j + s] = s * 2240 + 1024 + 64 * h + dd
            ki1[64:70, 2 * j + s] = s * 2240 + 2144 + 6 * h + np.arange(6)
    d["c_qi0"] = qi0
    d["c_ki0"] = ki0
    d["c_qi1"] = qi1
    d["c_ki1"] = ki1
    vi = np.zeros((128, 512), np.uint32)
    pp = np.arange(128)
    for j in range(8):
        h = 8 * r + j
        for s in range(2):
            for a_ in range(16):
                vi[:, j * 32 + s * 16 + a_] = ((s * 16 + a_) * 128 + pp) * 16 + h
    d["c_vi"] = vi
    sel = np.zeros((16, 16, 128), np.float32)
    for e_ in range(16):
        sel[e_, e_, :] = 1.0
    d["c_sel16"] = C(sel.transpose(1, 0, 2).reshape(16, 2048))
    d["_pi"] = pi
    return d


def kernel(**inputs):
    nc = build_nc()
    in_maps = []
    pis = []
    for core in range(8):
        d = _prep_inputs(inputs, core)
        pis.append(d.pop("_pi"))
        in_maps.append(d)
    res = _bu.run_bass_kernel_spmd(nc, in_maps, core_ids=list(range(8)))
    y = np.empty((4, T, D), np.float32)
    for core in range(8):
        b, r = core // 2, core % 2
        o = np.asarray(res.results[core]["out"]).reshape(1024, TH)
        yo = np.empty_like(o)
        yo[pis[core], :] = o
        y[b, r * TH:(r + 1) * TH, :] = yo.T
    return y


# version 03
# speedup vs baseline: 1.2392x; 1.2392x over previous
import numpy as np
import concourse.bass as bass
import concourse.mybir as mybir
from concourse import bass_utils as _bu

F32 = mybir.dt.float32
AF = mybir.ActivationFunctionType
OP = mybir.AluOpType
AX = mybir.AxisListType

D = 1024
T = 4096
TH = 2048
NEG = -1.0e30
FMIN = -3.0e38
GROUPS = [[0, 1], [2, 3], [4, 5], [6, 7]]

STAGE = 4


def build_nc(stage=None):
    if stage is None:
        stage = STAGE
    nc = bass.Bass("TRN2", target_bir_lowering=False)

    def din(name, shape):
        return nc.dram_tensor(name, shape, F32, kind="ExternalInput").ap()

    x_in = din("x", [8, 128, TH])
    l0_wqk = din("l0_wqk", [128, 128])
    l0_wv = din("l0_wv", [128, 64])
    l0_bqk = din("l0_bqk", [128, 1])
    l0_sqk = din("l0_sqk", [128, 1])
    l0_bv = din("l0_bv", [128, 1024])
    l0_wr = din("l0_wr", [4, 128, D])
    l0_w1 = din("l0_w1", [32, 128, D])
    l0_b1 = din("l0_b1", [128, 32])
    l0_w2 = din("l0_w2", [8, 128, 4096])
    l1_wqkv = din("l1_wqkv", [8, 128, 3072])
    l1_bv = din("l1_bv", [128, 1024])
    l1_w1p = din("l1_w1p", [8, 128, 64])
    l1_w2p = din("l1_w2p", [8, 128, 64])
    l1_ps96 = din("l1_ps96", [96, 1])
    l1_wout = din("l1_wout", [4, 128, D])
    m_gatew = din("m_gatew", [8, 128, 16])
    m_gateb = din("m_gateb", [128, 16])
    m_we1 = din("m_we1", [16, 8, 128, 512])
    m_be1 = din("m_be1", [16, 128, 4])
    m_we3 = din("m_we3", [16, 8, 128, 512])
    m_be3 = din("m_be3", [16, 128, 4])
    m_we2 = din("m_we2", [16, 4, 128, D])
    m_be2 = din("m_be2", [16, D])
    m_ws1 = din("m_ws1", [8, 128, 1024])
    m_ws3 = din("m_ws3", [8, 128, 1024])
    m_ws2 = din("m_ws2", [8, 128, D])
    b_pack = din("b_pack", [128, 64])
    c_ident = din("c_ident", [128, 128])
    c_tril = din("c_tril", [128, 128])
    c_triuneg = din("c_triuneg", [128, 128])
    c_spi = din("c_spi", [64, 96])
    c_spj = din("c_spj", [64, 96])
    c_g96 = din("c_g96", [96, 16])
    c_gt96 = din("c_gt96", [16, 96])
    c_bj = din("c_bj", [96, 96])
    c_ones1 = din("c_ones1", [1, 128])
    c_onesd = din("c_onesd", [128, 1])
    c_u32 = din("c_u32", [32, 32])
    c_eps = din("c_eps", [1, 2])
    U32 = mybir.dt.uint32
    c_qi0 = nc.dram_tensor("c_qi0", [64, 32], U32, kind="ExternalInput").ap()
    c_ki0 = nc.dram_tensor("c_ki0", [64, 32], U32, kind="ExternalInput").ap()
    c_qi1 = nc.dram_tensor("c_qi1", [70, 32], U32, kind="ExternalInput").ap()
    c_ki1 = nc.dram_tensor("c_ki1", [70, 32], U32, kind="ExternalInput").ap()
    c_vi = nc.dram_tensor("c_vi", [128, 16], U32, kind="ExternalInput").ap()
    c_sel16 = din("c_sel16", [16, 2048])

    out = nc.dram_tensor("out", [8, 128, TH], F32, kind="ExternalOutput").ap()

    aq0 = nc.dram_tensor("aq0", [2048, TH], F32).ap()
    aq0g = nc.dram_tensor("aq0g", [2, 2048, TH], F32).ap()
    aq1 = nc.dram_tensor("aq1", [2240, TH], F32).ap()
    aq1g = nc.dram_tensor("aq1g", [2, 2240, TH], F32).ap()
    av = nc.dram_tensor("av", [16, 128, 16, 64], F32).ap()
    avg = nc.dram_tensor("avg", [2, 16, 128, 16, 64], F32).ap()
    om = nc.dram_tensor("om", [512, T], F32).ap()
    rs_s = nc.dram_tensor("rs_s", [2, 8, 128, TH], F32).ap()
    rs_d = nc.dram_tensor("rs_d", [8, 128, TH], F32).ap()
    bw1t = nc.dram_tensor("bw1t", [32, 128, 1024], F32).ap()
    bw2t = nc.dram_tensor("bw2t", [32, 128, 1024], F32).ap()

    ctx = []

    def S(name, shape):
        cm = nc.sbuf_tensor(name, shape, F32)
        t = cm.__enter__()
        ctx.append(cm)
        return t

    x_sb = S("x_sb", [128, 8 * TH])
    rega = S("rega", [128, 8 * TH])
    regb = S("regb", [128, 12 * 1024])
    ident = S("ident", [128, 128])
    tril = S("tril", [128, 128])
    triun = S("triun", [128, 128])
    spi = S("spi", [64, 96])
    spj = S("spj", [64, 96])
    g96 = S("g96", [96, 16])
    gt96 = S("gt96", [16, 96])
    bjc = S("bjc", [96, 96])
    ones1 = S("ones1", [1, 128])
    onesd = S("onesd", [128, 1])
    u32 = S("u32", [32, 32])
    ps96 = S("ps96", [96, 1])
    wqk0 = S("wqk0", [128, 128])
    wv0 = S("wv0", [128, 64])
    bqk0 = S("bqk0", [128, 1])
    sqk0 = S("sqk0", [128, 1])
    bqk0s = S("bqk0s", [128, 1])
    gateb = S("gateb", [128, 16])
    bpk = S("bpk", [128, 64])
    b1sb = S("b1sb", [128, 32])
    be1sb = S("be1sb", [128, 4])
    be3sb = S("be3sb", [128, 4])
    stat = S("stat", [128, 24])
    srow = S("srow", [1, 2048])
    gcf = S("gcf", [16, TH])
    ceps = S("ceps", [1, 2])
    _u32ctx = []
    def SU(name, shape):
        cm = nc.sbuf_tensor(name, shape, mybir.dt.uint32)
        t = cm.__enter__()
        ctx.append(cm)
        return t
    qi0s = SU("qi0s", [64, 32])
    ki0s = SU("ki0s", [64, 32])
    qi1s = SU("qi1s", [70, 32])
    ki1s = SU("ki1s", [70, 32])
    vis = SU("vis", [128, 16])
    sel16 = S("sel16", [16, 2048])
    glt = regb[:, 8192:9216]
    gw_sb = regb[:, 9216:9344]
    bsum = regb[0:64, 8448:8480]
    bsumT = regb[0:32, 8480:8544]
    vsufT = regb[0:32, 8544:8608]

    pbcm = [nc.psum_tensor(f"pb{i}", [128, 512], F32) for i in range(8)]
    pbt = [p.__enter__() for p in pbcm]
    ctx.extend(pbcm)

    sems = {}
    for sname in ["ld", "st", "tt", "vv", "ss", "gg", "oo", "sq0", "sq1",
                  "sv0", "sv1", "sv2", "sv3", "lv0", "lv1", "lw2", "lw3",
                  "sto", "lg"]:
        cm = nc.semaphore(sname)
        sems[sname] = [cm.__enter__(), 0]
        ctx.append(cm)

    def inc(obj, s, k=1):
        obj.then_inc(sems[s][0], k)
        sems[s][1] += k
        return sems[s][1]

    def cnt(s):
        return sems[s][1]

    blkcm = nc.Block()
    block = blkcm.__enter__()

    ENGS = ["sync", "tensor", "vector", "scalar", "gpsimd"]

    def barrier(engs=ENGS, on=("ld", "st", "tt", "vv", "ss", "gg", "sq0",
                               "sq1", "sv0", "sv1", "sv2", "sv3", "lv0",
                               "lv1", "lw2", "lw3", "sto", "lg")):
        snap = [(s, sems[s][1]) for s in on if sems[s][1] > 0]
        for e in engs:
            dec = getattr(block, e)

            @dec
            def _(eng):
                for s, v in snap:
                    eng.wait_ge(sems[s][0], v)

    def bcol(i):
        return bpk[:, i:i + 1]

    def x_ap(c, t0, w=512):
        return x_sb[:, c * TH + t0: c * TH + t0 + w]

    def xln_ap(c, t0, w=512):
        return rega[:, c * TH + t0: c * TH + t0 + w]

    @block.sync
    def _(sync):
        for c in range(8):
            inc(sync.dma_start(out=x_sb[:, c * TH:(c + 1) * TH], in_=x_in[c]), "ld", 16)
        for dst, src in [(ident, c_ident), (tril, c_tril), (triun, c_triuneg),
                         (spi, c_spi), (spj, c_spj), (g96, c_g96),
                         (gt96, c_gt96), (bjc, c_bj), (ones1, c_ones1),
                         (onesd, c_onesd), (u32, c_u32), (ps96, l1_ps96),
                         (wqk0, l0_wqk), (wv0, l0_wv), (bqk0, l0_bqk),
                         (sqk0, l0_sqk), (gateb, m_gateb), (bpk, b_pack),
                         (b1sb, l0_b1), (ceps, c_eps), (sel16, c_sel16),
                         (qi0s, c_qi0), (ki0s, c_ki0), (qi1s, c_qi1),
                         (ki1s, c_ki1), (vis, c_vi)]:
            inc(sync.dma_start(out=dst[:], in_=src), "ld", 16)
    LD0 = cnt("ld")

    @block.vector
    def _(vector):
        vector.wait_ge(sems["ld"][0], LD0)
        vector.tensor_tensor(out=bqk0s[:], in0=bqk0[:], in1=sqk0[:], op=OP.mult)
        inc(vector.tensor_scalar(out=bpk[:, 32:40], in0=bpk[:, 32:40],
                                 scalar1=0.125, scalar2=None, op0=OP.mult), "vv")
    barrier()

    def emit_ln(srcf, dstf, resf, tokw, sq_off, tmp_off):
        tprev = cnt("tt")

        @block.vector
        def _(vector):
            vector.wait_ge(sems["tt"][0], tprev)
            vector.drain()
            for c in range(8):
                s = srcf(c)
                inc(vector.tensor_tensor(
                    out=regb[:, sq_off + c * tokw: sq_off + (c + 1) * tokw],
                    in0=s, in1=s, op=OP.mult), "vv")
        vsq = cnt("vv")

        @block.tensor
        def _(tensor):
            tensor.wait_ge(sems["vv"][0], vsq)
            for c in range(8):
                inc(tensor.matmul(pbt[0][0:1, 0:tokw], onesd[:], srcf(c),
                                  start=(c == 0), stop=(c == 7)), "tt")
            for c in range(8):
                inc(tensor.matmul(
                    pbt[1][0:1, 0:tokw], onesd[:],
                    regb[:, sq_off + c * tokw: sq_off + (c + 1) * tokw],
                    start=(c == 0), stop=(c == 7)), "tt")
        tstat = cnt("tt")

        @block.vector
        def _(vector):
            vector.wait_ge(sems["tt"][0], tstat)
            vector.tensor_scalar(out=srow[0:1, 0:tokw], in0=pbt[0][0:1, 0:tokw],
                                 scalar1=1.0 / 1024.0, scalar2=None, op0=OP.mult)
            vector.drain()
            vector.tensor_tensor(out=srow[0:1, 512:512 + tokw], in0=srow[0:1, 0:tokw],
                                 in1=srow[0:1, 0:tokw], op=OP.mult)
            vector.drain()
            inc(vector.scalar_tensor_tensor(
                out=srow[0:1, 512:512 + tokw], in0=pbt[1][0:1, 0:tokw],
                scalar=1.0 / 1024.0, in1=srow[0:1, 512:512 + tokw],
                op0=OP.mult, op1=OP.subtract), "vv")
        vstat = cnt("vv")

        @block.scalar
        def _(scalar):
            scalar.wait_ge(sems["vv"][0], vstat)
            inc(scalar.activation(out=srow[0:1, 1024:1024 + tokw],
                                  in_=srow[0:1, 512:512 + tokw],
                                  func=AF.Sqrt, bias=ceps[0:1, 1:2]), "ss")
        sstat = cnt("ss")

        @block.vector
        def _(vector):
            vector.wait_ge(sems["ss"][0], sstat)
            inc(vector.reciprocal(out=srow[0:1, 1536:1536 + tokw],
                                  in_=srow[0:1, 1024:1024 + tokw]), "vv")
        vrec = cnt("vv")

        @block.tensor
        def _(tensor):
            tensor.wait_ge(sems["vv"][0], vrec)
            inc(tensor.matmul(pbt[2][:, 0:tokw], ones1[:], srow[0:1, 0:tokw],
                              start=True, stop=True), "tt")
            inc(tensor.matmul(pbt[3][:, 0:tokw], ones1[:], srow[0:1, 1536:1536 + tokw],
                              start=True, stop=True), "tt")
        tbc = cnt("tt")

        @block.vector
        def _(vector):
            vector.wait_ge(sems["tt"][0], tbc)
            for c in range(8):
                tmp = regb[:, tmp_off: tmp_off + tokw]
                if c > 0:
                    vector.drain()
                vector.tensor_tensor(out=tmp, in0=srcf(c), in1=pbt[2][:, 0:tokw],
                                     op=OP.subtract)
                vector.drain()
                d = dstf(c) if dstf is not None else None
                if d is not None:
                    mm = vector.tensor_tensor(out=d, in0=tmp, in1=pbt[3][:, 0:tokw],
                                              op=OP.mult)
                    lnres = d
                else:
                    mm = vector.tensor_tensor(out=tmp, in0=tmp, in1=pbt[3][:, 0:tokw],
                                              op=OP.mult)
                    lnres = tmp
                if resf is not None:
                    vector.drain()
                    r = resf(c)
                    mm = vector.tensor_tensor(out=r, in0=r, in1=lnres, op=OP.add)
                inc(mm, "vv")
        vapply = cnt("vv")

        @block.tensor
        def _(tensor):
            tensor.wait_ge(sems["vv"][0], vapply)

    def emit_preln(t0_list, tokw=512):
        for t0 in t0_list:
            emit_ln(lambda c, t0=t0: x_ap(c, t0, tokw),
                    lambda c, t0=t0: xln_ap(c, t0, tokw),
                    None, tokw, 0, 8 * tokw)

    def emit_qkv_l0():
        evac_ms = {}
        dma_ms = {}
        for idx in range(64):
            j, n = idx // 4, idx % 4
            base = 64 * (j % 2)
            ch = j // 2
            t0 = n * 512

            @block.tensor
            def _(tensor, idx=idx, j=j, n=n, base=base, ch=ch, t0=t0):
                if idx >= 2:
                    tensor.wait_ge(sems["ss"][0], evac_ms[idx - 2])
                inc(tensor.matmul(pbt[idx % 2][:, :],
                                  wqk0[base:base + 64, 0:128],
                                  xln_ap(ch, t0)[base:base + 64, :],
                                  start=True, stop=True), "tt")
            tqk = cnt("tt")

            @block.scalar
            def _(scalar, idx=idx, j=j, t0=t0, tqk=tqk):
                scalar.wait_ge(sems["tt"][0], tqk)
                sq = "sq%d" % (idx % 2)
                if idx >= 2:
                    scalar.wait_ge(sems[sq][0], dma_ms[idx - 2])
                stg = regb[:, 8192 + (idx % 2) * 512: 8192 + (idx % 2) * 512 + 512]
                ev = scalar.activation(out=stg, in_=pbt[idx % 2][:, :],
                                       func=AF.Identity, bias=bqk0s[:],
                                       scale=sqk0[:])
                evac_ms[idx] = inc(ev, "ss")
                scalar.drain()
                inc(scalar.dma_start(out=aq0[64 * j:64 * j + 64, t0:t0 + 512],
                                     in_=stg[0:64, :]), sq, 16)
                dma_ms[idx] = inc(scalar.dma_start(
                    out=aq0[1024 + 64 * j:1024 + 64 * j + 64, t0:t0 + 512],
                    in_=stg[64:128, :]), sq, 16)
        emit_v_common(wv0, l0_bv)
        barrier()

    def emit_v_common(wv_tile, bv_src):
        @block.sync
        def _(sync):
            inc(sync.dma_start(out=regb[:, 11264:12288], in_=bv_src), "ld", 16)
        lbv = cnt("ld")
        bv_tile = regb[:, 11264:12288]
        for tg in range(4):
            vprev = cnt("vv")
            sprev = cnt("ss")

            @block.tensor
            def _(tensor, tg=tg, vprev=vprev, sprev=sprev):
                tensor.wait_ge(sems["vv"][0], vprev)
                if tg == 0:
                    tensor.wait_ge(sems["ss"][0], sprev)
                for j in range(16):
                    base = 64 * (j % 2)
                    ch = j // 2
                    for tt in range(4):
                        a = tg * 4 + tt
                        mm = tensor.matmul(
                            pbt[tt * 2 + (j // 8)][:, (j % 8) * 64:(j % 8) * 64 + 64],
                            xln_ap(ch, a * 128, 128)[base:base + 64, :],
                            wv_tile[base:base + 64, :],
                            start=True, stop=True)
                inc(mm, "tt")
            tv = cnt("tt")
            for tt in range(4):
                a = tg * 4 + tt

                @block.vector
                def _(vector, tt=tt, tv=tv, tg=tg):
                    vector.wait_ge(sems["tt"][0], tv)
                    vector.wait_ge(sems["ld"][0], lbv)
                    if tg > 0:
                        vector.wait_ge(sems["sv%d" % tt][0],
                                       emit_v_common.dma_ms.get((tg - 1, tt), 0))
                    stg = regb[:, tt * 1024: tt * 1024 + 1024]
                    inc(vector.tensor_tensor(out=stg[:, 0:512], in0=pbt[tt * 2][:, :],
                                             in1=bv_tile[:, 0:512], op=OP.add), "vv")
                    inc(vector.tensor_tensor(out=stg[:, 512:1024],
                                             in0=pbt[tt * 2 + 1][:, :],
                                             in1=bv_tile[:, 512:1024], op=OP.add), "vv")
                vva = cnt("vv")

                @block.sync
                def _(sync, tt=tt, a=a, vva=vva):
                    sync.wait_ge(sems["vv"][0], vva)
                    for h in range(16):
                        dm = sync.dma_start(
                            out=av[h, :, a, :],
                            in_=regb[:, tt * 1024 + 64 * h: tt * 1024 + 64 * h + 64])
                        inc(dm, "sv%d" % tt, 16)
                    emit_v_common.dma_ms[(tg, tt)] = cnt("sv%d" % tt)
    emit_v_common.dma_ms = {}

    def emit_qkv_l1():
        for mg in range(4):
            tprev = cnt("tt")

            @block.sync
            def _(sync, mg=mg, tprev=tprev):
                sync.wait_ge(sems["tt"][0], tprev)
                for c in range(8):
                    inc(sync.dma_start(
                        out=regb[:, (mg % 2) * 4096 + c * 512:
                                 (mg % 2) * 4096 + c * 512 + 512],
                        in_=l1_wqkv[c][:, mg * 512: mg * 512 + 512]), "ld", 16)
            lw = cnt("ld")
            for n in range(4):
                t0 = n * 512
                sprev = cnt("ss")

                @block.tensor
                def _(tensor, mg=mg, n=n, t0=t0, lw=lw, sprev=sprev):
                    if n == 0:
                        tensor.wait_ge(sems["ld"][0], lw)
                    if mg * 4 + n >= 1:
                        tensor.wait_ge(sems["ss"][0], emit_qkv_l1.ev.get(mg * 4 + n - 1, 0))
                    for mt in range(4):
                        for c in range(8):
                            mm = tensor.matmul(
                                pbt[4 + mt][:, :],
                                regb[:, (mg % 2) * 4096 + c * 512 + mt * 128:
                                     (mg % 2) * 4096 + c * 512 + mt * 128 + 128],
                                xln_ap(c, t0), start=(c == 0), stop=(c == 7))
                    inc(mm, "tt")
                tqk = cnt("tt")

                @block.scalar
                def _(scalar, mg=mg, n=n, t0=t0, tqk=tqk):
                    scalar.wait_ge(sems["tt"][0], tqk)
                    idx = mg * 4 + n
                    sq = "sq%d" % (idx % 2)
                    if idx >= 2:
                        scalar.wait_ge(sems[sq][0], emit_qkv_l1.dm.get(idx - 2, 0))
                    for mt in range(4):
                        gm = mg * 4 + mt
                        stg = regb[:, 8192 + ((idx % 2) * 4 + mt) * 512:
                                   8192 + ((idx % 2) * 4 + mt) * 512 + 512]
                        sc = 0.125 if gm < 8 else 1.0
                        inc(scalar.activation(out=stg, in_=pbt[4 + mt][:, :],
                                              func=AF.Identity, bias=bcol(32 + gm),
                                              scale=sc), "ss")
                    scalar.drain()
                    for mt in range(4):
                        gm = mg * 4 + mt
                        stg = regb[:, 8192 + ((idx % 2) * 4 + mt) * 512:
                                   8192 + ((idx % 2) * 4 + mt) * 512 + 512]
                        dm = scalar.dma_start(
                            out=aq1[128 * gm:128 * gm + 128, t0:t0 + 512], in_=stg)
                        inc(dm, sq, 16)
                    emit_qkv_l1.ev[idx] = cnt("ss")
                    emit_qkv_l1.dm[idx] = cnt(sq)
        barrier()
        @block.sync
        def _(sync):
            inc(sync.dma_start(out=regb[:, 11264:12288], in_=l1_bv), "ld", 16)
        lbv1 = cnt("ld")
        bv1t = regb[:, 11264:12288]
        for tg in range(4):
            for c in range(8):
                tprev = cnt("tt")

                @block.sync
                def _(sync, tg=tg, c=c, tprev=tprev):
                    if tg * 8 + c >= 2:
                        sync.wait_ge(sems["tt"][0], emit_qkv_l1.vt.get(tg * 8 + c - 2, 0))
                    inc(sync.dma_start(out=regb[:, 9216 + (c % 2) * 1024:
                                                9216 + (c % 2) * 1024 + 1024],
                                       in_=l1_wqkv[c][:, 2048:3072]), "lv%d" % (c % 2), 16)
                lv = cnt("lv%d" % (c % 2))

                @block.tensor
                def _(tensor, tg=tg, c=c, lv=lv):
                    tensor.wait_ge(sems["lv%d" % (c % 2)][0], lv)
                    if c == 0:
                        tensor.wait_ge(sems["vv"][0], cnt("vv"))
                    for tt in range(4):
                        a = tg * 4 + tt
                        for hh in range(2):
                            mm = tensor.matmul(
                                pbt[tt * 2 + hh][:, :],
                                xln_ap(c, a * 128, 128),
                                regb[:, 9216 + (c % 2) * 1024 + hh * 512:
                                     9216 + (c % 2) * 1024 + hh * 512 + 512],
                                start=(c == 0), stop=(c == 7))
                    emit_qkv_l1.vt[tg * 8 + c] = inc(mm, "tt")
            tv = cnt("tt")
            for tt in range(4):
                a = tg * 4 + tt

                @block.vector
                def _(vector, tt=tt, tv=tv, tg=tg):
                    vector.wait_ge(sems["tt"][0], tv)
                    vector.wait_ge(sems["ld"][0], lbv1)
                    if tg > 0:
                        vector.wait_ge(sems["sv%d" % tt][0],
                                       emit_qkv_l1.vdm.get((tg - 1, tt), 0))
                    stg = regb[:, tt * 1024: tt * 1024 + 1024]
                    inc(vector.tensor_tensor(out=stg[:, 0:512], in0=pbt[tt * 2][:, :],
                                             in1=bv1t[:, 0:512], op=OP.add), "vv")
                    inc(vector.tensor_tensor(out=stg[:, 512:1024],
                                             in0=pbt[tt * 2 + 1][:, :],
                                             in1=bv1t[:, 512:1024], op=OP.add), "vv")
                vva = cnt("vv")

                @block.sync
                def _(sync, tt=tt, a=a, vva=vva, tg=tg):
                    sync.wait_ge(sems["vv"][0], vva)
                    for h in range(16):
                        dm = sync.dma_start(
                            out=av[h, :, a, :],
                            in_=regb[:, tt * 1024 + 64 * h: tt * 1024 + 64 * h + 64])
                        inc(dm, "sv%d" % tt, 16)
                    emit_qkv_l1.vdm[(tg, tt)] = cnt("sv%d" % tt)
        barrier()
        emit_plucker()
        barrier()
    emit_qkv_l1.ev = {}
    emit_qkv_l1.dm = {}
    emit_qkv_l1.vt = {}
    emit_qkv_l1.vdm = {}

    def emit_plucker():
        w1p_sb = regb[:, 4096:4608]
        w2p_sb = regb[:, 4608:5120]

        @block.sync
        def _(sync):
            for c in range(8):
                inc(sync.dma_start(out=w1p_sb[:, c * 64:(c + 1) * 64], in_=l1_w1p[c]),
                    "ld", 16)
                inc(sync.dma_start(out=w2p_sb[:, c * 64:(c + 1) * 64], in_=l1_w2p[c]),
                    "ld", 16)
        lwp = cnt("ld")
        pev = {}
        for n in range(4):
            t0 = n * 512

            @block.tensor
            def _(tensor, n=n, t0=t0):
                if n == 0:
                    tensor.wait_ge(sems["ld"][0], lwp)
                else:
                    tensor.wait_ge(sems["ss"][0], pev[n - 1])
                for c in range(8):
                    tensor.matmul(pbt[0][0:64, :], w1p_sb[:, c * 64:(c + 1) * 64],
                                  xln_ap(c, t0), start=(c == 0), stop=(c == 7))
                    mm = tensor.matmul(pbt[1][0:64, :], w2p_sb[:, c * 64:(c + 1) * 64],
                                       xln_ap(c, t0), start=(c == 0), stop=(c == 7))
                inc(mm, "tt")
            tp = cnt("tt")

            @block.scalar
            def _(scalar, n=n, t0=t0, tp=tp):
                scalar.wait_ge(sems["tt"][0], tp)
                inc(scalar.activation(out=regb[0:64, t0:t0 + 512], in_=pbt[0][0:64, :],
                                      func=AF.Copy), "ss")
                inc(scalar.activation(out=regb[0:64, 2048 + t0:2048 + t0 + 512],
                                      in_=pbt[1][0:64, :], func=AF.Copy), "ss")
                pev[n] = cnt("ss")
        sp = cnt("ss")

        shev = {}
        for n in range(4):
            t0 = n * 512

            @block.tensor
            def _(tensor, n=n, t0=t0, sp=sp):
                tensor.wait_ge(sems["ss"][0], sp if n == 0 else shev[n - 1])
                tensor.matmul(pbt[2][0:96, :], spi[:], regb[0:64, t0:t0 + 512],
                              start=True, stop=True)
                tensor.matmul(pbt[3][0:96, :], spj[:],
                              regb[0:64, 2048 + t0:2048 + t0 + 512],
                              start=True, stop=True)
                tensor.matmul(pbt[4][0:96, :], spj[:], regb[0:64, t0:t0 + 512],
                              start=True, stop=True)
                inc(tensor.matmul(pbt[5][0:96, :], spi[:],
                                  regb[0:64, 2048 + t0:2048 + t0 + 512],
                                  start=True, stop=True), "tt")
            ts = cnt("tt")

            @block.scalar
            def _(scalar, n=n, t0=t0, ts=ts):
                scalar.wait_ge(sems["tt"][0], ts)
                inc(scalar.activation(out=regb[0:96, 4096 + t0:4096 + t0 + 512],
                                      in_=pbt[2][0:96, :], func=AF.Copy), "ss")
                inc(scalar.activation(out=regb[0:96, 6144 + t0:6144 + t0 + 512],
                                      in_=pbt[3][0:96, :], func=AF.Copy), "ss")
                inc(scalar.activation(out=regb[0:96, 8192 + t0:8192 + t0 + 512],
                                      in_=pbt[4][0:96, :], func=AF.Copy), "ss")
                inc(scalar.activation(out=regb[0:96, 10240 + t0:10240 + t0 + 512],
                                      in_=pbt[5][0:96, :], func=AF.Copy), "ss")
                shev[n] = cnt("ss")
        ssh = cnt("ss")

        @block.vector
        def _(vector):
            vector.wait_ge(sems["ss"][0], ssh)
            A1 = regb[0:96, 4096:6144]
            B2 = regb[0:96, 6144:8192]
            A2 = regb[0:96, 8192:10240]
            B1 = regb[0:96, 10240:12288]
            inc(vector.tensor_tensor(out=A1, in0=A1, in1=B2, op=OP.mult), "vv")
            inc(vector.tensor_tensor(out=A2, in0=A2, in1=B1, op=OP.mult), "vv")
            vector.drain()
            inc(vector.tensor_tensor(out=A1, in0=A1, in1=A2, op=OP.subtract), "vv")
            vector.drain()
            inc(vector.tensor_tensor(out=B2, in0=A1, in1=A1, op=OP.mult), "vv")
        vl = cnt("vv")

        nev = {}
        for n in range(4):
            t0 = n * 512

            @block.tensor
            def _(tensor, n=n, t0=t0, vl=vl):
                tensor.wait_ge(sems["vv"][0], vl)
                if n > 0:
                    tensor.wait_ge(sems["ss"][0], nev[n - 1])
                inc(tensor.matmul(pbt[0][0:16, :], g96[:],
                                  regb[0:96, 6144 + t0:6144 + t0 + 512],
                                  start=True, stop=True), "tt")
            tn = cnt("tt")

            @block.scalar
            def _(scalar, n=n, t0=t0, tn=tn):
                scalar.wait_ge(sems["tt"][0], tn)
                inc(scalar.activation(out=regb[0:16, 8192 + t0:8192 + t0 + 512],
                                      in_=pbt[0][0:16, :], func=AF.Sqrt), "ss")
                nev[n] = cnt("ss")
        sn = cnt("ss")

        @block.vector
        def _(vector):
            vector.wait_ge(sems["ss"][0], sn)
            nr = regb[0:16, 8192:10240]
            vector.tensor_scalar(out=nr, in0=nr, scalar1=1e-12, scalar2=None,
                                 op0=OP.max)
            vector.drain()
            inc(vector.reciprocal(out=regb[0:16, 10240:12288], in_=nr), "vv")
        vrn = cnt("vv")

        rnv = {}
        jlv = {}
        for n in range(4):
            t0 = n * 512

            @block.tensor
            def _(tensor, n=n, t0=t0, vrn=vrn):
                tensor.wait_ge(sems["vv"][0], vrn if n == 0 else rnv[n - 1])
                inc(tensor.matmul(pbt[1][0:96, :], gt96[:],
                                  regb[0:16, 10240 + t0:10240 + t0 + 512],
                                  start=True, stop=True), "tt")
            tb = cnt("tt")

            @block.vector
            def _(vector, n=n, t0=t0, tb=tb):
                vector.wait_ge(sems["tt"][0], tb)
                inc(vector.tensor_tensor(out=regb[0:96, 4096 + t0:4096 + t0 + 512],
                                         in0=regb[0:96, 4096 + t0:4096 + t0 + 512],
                                         in1=pbt[1][0:96, :], op=OP.mult), "vv")
            vno = cnt("vv")
            rnv[n] = vno

            @block.tensor
            def _(tensor, n=n, t0=t0, vno=vno):
                tensor.wait_ge(sems["vv"][0], vno)
                if n > 0:
                    tensor.wait_ge(sems["ss"][0], jlv[n - 1])
                inc(tensor.matmul(pbt[2][0:96, :], bjc[:],
                                  regb[0:96, 4096 + t0:4096 + t0 + 512],
                                  start=True, stop=True), "tt")
            tj = cnt("tt")

            @block.scalar
            def _(scalar, n=n, t0=t0, tj=tj):
                scalar.wait_ge(sems["tt"][0], tj)
                inc(scalar.activation(out=regb[0:96, 6144 + t0:6144 + t0 + 512],
                                      in_=pbt[2][0:96, :], func=AF.Copy), "ss")
                jlv[n] = cnt("ss")

            @block.vector
            def _(vector, n=n, t0=t0):
                vector.drain()
                inc(vector.tensor_scalar(out=regb[0:96, 8192 + t0:8192 + t0 + 512],
                                         in0=regb[0:96, 4096 + t0:4096 + t0 + 512],
                                         scalar1=ps96[:], scalar2=None,
                                         op0=OP.mult), "vv")
        sJ = cnt("ss")
        vL = cnt("vv")

        @block.sync
        def _(sync):
            sync.wait_ge(sems["ss"][0], sJ)
            sync.wait_ge(sems["vv"][0], vL)
            inc(sync.dma_start(out=aq1[2144:2240, :], in_=regb[0:96, 6144:8192]),
                "st", 16)
            inc(sync.dma_start(out=aq1[2048:2144, :], in_=regb[0:96, 8192:10240]),
                "st", 16)

    def emit_gather(layer):
        aq_src = aq0 if layer == 0 else aq1
        aq_dst = aq0g if layer == 0 else aq1g

        @block.gpsimd
        def _(gpsimd):
            for s in ("st", "sq0", "sq1", "sv0", "sv1", "sv2", "sv3"):
                if cnt(s):
                    gpsimd.wait_ge(sems[s][0], cnt(s))
            inc(gpsimd.collective_compute(
                "AllGather", OP.bypass, replica_groups=GROUPS,
                ins=[aq_src.opt()], outs=[aq_dst.opt()]), "gg")
            inc(gpsimd.collective_compute(
                "AllGather", OP.bypass, replica_groups=GROUPS,
                ins=[av.opt()], outs=[avg.opt()]), "gg")
        barrier()

    def emit_attention(layer):
        aqg = aq0g if layer == 0 else aq1g
        KR = 64 if layer == 0 else 70
        qis = qi0s if layer == 0 else qi1s
        kis = ki0s if layer == 0 else ki1s
        aqg_flat = aqg.rearrange("s r c -> (s r) c")
        avg_flat = avg.rearrange("s h p a c -> (s h p) (a c)")
        rzb = regb[0:64, 8608:8736]
        for j in range(8):
            tprev = cnt("tt")
            sprev = cnt("ss")
            vprev = cnt("vv")

            @block.gpsimd
            def _(gpsimd, j=j, tprev=tprev, sprev=sprev, vprev=vprev,
                  qis=qis, kis=kis, aqg_flat=aqg_flat, avg_flat=avg_flat, KR=KR):
                gpsimd.wait_ge(sems["tt"][0], tprev)
                gpsimd.wait_ge(sems["ss"][0], sprev)
                gpsimd.wait_ge(sems["vv"][0], vprev)
                for s in range(2):
                    inc(gpsimd.indirect_dma_start(
                        out=rega[0:KR, s * 2048:(s + 1) * 2048], out_offset=None,
                        in_=aqg_flat,
                        in_offset=bass.IndirectOffsetOnAxis(
                            ap=qis[:, 2 * j + s:2 * j + s + 1], axis=0)), "lg", 16)
                    inc(gpsimd.indirect_dma_start(
                        out=rega[0:KR, 4096 + s * 2048:4096 + (s + 1) * 2048],
                        out_offset=None, in_=aqg_flat,
                        in_offset=bass.IndirectOffsetOnAxis(
                            ap=kis[:, 2 * j + s:2 * j + s + 1], axis=0)), "lg", 16)
                    inc(gpsimd.indirect_dma_start(
                        out=rega[:, 8192 + s * 1024:8192 + s * 1024 + 1024],
                        out_offset=None, in_=avg_flat,
                        in_offset=bass.IndirectOffsetOnAxis(
                            ap=vis[:, 2 * j + s:2 * j + s + 1], axis=0)), "lg", 16)
            lh = cnt("lg")

            if layer == 0:

                @block.tensor
                def _(tensor, lh=lh):
                    tensor.wait_ge(sems["lg"][0], lh)
                    for g in range(32):
                        mm = tensor.matmul(pbt[6][0:64, g:g + 1],
                                           rega[:, 8192 + g * 64:8192 + g * 64 + 64],
                                           onesd[:, 0:1], start=True, stop=True)
                    inc(mm, "tt")
                tbs = cnt("tt")

                @block.scalar
                def _(scalar, tbs=tbs):
                    scalar.wait_ge(sems["tt"][0], tbs)
                    inc(scalar.activation(out=bsum[:], in_=pbt[6][0:64, 0:32],
                                          func=AF.Copy), "ss")
                sbs = cnt("ss")

                @block.tensor
                def _(tensor, sbs=sbs):
                    tensor.wait_ge(sems["ss"][0], sbs)
                    inc(tensor.transpose(pbt[5][0:32, 0:64], bsum[:],
                                         ident[0:64, 0:64]), "tt")
                tbt = cnt("tt")

                @block.scalar
                def _(scalar, tbt=tbt):
                    scalar.wait_ge(sems["tt"][0], tbt)
                    inc(scalar.activation(out=bsumT[:], in_=pbt[5][0:32, 0:64],
                                          func=AF.Copy), "ss")
                sbt = cnt("ss")

                @block.tensor
                def _(tensor, sbt=sbt):
                    tensor.wait_ge(sems["ss"][0], sbt)
                    inc(tensor.matmul(pbt[7][0:32, 0:64], u32[:], bsumT[:],
                                      start=True, stop=True), "tt")
                tvs = cnt("tt")

                @block.scalar
                def _(scalar, tvs=tvs):
                    scalar.wait_ge(sems["tt"][0], tvs)
                    inc(scalar.activation(out=vsufT[:], in_=pbt[7][0:32, 0:64],
                                          func=AF.Copy), "ss")
                svs = cnt("ss")

                @block.sync
                def _(sync, svs=svs):
                    sync.wait_ge(sems["ss"][0], svs)
                    inc(sync.dma_start(out=rega[0:1, 14336:16384], in_=vsufT[:]),
                        "ld", 16)
                vsuf_ld = cnt("ld")
            else:
                vsuf_ld = 0

            copy_ms = {}
            exp_ms = {}
            rz_ms = {}
            tav_ms = {}
            oev_ms = {}

            def emit_qk_softmax(i, j=j, lh=lh):
                K_i = 128 * (i + 1)
                ncH = (K_i + 511) // 512
                nsuf = T - K_i
                buf = regb[:, (i % 2) * 4096:(i % 2) * 4096 + 4096]
                for kc in range(ncH):
                    w = min(512, K_i - kc * 512)
                    gidx = emit_attention.qk_idx
                    emit_attention.qk_idx += 1
                    bank = gidx % 2

                    @block.tensor
                    def _(tensor, i=i, kc=kc, w=w, gidx=gidx, bank=bank):
                        if i == 0 and kc == 0:
                            tensor.wait_ge(sems["lg"][0], lh)
                        prev = emit_attention.qk_hist.get(gidx - 2)
                        if prev is not None:
                            tensor.wait_ge(sems["vv"][0], prev)
                        inc(tensor.matmul(pbt[bank][:, 0:w],
                                          rega[0:KR, 128 * i:128 * i + 128],
                                          rega[0:KR, 4096 + kc * 512:4096 + kc * 512 + w],
                                          start=True, stop=True), "tt")
                    tq = cnt("tt")

                    @block.vector
                    def _(vector, i=i, kc=kc, w=w, tq=tq, bank=bank, gidx=gidx, buf=buf):
                        vector.wait_ge(sems["tt"][0], tq)
                        if kc == 0 and i >= 2:
                            vector.wait_ge(sems["tt"][0], tav_ms.get(i - 2, 0))
                        mm = vector.tensor_scalar(
                            out=buf[:, kc * 512:kc * 512 + w],
                            in0=pbt[bank][:, 0:w],
                            scalar1=0.0, scalar2=FMIN, op0=OP.add, op1=OP.max,
                            accum_out=stat[:, kc:kc + 1])
                        emit_attention.qk_hist[gidx] = inc(mm, "vv")

                @block.vector
                def _(vector, i=i, ncH=ncH, buf=buf):
                    vector.drain()
                    if layer == 0:
                        vector.tensor_tensor(out=buf[:, 128 * i:128 * i + 128],
                                             in0=buf[:, 128 * i:128 * i + 128],
                                             in1=tril[:], op=OP.mult)
                    else:
                        vector.tensor_tensor(out=buf[:, 128 * i:128 * i + 128],
                                             in0=buf[:, 128 * i:128 * i + 128],
                                             in1=triun[:], op=OP.add)
                    vector.reduce_max(out=stat[:, 8:9], in_=stat[:, 0:ncH],
                                      axis=AX.X)
                    vector.drain()
                    pq = i % 2
                    if layer == 0:
                        inc(vector.tensor_scalar(out=stat[:, 9 + pq:10 + pq],
                                                 in0=stat[:, 8:9],
                                                 scalar1=0.0, scalar2=-1.0,
                                                 op0=OP.max, op1=OP.mult), "vv")
                    else:
                        inc(vector.tensor_scalar(out=stat[:, 9 + pq:10 + pq],
                                                 in0=stat[:, 8:9],
                                                 scalar1=-1.0, scalar2=None, op0=OP.mult),
                            "vv")
                vmask = cnt("vv")

                @block.scalar
                def _(scalar, i=i, K_i=K_i, vmask=vmask, buf=buf):
                    scalar.wait_ge(sems["vv"][0], vmask)
                    pq = i % 2
                    ex = scalar.activation(out=buf[:, 0:K_i], in_=buf[:, 0:K_i],
                                           func=AF.Exp, bias=stat[:, 9 + pq:10 + pq],
                                           accum_out=stat[:, 15 + pq:16 + pq])
                    if layer == 0:
                        scalar.drain()
                        ex = scalar.activation(out=stat[:, 11 + pq:12 + pq],
                                               in_=stat[:, 9 + pq:10 + pq],
                                               func=AF.Exp)
                    exp_ms[i] = inc(ex, "ss")

                @block.vector
                def _(vector, i=i, nsuf=nsuf):
                    vector.wait_ge(sems["ss"][0], exp_ms[i])
                    pq = i % 2
                    if layer == 0 and nsuf > 0:
                        vector.scalar_tensor_tensor(out=stat[:, 15 + pq:16 + pq],
                                                    in0=stat[:, 11 + pq:12 + pq],
                                                    scalar=float(nsuf),
                                                    in1=stat[:, 15 + pq:16 + pq],
                                                    op0=OP.mult, op1=OP.add)
                        vector.drain()
                    inc(vector.reciprocal(out=stat[:, 13 + pq:14 + pq],
                                          in_=stat[:, 15 + pq:16 + pq]), "vv")
                rz_ms[i] = cnt("vv")

            def emit_tav(i, j=j, vsuf_ld=vsuf_ld):
                K_i = 128 * (i + 1)
                nsuf = T - K_i
                buf = regb[:, (i % 2) * 4096:(i % 2) * 4096 + 4096]
                if layer == 0 and nsuf > 0:

                    @block.tensor
                    def _(tensor, i=i):
                        tensor.wait_ge(sems["ss"][0], exp_ms[i])
                        prev = emit_attention.emn_ms.get(i - 1)
                        if prev:
                            tensor.wait_ge(sems["ss"][0], prev)
                        inc(tensor.transpose(pbt[5][0:1, 256:384],
                                             stat[:, 11 + i % 2:12 + i % 2],
                                             ident[:]), "tt")
                    tem = cnt("tt")

                    @block.scalar
                    def _(scalar, i=i, tem=tem):
                        scalar.wait_ge(sems["tt"][0], tem)
                        scalar.wait_ge(sems["tt"][0], tav_ms.get(i - 1, 0))
                        inc(scalar.activation(out=srow[0:1, 0:128],
                                              in_=pbt[5][0:1, 256:384],
                                              func=AF.Copy), "ss")
                    emit_attention.emn_ms[i] = cnt("ss")

                for g in range(i + 1):
                    ptg = emit_attention.pt_idx
                    emit_attention.pt_idx += 1

                    @block.tensor
                    def _(tensor, i=i, g=g, ptg=ptg, buf=buf):
                        if g == 0:
                            tensor.wait_ge(sems["ss"][0], exp_ms[i])
                        prev = emit_attention.pt_hist.get(ptg - 2)
                        if prev is not None:
                            tensor.wait_ge(sems["vv"][0], prev)
                        inc(tensor.transpose(pbt[2 + ptg % 2][:, 0:128],
                                             buf[:, 128 * g:128 * g + 128],
                                             ident[:]), "tt")
                    tpt = cnt("tt")

                    @block.vector
                    def _(vector, i=i, g=g, tpt=tpt, ptg=ptg):
                        vector.wait_ge(sems["tt"][0], tpt)
                        emit_attention.pt_hist[ptg] = inc(vector.tensor_copy(
                            out=regb[:, 8192 + (ptg % 2) * 128:8192 + (ptg % 2) * 128 + 128],
                            in_=pbt[2 + ptg % 2][:, 0:128]), "vv")

                    @block.tensor
                    def _(tensor, i=i, g=g, ptg=ptg):
                        tensor.wait_ge(sems["vv"][0], emit_attention.pt_hist[ptg])
                        if g == 0 and i >= 1:
                            tensor.wait_ge(sems["vv"][0], oev_ms.get(i - 1, 0))
                        last = (g == i) and not (layer == 0 and nsuf > 0)
                        mm = tensor.matmul(pbt[4][0:64, 0:128],
                                           rega[:, 8192 + g * 64:8192 + g * 64 + 64],
                                           regb[:, 8192 + (ptg % 2) * 128:8192 + (ptg % 2) * 128 + 128],
                                           start=(g == 0), stop=last)
                        if last:
                            inc(mm, "tt")

                if layer == 0 and nsuf > 0:

                    @block.tensor
                    def _(tensor, i=i, vsuf_ld=vsuf_ld):
                        tensor.wait_ge(sems["ss"][0], emit_attention.emn_ms[i])
                        tensor.wait_ge(sems["ld"][0], vsuf_ld)
                        inc(tensor.matmul(pbt[4][0:64, 0:128],
                                          rega[0:1, 14336 + 64 * i:14336 + 64 * i + 64],
                                          srow[0:1, 0:128],
                                          start=False, stop=True), "tt")

                @block.tensor
                def _(tensor, i=i):
                    tensor.wait_ge(sems["vv"][0], rz_ms[i])
                    tensor.wait_ge(sems["ss"][0], emit_attention.rzs_ms.get(i - 1, 0))
                    inc(tensor.transpose(pbt[5][0:1, 384:512],
                                         stat[:, 13 + i % 2:14 + i % 2],
                                         ident[:]), "tt")
                trz = cnt("tt")

                @block.scalar
                def _(scalar, i=i, trz=trz):
                    scalar.wait_ge(sems["tt"][0], trz)
                    scalar.wait_ge(sems["tt"][0], tav_ms.get(i - 1, 0))
                    inc(scalar.activation(out=srow[0:1, 128:256],
                                          in_=pbt[5][0:1, 384:512],
                                          func=AF.Copy), "ss")
                emit_attention.rzs_ms[i] = cnt("ss")
                srz = cnt("ss")

                @block.tensor
                def _(tensor, i=i, srz=srz):
                    tensor.wait_ge(sems["ss"][0], srz)
                    inc(tensor.matmul(pbt[5][0:64, 128:256], ones1[0:1, 0:64],
                                      srow[0:1, 128:256], start=True, stop=True),
                        "tt")
                tbc = cnt("tt")

                @block.scalar
                def _(scalar, i=i, tbc=tbc):
                    scalar.wait_ge(sems["tt"][0], tbc)
                    scalar.wait_ge(sems["vv"][0], oev_ms.get(i - 1, 0))
                    inc(scalar.activation(out=rzb[:], in_=pbt[5][0:64, 128:256],
                                          func=AF.Copy), "ss")
                srzb = cnt("ss")
                tav_ms[i] = cnt("tt")

                @block.vector
                def _(vector, i=i, srzb=srzb):
                    vector.wait_ge(sems["tt"][0], tav_ms[i])
                    vector.wait_ge(sems["ss"][0], srzb)
                    if i == 0:
                        vector.wait_ge(sems["sto"][0], cnt("sto"))
                    oev_ms[i] = inc(vector.tensor_tensor(
                        out=rega[0:64, 10240 + 128 * i:10240 + 128 * i + 128],
                        in0=pbt[4][0:64, 0:128], in1=rzb[:], op=OP.mult), "vv")

            for i in range(32):
                emit_qk_softmax(i)
                emit_tav(i)

            @block.sync
            def _(sync, j=j):
                sync.wait_ge(sems["vv"][0], oev_ms[31])
                inc(sync.dma_start(out=om[64 * j:64 * j + 64, :],
                                   in_=rega[0:64, 10240:14336]), "sto", 16)
        barrier()
    emit_attention.qk_hist = {}
    emit_attention.qk_idx = 0
    emit_attention.pt_hist = {}
    emit_attention.pt_idx = 0
    emit_attention.emn_ms = {}
    emit_attention.rzs_ms = {}

    def emit_oproj(layer):
        wsrc = l0_wr if layer == 0 else l1_wout

        @block.sync
        def _(sync):
            for c in range(4):
                inc(sync.dma_start(out=regb[:, 1024 + c * D:1024 + (c + 1) * D],
                                   in_=wsrc[c]), "ld", 16)
        lwr = cnt("ld")
        for tk in range(8):
            for c in range(4):
                idx = tk * 4 + c

                @block.sync
                def _(sync, tk=tk, c=c, idx=idx):
                    if idx >= 2:
                        sync.wait_ge(sems["tt"][0], emit_oproj.tms.get(idx - 2, 0))
                    inc(sync.dma_start(
                        out=regb[:, (idx % 2) * 512:(idx % 2) * 512 + 512],
                        in_=om[128 * c:128 * c + 128, 512 * tk:512 * tk + 512]),
                        "lv%d" % (idx % 2), 16)
                lom = cnt("lv%d" % (idx % 2))

                @block.tensor
                def _(tensor, tk=tk, c=c, idx=idx, lom=lom, lwr=lwr):
                    tensor.wait_ge(sems["lv%d" % (idx % 2)][0], lom)
                    tensor.wait_ge(sems["ld"][0], lwr)
                    if c == 0:
                        tensor.wait_ge(sems["ss"][0], emit_oproj.sms.get(tk - 1, 0))
                    for m in range(8):
                        mm = tensor.matmul(
                            pbt[m][:, :],
                            regb[:, 1024 + c * D + 128 * m:1024 + c * D + 128 * m + 128],
                            regb[:, (idx % 2) * 512:(idx % 2) * 512 + 512],
                            start=(c == 0), stop=(c == 3))
                    emit_oproj.tms[idx] = inc(mm, "tt")
            tpr = cnt("tt")

            @block.scalar
            def _(scalar, tk=tk, tpr=tpr):
                scalar.wait_ge(sems["tt"][0], tpr)
                for m in range(8):
                    su = emit_oproj.slot_use
                    emit_oproj.slot_use += 1
                    prev = emit_oproj.slot_dm.get(su - 4)
                    if prev is not None:
                        scalar.wait_ge(sems["sv%d" % (su % 4)][0], prev)
                    stg = regb[:, 5120 + (su % 4) * 512:5120 + (su % 4) * 512 + 512]
                    inc(scalar.activation(out=stg, in_=pbt[m][:, :], func=AF.Copy), "ss")
                    scalar.drain()
                    dm = scalar.dma_start(
                        out=rs_s[tk // 4, m, :, (tk % 4) * 512:(tk % 4) * 512 + 512],
                        in_=stg)
                    emit_oproj.slot_dm[su] = inc(dm, "sv%d" % (su % 4), 16)
                emit_oproj.sms[tk] = cnt("ss")

        @block.gpsimd
        def _(gpsimd):
            for s in ("st", "sv0", "sv1", "sv2", "sv3"):
                if cnt(s):
                    gpsimd.wait_ge(sems[s][0], cnt(s))
            inc(gpsimd.collective_compute(
                "ReduceScatter", OP.add, replica_groups=GROUPS,
                ins=[rs_s.opt()], outs=[rs_d.opt()]), "gg")
        barrier()
    emit_oproj.tms = {}
    emit_oproj.sms = {}
    emit_oproj.slot_use = 0
    emit_oproj.slot_dm = {}

    def emit_postln_attn(layer):
        bb = 0 if layer == 0 else 16
        for tc in range(4):
            t0 = tc * 512
            vprev = cnt("vv")

            @block.sync
            def _(sync, tc=tc, t0=t0, vprev=vprev):
                sync.wait_ge(sems["vv"][0], vprev)
                for c in range(8):
                    inc(sync.dma_start(out=regb[:, c * 512:(c + 1) * 512],
                                       in_=rs_d[c][:, t0:t0 + 512]), "ld", 16)
            lrs = cnt("ld")

            @block.scalar
            def _(scalar, tc=tc, lrs=lrs, bb=bb):
                scalar.wait_ge(sems["ld"][0], lrs)
                for c in range(8):
                    inc(scalar.activation(out=regb[:, 4096 + c * 512:4096 + (c + 1) * 512],
                                          in_=regb[:, c * 512:(c + 1) * 512],
                                          func=AF.Identity, bias=bcol(bb + c)), "ss")
            sbi = cnt("ss")

            @block.vector
            def _(vector, sbi=sbi):
                vector.wait_ge(sems["ss"][0], sbi)

            emit_ln(lambda c: regb[:, 4096 + c * 512:4096 + (c + 1) * 512],
                    None,
                    lambda c, t0=t0: x_ap(c, t0),
                    512, 8192, 8192 + 4096 - 512)

    def emit_binarize():
        for H in range(32):
            tprev = cnt("tt")

            @block.sync
            def _(sync, H=H, tprev=tprev):
                sync.wait_ge(sems["tt"][0], tprev)
                inc(sync.dma_start(out=regb[:, (H % 2) * 1024:(H % 2) * 1024 + 1024],
                                   in_=l0_w1[H]), "lv%d" % (H % 2), 16)
            lw = cnt("lv%d" % (H % 2))

            @block.scalar
            def _(scalar, H=H, lw=lw):
                scalar.wait_ge(sems["lv%d" % (H % 2)][0], lw)
                inc(scalar.activation(out=regb[:, 2048:3072],
                                      in_=regb[:, (H % 2) * 1024:(H % 2) * 1024 + 1024],
                                      func=AF.Abs), "ss")
                inc(scalar.activation(out=regb[:, 3072:4096],
                                      in_=regb[:, (H % 2) * 1024:(H % 2) * 1024 + 1024],
                                      func=AF.Sign), "ss")
            sab = cnt("ss")

            @block.vector
            def _(vector, H=H, sab=sab):
                vector.wait_ge(sems["ss"][0], sab)
                vector.reduce_sum(out=stat[:, 8:9], in_=regb[:, 2048:3072], axis=AX.X)
                vector.drain()
                inc(vector.tensor_scalar(out=regb[:, 4096:5120], in0=regb[:, 3072:4096],
                                         scalar1=stat[:, 8:9], scalar2=1.0 / 1024.0,
                                         op0=OP.mult, op1=OP.mult), "vv")
            vbw = cnt("vv")

            for c in range(8):

                @block.tensor
                def _(tensor, H=H, c=c, vbw=vbw):
                    if c == 0:
                        tensor.wait_ge(sems["vv"][0], vbw)
                    prevev = emit_binarize.ev.get(c % 2)
                    if prevev:
                        tensor.wait_ge(sems["ss"][0], prevev)
                    inc(tensor.transpose(pbt[c % 2][:, 0:128],
                                         regb[:, 4096 + 128 * c:4096 + 128 * c + 128],
                                         ident[:]), "tt")
                ttr = cnt("tt")

                @block.scalar
                def _(scalar, H=H, c=c, ttr=ttr):
                    scalar.wait_ge(sems["tt"][0], ttr)
                    prevdm = emit_binarize.dm.get(c % 4)
                    if prevdm:
                        scalar.wait_ge(sems["sv%d" % (c % 4)][0], prevdm)
                    stg = regb[:, 5120 + (c % 4) * 128:5120 + (c % 4) * 128 + 128]
                    emit_binarize.ev[c % 2] = inc(
                        scalar.activation(out=stg, in_=pbt[c % 2][:, 0:128],
                                          func=AF.Copy), "ss")
                    scalar.drain()
                    emit_binarize.dm[c % 4] = inc(
                        scalar.dma_start(out=bw1t[H][:, 128 * c:128 * c + 128], in_=stg),
                        "sv%d" % (c % 4), 16)
        barrier()
        for m in range(8):
            tprev = cnt("tt")

            @block.sync
            def _(sync, m=m, tprev=tprev):
                sync.wait_ge(sems["tt"][0], tprev)
                inc(sync.dma_start(out=regb[:, 0:4096], in_=l0_w2[m]), "ld", 16)
            lw = cnt("ld")

            @block.scalar
            def _(scalar, m=m, lw=lw):
                scalar.wait_ge(sems["ld"][0], lw)
                inc(scalar.activation(out=regb[:, 4096:8192], in_=regb[:, 0:4096],
                                      func=AF.Abs), "ss")
                scalar.drain()
                inc(scalar.activation(out=regb[:, 0:4096], in_=regb[:, 0:4096],
                                      func=AF.Sign), "ss")
            sab = cnt("ss")

            @block.vector
            def _(vector, m=m, sab=sab):
                vector.wait_ge(sems["ss"][0], sab)
                vector.reduce_sum(out=stat[:, 8:9], in_=regb[:, 4096:8192], axis=AX.X)
                vector.drain()
                inc(vector.tensor_scalar(out=regb[:, 4096:8192], in0=regb[:, 0:4096],
                                         scalar1=stat[:, 8:9], scalar2=1.0 / 4096.0,
                                         op0=OP.mult, op1=OP.mult), "vv")
            vbw = cnt("vv")

            for c in range(32):

                @block.tensor
                def _(tensor, m=m, c=c, vbw=vbw):
                    if c == 0:
                        tensor.wait_ge(sems["vv"][0], vbw)
                    prevev = emit_binarize.ev2.get(c % 2)
                    if prevev:
                        tensor.wait_ge(sems["ss"][0], prevev)
                    inc(tensor.transpose(pbt[c % 2][:, 0:128],
                                         regb[:, 4096 + 128 * c:4096 + 128 * c + 128],
                                         ident[:]), "tt")
                ttr = cnt("tt")

                @block.scalar
                def _(scalar, m=m, c=c, ttr=ttr):
                    scalar.wait_ge(sems["tt"][0], ttr)
                    prevdm = emit_binarize.dm2.get(c % 4)
                    if prevdm:
                        scalar.wait_ge(sems["sv%d" % (c % 4)][0], prevdm)
                    stg = regb[:, 8192 + (c % 4) * 128:8192 + (c % 4) * 128 + 128]
                    emit_binarize.ev2[c % 2] = inc(
                        scalar.activation(out=stg, in_=pbt[c % 2][:, 0:128],
                                          func=AF.Copy), "ss")
                    scalar.drain()
                    emit_binarize.dm2[c % 4] = inc(
                        scalar.dma_start(out=bw2t[c][:, 128 * m:128 * m + 128], in_=stg),
                        "sv%d" % (c % 4), 16)
        barrier()
    emit_binarize.ev = {}
    emit_binarize.dm = {}
    emit_binarize.ev2 = {}
    emit_binarize.dm2 = {}

    def emit_ffn0():
        for tc in range(8):
            t0 = tc * 256
            for Ht in range(32):
                idx = tc * 32 + Ht
                tpv = cnt("tt")

                @block.sync
                def _(sync, Ht=Ht, idx=idx):
                    if idx >= 2:
                        sync.wait_ge(sems["tt"][0], emit_ffn0.t1.get(idx - 2, 0))
                    inc(sync.dma_start(out=regb[:, 512 + (idx % 2) * 1024:
                                                512 + (idx % 2) * 1024 + 1024],
                                       in_=bw1t[Ht]), "lv%d" % (idx % 2), 16)
                l1_ = cnt("lv%d" % (idx % 2))

                @block.tensor
                def _(tensor, Ht=Ht, t0=t0, idx=idx, l1_=l1_):
                    tensor.wait_ge(sems["lv%d" % (idx % 2)][0], l1_)
                    if idx >= 2:
                        tensor.wait_ge(sems["ss"][0], emit_ffn0.sg.get(idx - 2, 0))
                    for c in range(8):
                        mm = tensor.matmul(
                            pbt[idx % 2][:, 0:256],
                            regb[:, 512 + (idx % 2) * 1024 + 128 * c:
                                 512 + (idx % 2) * 1024 + 128 * c + 128],
                            xln_ap(c, t0, 256), start=(c == 0), stop=(c == 7))
                    emit_ffn0.t1[idx] = inc(mm, "tt")
                th1 = cnt("tt")

                @block.scalar
                def _(scalar, Ht=Ht, idx=idx, th1=th1):
                    scalar.wait_ge(sems["tt"][0], th1)
                    if idx >= 2:
                        scalar.wait_ge(sems["tt"][0], emit_ffn0.t2.get(idx - 2, 0))
                    emit_ffn0.sg[idx] = inc(scalar.activation(
                        out=regb[:, (idx % 2) * 256:(idx % 2) * 256 + 256],
                        in_=pbt[idx % 2][:, 0:256], func=AF.Gelu,
                        bias=b1sb[:, Ht:Ht + 1]), "ss")

                @block.sync
                def _(sync, Ht=Ht, idx=idx):
                    if idx >= 2:
                        sync.wait_ge(sems["tt"][0], emit_ffn0.t2.get(idx - 2, 0))
                    inc(sync.dma_start(out=regb[:, 2560 + (idx % 2) * 1024:
                                                2560 + (idx % 2) * 1024 + 1024],
                                       in_=bw2t[Ht]), "lw%d" % (2 + idx % 2), 16)
                l2_ = cnt("lw%d" % (2 + idx % 2))

                @block.tensor
                def _(tensor, Ht=Ht, idx=idx, l2_=l2_):
                    tensor.wait_ge(sems["lw%d" % (2 + idx % 2)][0], l2_)
                    tensor.wait_ge(sems["ss"][0], emit_ffn0.sg[idx])
                    if idx >= 1:
                        tensor.wait_ge(sems["vv"][0], emit_ffn0.ya.get(idx - 1, 0))
                    for m in range(8):
                        mm = tensor.matmul(
                            pbt[2 + m // 2][:, (m % 2) * 256:(m % 2) * 256 + 256],
                            regb[:, 2560 + (idx % 2) * 1024 + 128 * m:
                                 2560 + (idx % 2) * 1024 + 128 * m + 128],
                            regb[:, (idx % 2) * 256:(idx % 2) * 256 + 256],
                            start=True, stop=True, skip_group_check=True)
                    emit_ffn0.t2[idx] = inc(mm, "tt")
                th2 = cnt("tt")

                @block.vector
                def _(vector, Ht=Ht, idx=idx, th2=th2):
                    vector.wait_ge(sems["tt"][0], th2)
                    vector.drain()
                    for m in range(8):
                        src = pbt[2 + m // 2][:, (m % 2) * 256:(m % 2) * 256 + 256]
                        ydst = regb[:, 4608 + m * 256:4608 + m * 256 + 256]
                        if Ht == 0:
                            inc(vector.tensor_copy(out=ydst, in_=src), "vv")
                        else:
                            inc(vector.tensor_tensor(out=ydst, in0=ydst, in1=src,
                                                     op=OP.add), "vv")
                    emit_ffn0.ya[idx] = cnt("vv")
            yb = cnt("vv")

            @block.vector
            def _(vector, tc=tc, yb=yb):
                vector.drain()
                for c in range(8):
                    inc(vector.tensor_scalar(
                        out=regb[:, 4608 + c * 256:4608 + c * 256 + 256],
                        in0=regb[:, 4608 + c * 256:4608 + c * 256 + 256],
                        scalar1=bcol(8 + c), scalar2=None, op0=OP.add), "vv")

            emit_ln(lambda c: regb[:, 4608 + c * 256:4608 + c * 256 + 256],
                    None,
                    lambda c, t0=t0: x_ap(c, t0, 256),
                    256, 6656, 8704)
        barrier()
    emit_ffn0.t1 = {}
    emit_ffn0.t2 = {}
    emit_ffn0.sg = {}
    emit_ffn0.ya = {}

    def emit_routing():
        @block.sync
        def _(sync):
            for c in range(8):
                inc(sync.dma_start(out=gw_sb[:, c * 16:(c + 1) * 16], in_=m_gatew[c]),
                    "ld", 16)
        lgw = cnt("ld")
        glev = {}
        for n in range(4):
            t0 = n * 512

            @block.tensor
            def _(tensor, n=n, t0=t0, lgw=lgw):
                if n == 0:
                    tensor.wait_ge(sems["ld"][0], lgw)
                else:
                    tensor.wait_ge(sems["ss"][0], glev[n - 1])
                for c in range(8):
                    mm = tensor.matmul(pbt[0][0:16, :], gw_sb[:, c * 16:(c + 1) * 16],
                                       xln_ap(c, t0), start=(c == 0), stop=(c == 7))
                inc(mm, "tt")
            tg = cnt("tt")

            @block.scalar
            def _(scalar, n=n, t0=t0, tg=tg):
                scalar.wait_ge(sems["tt"][0], tg)
                inc(scalar.activation(out=gcf[:, t0:t0 + 512], in_=pbt[0][0:16, :],
                                      func=AF.Copy), "ss")
                glev[n] = cnt("ss")
        sg = cnt("ss")
        for tt in range(16):

            @block.tensor
            def _(tensor, tt=tt, sg=sg):
                if tt == 0:
                    tensor.wait_ge(sems["ss"][0], sg)
                if tt >= 2:
                    tensor.wait_ge(sems["ss"][0], emit_routing.ev.get(tt - 2, 0))
                inc(tensor.transpose(pbt[tt % 2][0:128, 0:16],
                                     gcf[:, 128 * tt:128 * tt + 128],
                                     ident[0:16, 0:16]), "tt")
            ttr = cnt("tt")

            @block.scalar
            def _(scalar, tt=tt, ttr=ttr):
                scalar.wait_ge(sems["tt"][0], ttr)
                emit_routing.ev[tt] = inc(scalar.activation(
                    out=glt[:, 16 * tt:16 * tt + 16], in_=pbt[tt % 2][0:128, 0:16],
                    func=AF.Copy), "ss")
        sev = cnt("ss")
        for tt in range(16):
            gl = glt[:, 16 * tt:16 * tt + 16]
            sreg = glt[:, 256 + 16 * tt:256 + 16 * tt + 16]
            msk = glt[:, 512 + 16 * tt:512 + 16 * tt + 16]
            cmb = glt[:, 768 + 16 * tt:768 + 16 * tt + 16]

            @block.vector
            def _(vector, tt=tt, gl=gl, sev=sev):
                if tt == 0:
                    vector.wait_ge(sems["ss"][0], sev)
                vector.reduce_max(out=stat[:, 8:9], in_=gl, axis=AX.X)
                vector.drain()
                inc(vector.tensor_scalar(out=stat[:, 9:10], in0=stat[:, 8:9],
                                         scalar1=-1.0, scalar2=None, op0=OP.mult), "vv")
            v1 = cnt("vv")

            @block.scalar
            def _(scalar, tt=tt, gl=gl, sreg=sreg, v1=v1):
                scalar.wait_ge(sems["vv"][0], v1)
                inc(scalar.activation(out=sreg, in_=gl, func=AF.Exp,
                                      bias=stat[:, 9:10], accum_out=stat[:, 10:11]),
                    "ss")
            s1 = cnt("ss")

            @block.vector
            def _(vector, tt=tt, sreg=sreg, msk=msk, cmb=cmb, s1=s1):
                vector.wait_ge(sems["ss"][0], s1)
                vector.reciprocal(out=stat[:, 11:12], in_=stat[:, 10:11])
                vector.drain()
                vector.tensor_scalar(out=sreg, in0=sreg, scalar1=stat[:, 11:12],
                                     scalar2=None, op0=OP.mult)
                vector.drain()
                vector.tensor_tensor(out=msk, in0=sreg, in1=gateb[:], op=OP.add)
                vector.drain()
                for g in range(4):
                    mg = msk[:, 4 * g:4 * g + 4]
                    tmp = cmb[:, 4 * g:4 * g + 4]
                    vector.reduce_max(out=stat[:, 12:13], in_=mg, axis=AX.X)
                    vector.drain()
                    vector.tensor_scalar(out=tmp, in0=mg, scalar1=stat[:, 12:13],
                                         scalar2=None, op0=OP.is_equal)
                    vector.drain()
                    vector.scalar_tensor_tensor(out=tmp, in0=tmp, scalar=NEG,
                                                in1=mg, op0=OP.mult, op1=OP.add)
                    vector.drain()
                    vector.reduce_max(out=stat[:, 13:14], in_=tmp, axis=AX.X)
                    vector.drain()
                    vector.tensor_tensor(out=stat[:, 16 + g:17 + g],
                                         in0=stat[:, 12:13], in1=stat[:, 13:14],
                                         op=OP.add)
                    vector.drain()
                gsc = stat[:, 16:20]
                vector.reduce_max(out=stat[:, 12:13], in_=gsc, axis=AX.X)
                vector.drain()
                vector.tensor_scalar(out=stat[:, 20:24], in0=gsc,
                                     scalar1=stat[:, 12:13], scalar2=None,
                                     op0=OP.is_equal)
                vector.drain()
                vector.scalar_tensor_tensor(out=stat[:, 20:24], in0=stat[:, 20:24],
                                            scalar=NEG, in1=gsc, op0=OP.mult,
                                            op1=OP.add)
                vector.drain()
                vector.reduce_max(out=stat[:, 13:14], in_=stat[:, 20:24], axis=AX.X)
                vector.drain()
                for g in range(4):
                    vector.tensor_scalar(out=stat[:, 20 + g:21 + g],
                                         in0=stat[:, 16 + g:17 + g],
                                         scalar1=stat[:, 13:14], scalar2=1.0,
                                         op0=OP.is_ge, op1=OP.subtract)
                    vector.drain()
                    vector.tensor_scalar(out=stat[:, 20 + g:21 + g],
                                         in0=stat[:, 20 + g:21 + g],
                                         scalar1=-NEG, scalar2=None, op0=OP.mult)
                    vector.drain()
                    vector.tensor_scalar(out=msk[:, 4 * g:4 * g + 4],
                                         in0=msk[:, 4 * g:4 * g + 4],
                                         scalar1=stat[:, 20 + g:21 + g],
                                         scalar2=None, op0=OP.add)
                    vector.drain()
                vector.reduce_max(out=stat[:, 12:13], in_=msk, axis=AX.X)
                vector.drain()
                vector.tensor_scalar(out=cmb, in0=msk, scalar1=stat[:, 12:13],
                                     scalar2=None, op0=OP.is_equal)
                vector.drain()
                vector.scalar_tensor_tensor(out=cmb, in0=cmb, scalar=NEG, in1=msk,
                                            op0=OP.mult, op1=OP.add)
                vector.drain()
                vector.reduce_max(out=stat[:, 13:14], in_=cmb, axis=AX.X)
                vector.drain()
                vector.tensor_scalar(out=cmb, in0=msk, scalar1=stat[:, 13:14],
                                     scalar2=None, op0=OP.is_ge)
                vector.drain()
                inc(vector.tensor_tensor(out=cmb, in0=cmb, in1=sreg, op=OP.mult),
                    "vv")
        vcm = cnt("vv")
        for tt in range(16):

            @block.tensor
            def _(tensor, tt=tt, vcm=vcm):
                if tt == 0:
                    tensor.wait_ge(sems["vv"][0], vcm)
                if tt >= 2:
                    tensor.wait_ge(sems["ss"][0], emit_routing.ev2.get(tt - 2, 0))
                inc(tensor.transpose(pbt[tt % 2][0:16, 0:128],
                                     glt[:, 768 + 16 * tt:768 + 16 * tt + 16],
                                     ident[:]), "tt")
            ttr = cnt("tt")

            @block.scalar
            def _(scalar, tt=tt, ttr=ttr):
                scalar.wait_ge(sems["tt"][0], ttr)
                emit_routing.ev2[tt] = inc(scalar.activation(
                    out=gcf[:, 128 * tt:128 * tt + 128], in_=pbt[tt % 2][0:16, 0:128],
                    func=AF.Copy), "ss")
        barrier()
    emit_routing.ev = {}
    emit_routing.ev2 = {}

    def emit_moe():
        for tc in range(4):
            t0 = tc * 512

            @block.sync
            def _(sync, tc=tc):
                sync.wait_ge(sems["tt"][0], cnt("tt"))
                sync.wait_ge(sems["vv"][0], cnt("vv"))
                inc(sync.dma_start(out=regb[0:16, 4096:5120], in_=m_be2), "ld", 16)
            lbe2 = cnt("ld")

            @block.tensor
            def _(tensor, tc=tc, t0=t0, lbe2=lbe2):
                tensor.wait_ge(sems["ld"][0], lbe2)
                tensor.wait_ge(sems["vv"][0], cnt("vv"))
                for m in range(8):
                    mm = tensor.matmul(pbt[m][:, :],
                                       regb[0:16, 4096 + 128 * m:4096 + 128 * m + 128],
                                       gcf[:, t0:t0 + 512],
                                       start=True, stop=True)
                inc(mm, "tt")
            tbe = cnt("tt")

            @block.scalar
            def _(scalar, tc=tc, tbe=tbe):
                scalar.wait_ge(sems["tt"][0], tbe)
                for m in range(8):
                    inc(scalar.activation(out=regb[:, 8192 + m * 512:8192 + m * 512 + 512],
                                          in_=pbt[m][:, :], func=AF.Copy), "ss")
            barrier(engs=["tensor", "scalar", "vector", "sync"], on=("tt", "ss"))

            def shared_mm(wsrc, rhsf, twait_ss=0):
                shm = {}
                tentry = cnt("tt")
                for c in range(8):

                    @block.sync
                    def _(sync, c=c, wsrc=wsrc, tentry=tentry):
                        if c < 2:
                            sync.wait_ge(sems["tt"][0], tentry)
                        else:
                            sync.wait_ge(sems["tt"][0], shm[c - 2])
                        inc(sync.dma_start(out=regb[:, 4096 + (c % 2) * 1024:
                                                    4096 + (c % 2) * 1024 + 1024],
                                           in_=wsrc[c]), "lv%d" % (c % 2), 16)
                    lw = cnt("lv%d" % (c % 2))

                    @block.tensor
                    def _(tensor, c=c, lw=lw, twait_ss=twait_ss, rhsf=rhsf):
                        tensor.wait_ge(sems["lv%d" % (c % 2)][0], lw)
                        if c == 0 and twait_ss:
                            tensor.wait_ge(sems["ss"][0], twait_ss)
                        for m in range(8):
                            mm = tensor.matmul(
                                pbt[m][:, :],
                                regb[:, 4096 + (c % 2) * 1024 + 128 * m:
                                     4096 + (c % 2) * 1024 + 128 * m + 128],
                                rhsf(c), start=(c == 0), stop=(c == 7))
                        shm[c] = inc(mm, "tt")
                return cnt("tt")

            tsh1 = shared_mm(m_ws1, lambda c: xln_ap(c, t0))

            @block.scalar
            def _(scalar, tsh1=tsh1):
                scalar.wait_ge(sems["tt"][0], tsh1)
                for m in range(8):
                    inc(scalar.activation(
                        out=regb[:, m * 512:m * 512 + 512],
                        in_=pbt[m][:, :], func=AF.Silu, bias=bcol(48 + m)), "ss")
            s1done = cnt("ss")

            tsh3 = shared_mm(m_ws3, lambda c: xln_ap(c, t0), twait_ss=s1done)

            @block.vector
            def _(vector, tsh3=tsh3, s1done=s1done):
                vector.wait_ge(sems["tt"][0], tsh3)
                vector.wait_ge(sems["ss"][0], s1done)
                for m in range(8):
                    inc(vector.scalar_tensor_tensor(
                        out=regb[:, m * 512:m * 512 + 512],
                        in0=pbt[m][:, :], scalar=bcol(56 + m),
                        in1=regb[:, m * 512:m * 512 + 512],
                        op0=OP.add, op1=OP.mult), "vv")
            vh3 = cnt("vv")

            @block.tensor
            def _(tensor, vh3=vh3):
                tensor.wait_ge(sems["vv"][0], vh3)
            tz = shared_mm(m_ws2, lambda c: regb[:, c * 512:c * 512 + 512])

            @block.vector
            def _(vector, tz=tz):
                vector.wait_ge(sems["tt"][0], tz)
                vector.drain()
                for m in range(8):
                    inc(vector.scalar_tensor_tensor(
                        out=regb[:, 8192 + m * 512:8192 + m * 512 + 512],
                        in0=pbt[m][:, :], scalar=bcol(24 + m),
                        in1=regb[:, 8192 + m * 512:8192 + m * 512 + 512],
                        op0=OP.add, op1=OP.add), "vv")
            barrier(engs=["tensor", "scalar", "vector", "sync"], on=("tt", "vv", "ss"))

            for e in range(16):

                @block.sync
                def _(sync, e=e, tc=tc):
                    sync.wait_ge(sems["tt"][0], cnt("tt"))
                    for c in range(8):
                        inc(sync.dma_start(out=regb[:, 4096 + c * 512:4096 + c * 512 + 512],
                                           in_=m_we1[e, c]), "ld", 16)
                    inc(sync.dma_start(out=be1sb[:], in_=m_be1[e]), "ld", 16)
                    inc(sync.dma_start(out=be3sb[:], in_=m_be3[e]), "ld", 16)
                lw1 = cnt("ld")

                @block.tensor
                def _(tensor, e=e, t0=t0, lw1=lw1):
                    tensor.wait_ge(sems["ld"][0], lw1)
                    tensor.wait_ge(sems["vv"][0], cnt("vv"))
                    tensor.matmul(pbt[6][:, :], sel16[:, 128 * e:128 * e + 128],
                                  gcf[0:16, t0:t0 + 512], start=True, stop=True)
                    for mt in range(4):
                        for c in range(8):
                            mm = tensor.matmul(
                                pbt[mt][:, :],
                                regb[:, 4096 + c * 512 + mt * 128:
                                     4096 + c * 512 + mt * 128 + 128],
                                xln_ap(c, t0), start=(c == 0), stop=(c == 7))
                    inc(mm, "tt")
                t1 = cnt("tt")

                @block.scalar
                def _(scalar, e=e, t1=t1):
                    scalar.wait_ge(sems["tt"][0], t1)
                    for mt in range(4):
                        inc(scalar.activation(out=regb[:, mt * 512:mt * 512 + 512],
                                              in_=pbt[mt][:, :], func=AF.Silu,
                                              bias=be1sb[:, mt:mt + 1]), "ss")
                s1_ = cnt("ss")

                @block.sync
                def _(sync, e=e, s1_=s1_):
                    sync.wait_ge(sems["tt"][0], cnt("tt"))
                    for c in range(8):
                        inc(sync.dma_start(out=regb[:, 4096 + c * 512:4096 + c * 512 + 512],
                                           in_=m_we3[e, c]), "ld", 16)
                lw3 = cnt("ld")

                @block.tensor
                def _(tensor, e=e, t0=t0, lw3=lw3, s1_=s1_):
                    tensor.wait_ge(sems["ld"][0], lw3)
                    tensor.wait_ge(sems["ss"][0], s1_)
                    for mt in range(4):
                        for c in range(8):
                            mm = tensor.matmul(
                                pbt[mt][:, :],
                                regb[:, 4096 + c * 512 + mt * 128:
                                     4096 + c * 512 + mt * 128 + 128],
                                xln_ap(c, t0), start=(c == 0), stop=(c == 7))
                    inc(mm, "tt")
                t3 = cnt("tt")

                @block.vector
                def _(vector, e=e, t3=t3):
                    vector.wait_ge(sems["tt"][0], t3)
                    for mt in range(4):
                        inc(vector.scalar_tensor_tensor(
                            out=regb[:, 2048 + mt * 512:2048 + mt * 512 + 512],
                            in0=pbt[mt][:, :], scalar=be3sb[:, mt:mt + 1],
                            in1=regb[:, mt * 512:mt * 512 + 512],
                            op0=OP.add, op1=OP.mult), "vv")
                    vector.drain()
                    for mt in range(4):
                        inc(vector.tensor_tensor(
                            out=regb[:, mt * 512:mt * 512 + 512],
                            in0=regb[:, 2048 + mt * 512:2048 + mt * 512 + 512],
                            in1=pbt[6][:, :], op=OP.mult), "vv")
                vht = cnt("vv")

                @block.sync
                def _(sync, e=e, vht=vht):
                    sync.wait_ge(sems["tt"][0], cnt("tt"))
                    for hc in range(4):
                        inc(sync.dma_start(out=regb[:, 4096 + hc * 1024:4096 + hc * 1024 + 1024],
                                           in_=m_we2[e, hc]), "ld", 16)
                lw2 = cnt("ld")

                @block.tensor
                def _(tensor, e=e, lw2=lw2, vht=vht):
                    tensor.wait_ge(sems["ld"][0], lw2)
                    tensor.wait_ge(sems["vv"][0], vht)
                    for hc in range(4):
                        for m in range(8):
                            mm = tensor.matmul(
                                pbt[m][:, :],
                                regb[:, 4096 + hc * 1024 + 128 * m:
                                     4096 + hc * 1024 + 128 * m + 128],
                                regb[:, hc * 512:hc * 512 + 512],
                                start=(hc == 0), stop=(hc == 3))
                    inc(mm, "tt")
                t2 = cnt("tt")

                @block.vector
                def _(vector, e=e, t2=t2):
                    vector.wait_ge(sems["tt"][0], t2)
                    vector.drain()
                    for m in range(8):
                        inc(vector.tensor_tensor(
                            out=regb[:, 8192 + m * 512:8192 + m * 512 + 512],
                            in0=regb[:, 8192 + m * 512:8192 + m * 512 + 512],
                            in1=pbt[m][:, :], op=OP.add), "vv")

                @block.tensor
                def _(tensor, e=e):
                    tensor.wait_ge(sems["vv"][0], cnt("vv"))

            emit_ln(lambda c: regb[:, 8192 + c * 512:8192 + c * 512 + 512],
                    None,
                    lambda c, t0=t0: x_ap(c, t0),
                    512, 0, 4096)
        barrier()

    emit_preln([0, 512, 1024, 1536])
    barrier()
    emit_qkv_l0()
    emit_gather(0)
    emit_attention(0)
    emit_oproj(0)
    emit_postln_attn(0)
    if stage >= 2:
        emit_preln([0, 512, 1024, 1536])
        barrier()
        emit_binarize()
        emit_ffn0()
    if stage >= 3:
        emit_preln([0, 512, 1024, 1536])
        barrier()
        emit_qkv_l1()
        emit_gather(1)
        emit_attention(1)
        emit_oproj(1)
        emit_postln_attn(1)
    if stage >= 4:
        emit_preln([0, 512, 1024, 1536])
        barrier()
        emit_routing()
        emit_moe()
    barrier()

    @block.sync
    def _(sync):
        for c in range(8):
            inc(sync.dma_start(out=out[c], in_=x_sb[:, c * TH:(c + 1) * TH]), "oo", 16)
    OO = cnt("oo")

    @block.gpsimd
    def _(gpsimd):
        gpsimd.wait_ge(sems["oo"][0], OO)

    blkcm.__exit__(None, None, None)
    for cm in reversed(ctx):
        cm.__exit__(None, None, None)
    return nc


def _prep_inputs(inputs, core):
    b, r = core // 2, core % 2
    slot = list(range(16))
    pi = np.arange(1024)
    myrows = np.concatenate([np.arange(64 * h, 64 * h + 64)
                             for h in range(8 * r, 8 * r + 8)])
    C = np.ascontiguousarray

    d = {}
    x = np.asarray(inputs["L0_attn_x"][b, r * TH:(r + 1) * TH, :])
    d["x"] = C(x.T[pi].reshape(8, 128, TH))

    w0 = np.asarray(inputs["L0_attn_Wqkv"])
    bq0 = np.asarray(inputs["L0_attn_bqkv"])
    d["l0_wqk"] = C(np.concatenate([w0[:, 0:128], w0[:, 0:128]], axis=0))
    d["l0_wv"] = C(np.concatenate([w0[:, 128:192], w0[:, 128:192]], axis=0))
    d["l0_bqk"] = C(bq0[0:128].reshape(128, 1))
    d["l0_sqk"] = C(np.array([0.125] * 64 + [1.0] * 64, np.float32).reshape(128, 1))
    d["l0_bv"] = C(np.broadcast_to(np.tile(bq0[128:192], 16), (128, 1024)))
    wr = np.asarray(inputs["L0_attn_Wr"])
    d["l0_wr"] = C(wr[myrows][:, pi].reshape(4, 128, D))
    w1 = np.asarray(inputs["L0_ffn_w1"])
    d["l0_w1"] = C(w1[:, pi].reshape(32, 128, D))
    d["l0_b1"] = C(np.asarray(inputs["L0_ffn_b1"]).reshape(32, 128).T)
    w2 = np.asarray(inputs["L0_ffn_w2"])
    d["l0_w2"] = C(w2[pi, :].reshape(8, 128, 4096))

    wq1 = np.asarray(inputs["L1_attn_Wqkv"])
    colp = np.concatenate([np.concatenate([np.arange(64 * h, 64 * h + 64) + 1024 * s
                                           for h in slot]) for s in range(3)])
    d["l1_wqkv"] = C(wq1[pi][:, colp].reshape(8, 128, 3072))
    bq1 = np.asarray(inputs["L1_attn_bqkv"])[colp]
    d["l1_bv"] = C(np.broadcast_to(bq1[2048:3072], (128, 1024)))
    w1p = np.asarray(inputs["L1_attn_W1p"])
    w2p = np.asarray(inputs["L1_attn_W2p"])
    colp4 = np.concatenate([np.arange(4 * h, 4 * h + 4) for h in slot])
    d["l1_w1p"] = C(w1p[pi][:, colp4].reshape(8, 128, 64))
    d["l1_w2p"] = C(w2p[pi][:, colp4].reshape(8, 128, 64))
    d["l1_ps96"] = C(np.repeat(np.asarray(inputs["L1_attn_plucker_scale"])[slot], 6)
                     .reshape(96, 1))
    wout = np.asarray(inputs["L1_attn_Wout"])
    d["l1_wout"] = C(wout[myrows][:, pi].reshape(4, 128, D))

    d["m_gatew"] = C(np.asarray(inputs["L1_ffn_gate_w"]).T[pi].reshape(8, 128, 16))
    d["m_gateb"] = C(np.broadcast_to(np.asarray(inputs["L1_ffn_gate_b"]), (128, 16)))
    d["m_we1"] = C(np.asarray(inputs["L1_ffn_We1"])[:, pi, :].reshape(16, 8, 128, 512))
    d["m_be1"] = C(np.asarray(inputs["L1_ffn_be1"]).reshape(16, 4, 128)
                   .transpose(0, 2, 1))
    d["m_we3"] = C(np.asarray(inputs["L1_ffn_We3"])[:, pi, :].reshape(16, 8, 128, 512))
    d["m_be3"] = C(np.asarray(inputs["L1_ffn_be3"]).reshape(16, 4, 128)
                   .transpose(0, 2, 1))
    d["m_we2"] = C(np.asarray(inputs["L1_ffn_We2"])[:, :, pi].reshape(16, 4, 128, D))
    d["m_be2"] = C(np.asarray(inputs["L1_ffn_be2"])[:, pi])
    d["m_ws1"] = C(np.asarray(inputs["L1_ffn_ws1"])[pi].reshape(8, 128, 1024))
    d["m_ws3"] = C(np.asarray(inputs["L1_ffn_ws3"])[pi].reshape(8, 128, 1024))
    d["m_ws2"] = C(np.asarray(inputs["L1_ffn_ws2"])[:, pi].reshape(8, 128, D))

    bp = np.zeros((128, 64), np.float32)
    bp[:, 0:8] = np.asarray(inputs["L0_attn_br"])[pi].reshape(8, 128).T
    bp[:, 8:16] = np.asarray(inputs["L0_ffn_b2"])[pi].reshape(8, 128).T
    bp[:, 16:24] = np.asarray(inputs["L1_attn_bout"])[pi].reshape(8, 128).T
    bp[:, 24:32] = np.asarray(inputs["L1_ffn_bs2"])[pi].reshape(8, 128).T
    bp[:, 32:48] = bq1[0:2048].reshape(16, 128).T
    bp[:, 48:56] = np.asarray(inputs["L1_ffn_bs1"]).reshape(8, 128).T
    bp[:, 56:64] = np.asarray(inputs["L1_ffn_bs3"]).reshape(8, 128).T
    d["b_pack"] = C(bp)

    d["c_ident"] = np.eye(128, dtype=np.float32)
    jj = np.arange(128)
    d["c_tril"] = C((jj[None, :] < jj[:, None]).astype(np.float32))
    d["c_triuneg"] = C(np.where(jj[None, :] > jj[:, None], NEG, 0.0)
                       .astype(np.float32))
    PI_ = np.array([0, 0, 0, 1, 1, 2])
    PJ_ = np.array([1, 2, 3, 2, 3, 3])
    spi_ = np.zeros((64, 96), np.float32)
    spj_ = np.zeros((64, 96), np.float32)
    for h in range(16):
        for mth in range(6):
            spi_[4 * h + PI_[mth], 6 * h + mth] = 1.0
            spj_[4 * h + PJ_[mth], 6 * h + mth] = 1.0
    d["c_spi"] = spi_
    d["c_spj"] = spj_
    g_ = np.zeros((96, 16), np.float32)
    gt_ = np.zeros((16, 96), np.float32)
    for h in range(16):
        g_[6 * h:6 * h + 6, h] = 1.0
        gt_[h, 6 * h:6 * h + 6] = 1.0
    d["c_g96"] = g_
    d["c_gt96"] = gt_
    J6 = np.array([[0, 0, 0, 0, 0, 1], [0, 0, 0, 0, -1, 0], [0, 0, 0, 1, 0, 0],
                   [0, 0, 1, 0, 0, 0], [0, -1, 0, 0, 0, 0], [1, 0, 0, 0, 0, 0]],
                  np.float32)
    bj_ = np.zeros((96, 96), np.float32)
    for h in range(16):
        bj_[6 * h:6 * h + 6, 6 * h:6 * h + 6] = J6.T
    d["c_bj"] = bj_
    d["c_ones1"] = np.ones((1, 128), np.float32)
    d["c_onesd"] = np.ones((128, 1), np.float32)
    jj32 = np.arange(32)
    d["c_u32"] = C((jj32[:, None] > jj32[None, :]).astype(np.float32))
    d["c_eps"] = np.array([[0.0, 1e-5]], np.float32)
    dd = np.arange(64)
    jj8 = np.arange(8)
    qi0 = np.zeros((64, 32), np.uint32)
    ki0 = np.zeros((64, 32), np.uint32)
    qi1 = np.zeros((70, 32), np.uint32)
    ki1 = np.zeros((70, 32), np.uint32)
    for j in range(8):
        h = 8 * r + j
        for s in range(2):
            qi0[:, 2 * j + s] = s * 2048 + 64 * h + dd
            ki0[:, 2 * j + s] = s * 2048 + 1024 + 64 * h + dd
            qi1[0:64, 2 * j + s] = s * 2240 + 64 * h + dd
            qi1[64:70, 2 * j + s] = s * 2240 + 2048 + 6 * h + np.arange(6)
            ki1[0:64, 2 * j + s] = s * 2240 + 1024 + 64 * h + dd
            ki1[64:70, 2 * j + s] = s * 2240 + 2144 + 6 * h + np.arange(6)
    d["c_qi0"] = qi0
    d["c_ki0"] = ki0
    d["c_qi1"] = qi1
    d["c_ki1"] = ki1
    vi = np.zeros((128, 16), np.uint32)
    pp = np.arange(128)
    for j in range(8):
        h = 8 * r + j
        for s in range(2):
            vi[:, j * 2 + s] = (s * 16 + h) * 128 + pp
    d["c_vi"] = vi
    sel = np.zeros((16, 16, 128), np.float32)
    for e_ in range(16):
        sel[e_, e_, :] = 1.0
    d["c_sel16"] = C(sel.transpose(1, 0, 2).reshape(16, 2048))
    d["_pi"] = pi
    return d


def kernel(**inputs):
    nc = build_nc()
    in_maps = []
    pis = []
    for core in range(8):
        d = _prep_inputs(inputs, core)
        pis.append(d.pop("_pi"))
        in_maps.append(d)
    res = _bu.run_bass_kernel_spmd(nc, in_maps, core_ids=list(range(8)))
    y = np.empty((4, T, D), np.float32)
    for core in range(8):
        b, r = core // 2, core % 2
        o = np.asarray(res.results[core]["out"]).reshape(1024, TH)
        yo = np.empty_like(o)
        yo[pis[core], :] = o
        y[b, r * TH:(r + 1) * TH, :] = yo.T
    return y


# version 04 (final)
# speedup vs baseline: 1.3909x; 1.1224x over previous
import numpy as np
import concourse.bass as bass
import concourse.mybir as mybir
from concourse import bass_utils as _bu

F32 = mybir.dt.float32
AF = mybir.ActivationFunctionType
OP = mybir.AluOpType
AX = mybir.AxisListType

D = 1024
T = 4096
TH = 2048
NEG = -1.0e30
FMIN = -3.0e38
GROUPS = [[0, 1], [2, 3], [4, 5], [6, 7]]

STAGE = 4


def build_nc(stage=None):
    if stage is None:
        stage = STAGE
    nc = bass.Bass("TRN2", target_bir_lowering=False)

    def din(name, shape):
        return nc.dram_tensor(name, shape, F32, kind="ExternalInput").ap()

    x_in = din("x", [8, 128, TH])
    l0_wqk = din("l0_wqk", [128, 128])
    l0_wv = din("l0_wv", [128, 64])
    l0_bqk = din("l0_bqk", [128, 1])
    l0_sqk = din("l0_sqk", [128, 1])
    l0_bv = din("l0_bv", [128, 1024])
    l0_wr = din("l0_wr", [4, 128, D])
    l0_w1 = din("l0_w1", [32, 128, D])
    l0_b1 = din("l0_b1", [128, 32])
    l0_w2 = din("l0_w2", [8, 128, 4096])
    l1_wqkv = din("l1_wqkv", [8, 128, 3072])
    l1_bv = din("l1_bv", [128, 1024])
    l1_w1p = din("l1_w1p", [8, 128, 64])
    l1_w2p = din("l1_w2p", [8, 128, 64])
    l1_ps96 = din("l1_ps96", [96, 1])
    l1_wout = din("l1_wout", [4, 128, D])
    m_gatew = din("m_gatew", [8, 128, 16])
    m_gateb = din("m_gateb", [128, 16])
    m_we1 = din("m_we1", [16, 8, 128, 512])
    m_be1 = din("m_be1", [16, 128, 4])
    m_we3 = din("m_we3", [16, 8, 128, 512])
    m_be3 = din("m_be3", [16, 128, 4])
    m_we2 = din("m_we2", [16, 4, 128, D])
    m_be2 = din("m_be2", [16, D])
    m_ws1 = din("m_ws1", [8, 128, 1024])
    m_ws3 = din("m_ws3", [8, 128, 1024])
    m_ws2 = din("m_ws2", [8, 128, D])
    b_pack = din("b_pack", [128, 64])
    c_ident = din("c_ident", [128, 128])
    c_tril = din("c_tril", [128, 128])
    c_triuneg = din("c_triuneg", [128, 128])
    c_spi = din("c_spi", [64, 96])
    c_spj = din("c_spj", [64, 96])
    c_g96 = din("c_g96", [96, 16])
    c_gt96 = din("c_gt96", [16, 96])
    c_bj = din("c_bj", [96, 96])
    c_ones1 = din("c_ones1", [1, 128])
    c_onesd = din("c_onesd", [128, 1])
    c_u32 = din("c_u32", [32, 32])
    c_eps = din("c_eps", [1, 2])
    U32 = mybir.dt.uint32
    c_qi0 = nc.dram_tensor("c_qi0", [64, 32], U32, kind="ExternalInput").ap()
    c_ki0 = nc.dram_tensor("c_ki0", [64, 32], U32, kind="ExternalInput").ap()
    c_qi1 = nc.dram_tensor("c_qi1", [70, 32], U32, kind="ExternalInput").ap()
    c_ki1 = nc.dram_tensor("c_ki1", [70, 32], U32, kind="ExternalInput").ap()
    c_vi = nc.dram_tensor("c_vi", [128, 16], U32, kind="ExternalInput").ap()
    c_sel16 = din("c_sel16", [16, 2048])

    out = nc.dram_tensor("out", [8, 128, TH], F32, kind="ExternalOutput").ap()

    aq0 = nc.dram_tensor("aq0", [2048, TH], F32).ap()
    aq0g = nc.dram_tensor("aq0g", [2, 2048, TH], F32).ap()
    aq1 = nc.dram_tensor("aq1", [2240, TH], F32).ap()
    aq1g = nc.dram_tensor("aq1g", [2, 2240, TH], F32).ap()
    av = nc.dram_tensor("av", [16, 128, 16, 64], F32).ap()
    avg = nc.dram_tensor("avg", [2, 16, 128, 16, 64], F32).ap()
    om = nc.dram_tensor("om", [512, T], F32).ap()
    rs_s = nc.dram_tensor("rs_s", [2, 8, 128, TH], F32).ap()
    rs_d = nc.dram_tensor("rs_d", [8, 128, TH], F32).ap()
    bw1t = nc.dram_tensor("bw1t", [32, 128, 1024], F32).ap()
    bw2t = nc.dram_tensor("bw2t", [32, 128, 1024], F32).ap()

    ctx = []

    def S(name, shape):
        cm = nc.sbuf_tensor(name, shape, F32)
        t = cm.__enter__()
        ctx.append(cm)
        return t

    x_sb = S("x_sb", [128, 8 * TH])
    rega = S("rega", [128, 8 * TH])
    regb = S("regb", [128, 12 * 1024])
    ident = S("ident", [128, 128])
    tril = S("tril", [128, 128])
    triun = S("triun", [128, 128])
    spi = S("spi", [64, 96])
    spj = S("spj", [64, 96])
    g96 = S("g96", [96, 16])
    gt96 = S("gt96", [16, 96])
    bjc = S("bjc", [96, 96])
    ones1 = S("ones1", [1, 128])
    onesd = S("onesd", [128, 1])
    u32 = S("u32", [32, 32])
    ps96 = S("ps96", [96, 1])
    wqk0 = S("wqk0", [128, 128])
    wv0 = S("wv0", [128, 64])
    bqk0 = S("bqk0", [128, 1])
    sqk0 = S("sqk0", [128, 1])
    bqk0s = S("bqk0s", [128, 1])
    gateb = S("gateb", [128, 16])
    bpk = S("bpk", [128, 64])
    b1sb = S("b1sb", [128, 32])
    be1sb = S("be1sb", [128, 4])
    be3sb = S("be3sb", [128, 4])
    stat = S("stat", [128, 24])
    srow = S("srow", [1, 2048])
    gcf = S("gcf", [16, TH])
    ceps = S("ceps", [1, 2])
    _u32ctx = []
    def SU(name, shape):
        cm = nc.sbuf_tensor(name, shape, mybir.dt.uint32)
        t = cm.__enter__()
        ctx.append(cm)
        return t
    qi0s = SU("qi0s", [64, 32])
    ki0s = SU("ki0s", [64, 32])
    qi1s = SU("qi1s", [70, 32])
    ki1s = SU("ki1s", [70, 32])
    vis = SU("vis", [128, 16])
    sel16 = S("sel16", [16, 2048])
    glt = regb[:, 8192:9216]
    gw_sb = regb[:, 9216:9344]
    bsum = regb[0:64, 8448:8480]
    bsumT = regb[0:32, 8480:8544]
    vsufT = regb[0:32, 8544:8608]

    pbcm = [nc.psum_tensor(f"pb{i}", [128, 512], F32) for i in range(8)]
    pbt = [p.__enter__() for p in pbcm]
    ctx.extend(pbcm)

    sems = {}
    for sname in ["ld", "st", "tt", "vv", "ss", "gg", "oo", "sq0", "sq1",
                  "sv0", "sv1", "sv2", "sv3", "lv0", "lv1", "lw2", "lw3",
                  "sto", "lg"]:
        cm = nc.semaphore(sname)
        sems[sname] = [cm.__enter__(), 0]
        ctx.append(cm)

    def inc(obj, s, k=1):
        obj.then_inc(sems[s][0], k)
        sems[s][1] += k
        return sems[s][1]

    def cnt(s):
        return sems[s][1]

    blkcm = nc.Block()
    block = blkcm.__enter__()

    ENGS = ["sync", "tensor", "vector", "scalar", "gpsimd"]

    def barrier(engs=ENGS, on=("ld", "st", "tt", "vv", "ss", "gg", "sq0",
                               "sq1", "sv0", "sv1", "sv2", "sv3", "lv0",
                               "lv1", "lw2", "lw3", "sto", "lg")):
        snap = [(s, sems[s][1]) for s in on if sems[s][1] > 0]
        for e in engs:
            dec = getattr(block, e)

            @dec
            def _(eng):
                for s, v in snap:
                    eng.wait_ge(sems[s][0], v)

    def bcol(i):
        return bpk[:, i:i + 1]

    def x_ap(c, t0, w=512):
        return x_sb[:, c * TH + t0: c * TH + t0 + w]

    def xln_ap(c, t0, w=512):
        return rega[:, c * TH + t0: c * TH + t0 + w]

    @block.sync
    def _(sync):
        for c in range(8):
            inc(sync.dma_start(out=x_sb[:, c * TH:(c + 1) * TH], in_=x_in[c]), "ld", 16)
        for dst, src in [(ident, c_ident), (tril, c_tril), (triun, c_triuneg),
                         (spi, c_spi), (spj, c_spj), (g96, c_g96),
                         (gt96, c_gt96), (bjc, c_bj), (ones1, c_ones1),
                         (onesd, c_onesd), (u32, c_u32), (ps96, l1_ps96),
                         (wqk0, l0_wqk), (wv0, l0_wv), (bqk0, l0_bqk),
                         (sqk0, l0_sqk), (gateb, m_gateb), (bpk, b_pack),
                         (b1sb, l0_b1), (ceps, c_eps), (sel16, c_sel16),
                         (qi0s, c_qi0), (ki0s, c_ki0), (qi1s, c_qi1),
                         (ki1s, c_ki1), (vis, c_vi)]:
            inc(sync.dma_start(out=dst[:], in_=src), "ld", 16)
    LD0 = cnt("ld")

    @block.vector
    def _(vector):
        vector.wait_ge(sems["ld"][0], LD0)
        vector.tensor_tensor(out=bqk0s[:], in0=bqk0[:], in1=sqk0[:], op=OP.mult)
        inc(vector.tensor_scalar(out=bpk[:, 32:40], in0=bpk[:, 32:40],
                                 scalar1=0.125, scalar2=None, op0=OP.mult), "vv")
    barrier()

    def emit_ln(srcf, dstf, resf, tokw, sq_off, tmp_off):
        tprev = cnt("tt")

        @block.vector
        def _(vector):
            vector.wait_ge(sems["tt"][0], tprev)
            vector.drain()
            for c in range(8):
                s = srcf(c)
                inc(vector.tensor_tensor(
                    out=regb[:, sq_off + c * tokw: sq_off + (c + 1) * tokw],
                    in0=s, in1=s, op=OP.mult), "vv")
        vsq = cnt("vv")

        @block.tensor
        def _(tensor):
            tensor.wait_ge(sems["vv"][0], vsq)
            for c in range(8):
                inc(tensor.matmul(pbt[0][0:1, 0:tokw], onesd[:], srcf(c),
                                  start=(c == 0), stop=(c == 7)), "tt")
            for c in range(8):
                inc(tensor.matmul(
                    pbt[1][0:1, 0:tokw], onesd[:],
                    regb[:, sq_off + c * tokw: sq_off + (c + 1) * tokw],
                    start=(c == 0), stop=(c == 7)), "tt")
        tstat = cnt("tt")

        @block.vector
        def _(vector):
            vector.wait_ge(sems["tt"][0], tstat)
            vector.tensor_scalar(out=srow[0:1, 0:tokw], in0=pbt[0][0:1, 0:tokw],
                                 scalar1=1.0 / 1024.0, scalar2=None, op0=OP.mult)
            vector.drain()
            vector.tensor_tensor(out=srow[0:1, 512:512 + tokw], in0=srow[0:1, 0:tokw],
                                 in1=srow[0:1, 0:tokw], op=OP.mult)
            vector.drain()
            inc(vector.scalar_tensor_tensor(
                out=srow[0:1, 512:512 + tokw], in0=pbt[1][0:1, 0:tokw],
                scalar=1.0 / 1024.0, in1=srow[0:1, 512:512 + tokw],
                op0=OP.mult, op1=OP.subtract), "vv")
        vstat = cnt("vv")

        @block.scalar
        def _(scalar):
            scalar.wait_ge(sems["vv"][0], vstat)
            inc(scalar.activation(out=srow[0:1, 1024:1024 + tokw],
                                  in_=srow[0:1, 512:512 + tokw],
                                  func=AF.Sqrt, bias=ceps[0:1, 1:2]), "ss")
        sstat = cnt("ss")

        @block.vector
        def _(vector):
            vector.wait_ge(sems["ss"][0], sstat)
            inc(vector.reciprocal(out=srow[0:1, 1536:1536 + tokw],
                                  in_=srow[0:1, 1024:1024 + tokw]), "vv")
        vrec = cnt("vv")

        @block.tensor
        def _(tensor):
            tensor.wait_ge(sems["vv"][0], vrec)
            inc(tensor.matmul(pbt[2][:, 0:tokw], ones1[:], srow[0:1, 0:tokw],
                              start=True, stop=True), "tt")
            inc(tensor.matmul(pbt[3][:, 0:tokw], ones1[:], srow[0:1, 1536:1536 + tokw],
                              start=True, stop=True), "tt")
        tbc = cnt("tt")

        @block.vector
        def _(vector):
            vector.wait_ge(sems["tt"][0], tbc)
            for c in range(8):
                tmp = regb[:, tmp_off: tmp_off + tokw]
                if c > 0:
                    vector.drain()
                vector.tensor_tensor(out=tmp, in0=srcf(c), in1=pbt[2][:, 0:tokw],
                                     op=OP.subtract)
                vector.drain()
                d = dstf(c) if dstf is not None else None
                if d is not None:
                    mm = vector.tensor_tensor(out=d, in0=tmp, in1=pbt[3][:, 0:tokw],
                                              op=OP.mult)
                    lnres = d
                else:
                    mm = vector.tensor_tensor(out=tmp, in0=tmp, in1=pbt[3][:, 0:tokw],
                                              op=OP.mult)
                    lnres = tmp
                if resf is not None:
                    vector.drain()
                    r = resf(c)
                    mm = vector.tensor_tensor(out=r, in0=r, in1=lnres, op=OP.add)
                inc(mm, "vv")
        vapply = cnt("vv")

        @block.tensor
        def _(tensor):
            tensor.wait_ge(sems["vv"][0], vapply)

    def emit_preln(t0_list, tokw=512):
        for t0 in t0_list:
            emit_ln(lambda c, t0=t0: x_ap(c, t0, tokw),
                    lambda c, t0=t0: xln_ap(c, t0, tokw),
                    None, tokw, 0, 8 * tokw)

    def emit_qkv_l0():
        evac_ms = {}
        dma_ms = {}
        for idx in range(64):
            j, n = idx // 4, idx % 4
            base = 64 * (j % 2)
            ch = j // 2
            t0 = n * 512

            @block.tensor
            def _(tensor, idx=idx, j=j, n=n, base=base, ch=ch, t0=t0):
                if idx >= 2:
                    tensor.wait_ge(sems["ss"][0], evac_ms[idx - 2])
                inc(tensor.matmul(pbt[idx % 2][:, :],
                                  wqk0[base:base + 64, 0:128],
                                  xln_ap(ch, t0)[base:base + 64, :],
                                  start=True, stop=True), "tt")
            tqk = cnt("tt")

            @block.scalar
            def _(scalar, idx=idx, j=j, t0=t0, tqk=tqk):
                scalar.wait_ge(sems["tt"][0], tqk)
                sq = "sq%d" % (idx % 2)
                if idx >= 2:
                    scalar.wait_ge(sems[sq][0], dma_ms[idx - 2])
                stg = regb[:, 8192 + (idx % 2) * 512: 8192 + (idx % 2) * 512 + 512]
                ev = scalar.activation(out=stg, in_=pbt[idx % 2][:, :],
                                       func=AF.Identity, bias=bqk0s[:],
                                       scale=sqk0[:])
                evac_ms[idx] = inc(ev, "ss")
                scalar.drain()
                inc(scalar.dma_start(out=aq0[64 * j:64 * j + 64, t0:t0 + 512],
                                     in_=stg[0:64, :]), sq, 16)
                dma_ms[idx] = inc(scalar.dma_start(
                    out=aq0[1024 + 64 * j:1024 + 64 * j + 64, t0:t0 + 512],
                    in_=stg[64:128, :]), sq, 16)
        emit_v_common(wv0, l0_bv)
        barrier()

    def emit_v_common(wv_tile, bv_src):
        @block.sync
        def _(sync):
            inc(sync.dma_start(out=regb[:, 11264:12288], in_=bv_src), "ld", 16)
        lbv = cnt("ld")
        bv_tile = regb[:, 11264:12288]
        for tg in range(4):
            vprev = cnt("vv")
            sprev = cnt("ss")

            @block.tensor
            def _(tensor, tg=tg, vprev=vprev, sprev=sprev):
                tensor.wait_ge(sems["vv"][0], vprev)
                if tg == 0:
                    tensor.wait_ge(sems["ss"][0], sprev)
                for j in range(16):
                    base = 64 * (j % 2)
                    ch = j // 2
                    for tt in range(4):
                        a = tg * 4 + tt
                        mm = tensor.matmul(
                            pbt[tt * 2 + (j // 8)][:, (j % 8) * 64:(j % 8) * 64 + 64],
                            xln_ap(ch, a * 128, 128)[base:base + 64, :],
                            wv_tile[base:base + 64, :],
                            start=True, stop=True)
                inc(mm, "tt")
            tv = cnt("tt")
            for tt in range(4):
                a = tg * 4 + tt

                @block.vector
                def _(vector, tt=tt, tv=tv, tg=tg):
                    vector.wait_ge(sems["tt"][0], tv)
                    vector.wait_ge(sems["ld"][0], lbv)
                    if tg > 0:
                        vector.wait_ge(sems["sv%d" % tt][0],
                                       emit_v_common.dma_ms.get((tg - 1, tt), 0))
                    stg = regb[:, tt * 1024: tt * 1024 + 1024]
                    inc(vector.tensor_tensor(out=stg[:, 0:512], in0=pbt[tt * 2][:, :],
                                             in1=bv_tile[:, 0:512], op=OP.add), "vv")
                    inc(vector.tensor_tensor(out=stg[:, 512:1024],
                                             in0=pbt[tt * 2 + 1][:, :],
                                             in1=bv_tile[:, 512:1024], op=OP.add), "vv")
                vva = cnt("vv")

                @block.sync
                def _(sync, tt=tt, a=a, vva=vva):
                    sync.wait_ge(sems["vv"][0], vva)
                    for h in range(16):
                        dm = sync.dma_start(
                            out=av[h, :, a, :],
                            in_=regb[:, tt * 1024 + 64 * h: tt * 1024 + 64 * h + 64])
                        inc(dm, "sv%d" % tt, 16)
                    emit_v_common.dma_ms[(tg, tt)] = cnt("sv%d" % tt)
    emit_v_common.dma_ms = {}

    def emit_qkv_l1():
        for mg in range(4):
            tprev = cnt("tt")

            @block.sync
            def _(sync, mg=mg, tprev=tprev):
                sync.wait_ge(sems["tt"][0], tprev)
                for c in range(8):
                    inc(sync.dma_start(
                        out=regb[:, (mg % 2) * 4096 + c * 512:
                                 (mg % 2) * 4096 + c * 512 + 512],
                        in_=l1_wqkv[c][:, mg * 512: mg * 512 + 512]), "ld", 16)
            lw = cnt("ld")
            for n in range(4):
                t0 = n * 512
                sprev = cnt("ss")

                @block.tensor
                def _(tensor, mg=mg, n=n, t0=t0, lw=lw, sprev=sprev):
                    if n == 0:
                        tensor.wait_ge(sems["ld"][0], lw)
                    if mg * 4 + n >= 1:
                        tensor.wait_ge(sems["ss"][0], emit_qkv_l1.ev.get(mg * 4 + n - 1, 0))
                    for mt in range(4):
                        for c in range(8):
                            mm = tensor.matmul(
                                pbt[4 + mt][:, :],
                                regb[:, (mg % 2) * 4096 + c * 512 + mt * 128:
                                     (mg % 2) * 4096 + c * 512 + mt * 128 + 128],
                                xln_ap(c, t0), start=(c == 0), stop=(c == 7))
                    inc(mm, "tt")
                tqk = cnt("tt")

                @block.scalar
                def _(scalar, mg=mg, n=n, t0=t0, tqk=tqk):
                    scalar.wait_ge(sems["tt"][0], tqk)
                    idx = mg * 4 + n
                    sq = "sq%d" % (idx % 2)
                    if idx >= 2:
                        scalar.wait_ge(sems[sq][0], emit_qkv_l1.dm.get(idx - 2, 0))
                    for mt in range(4):
                        gm = mg * 4 + mt
                        stg = regb[:, 8192 + ((idx % 2) * 4 + mt) * 512:
                                   8192 + ((idx % 2) * 4 + mt) * 512 + 512]
                        sc = 0.125 if gm < 8 else 1.0
                        inc(scalar.activation(out=stg, in_=pbt[4 + mt][:, :],
                                              func=AF.Identity, bias=bcol(32 + gm),
                                              scale=sc), "ss")
                    scalar.drain()
                    for mt in range(4):
                        gm = mg * 4 + mt
                        stg = regb[:, 8192 + ((idx % 2) * 4 + mt) * 512:
                                   8192 + ((idx % 2) * 4 + mt) * 512 + 512]
                        dm = scalar.dma_start(
                            out=aq1[128 * gm:128 * gm + 128, t0:t0 + 512], in_=stg)
                        inc(dm, sq, 16)
                    emit_qkv_l1.ev[idx] = cnt("ss")
                    emit_qkv_l1.dm[idx] = cnt(sq)
        barrier()
        @block.sync
        def _(sync):
            inc(sync.dma_start(out=regb[:, 11264:12288], in_=l1_bv), "ld", 16)
        lbv1 = cnt("ld")
        bv1t = regb[:, 11264:12288]
        for tg in range(4):
            for c in range(8):
                tprev = cnt("tt")

                @block.sync
                def _(sync, tg=tg, c=c, tprev=tprev):
                    if tg * 8 + c >= 2:
                        sync.wait_ge(sems["tt"][0], emit_qkv_l1.vt.get(tg * 8 + c - 2, 0))
                    inc(sync.dma_start(out=regb[:, 9216 + (c % 2) * 1024:
                                                9216 + (c % 2) * 1024 + 1024],
                                       in_=l1_wqkv[c][:, 2048:3072]), "lv%d" % (c % 2), 16)
                lv = cnt("lv%d" % (c % 2))

                @block.tensor
                def _(tensor, tg=tg, c=c, lv=lv):
                    tensor.wait_ge(sems["lv%d" % (c % 2)][0], lv)
                    if c == 0:
                        tensor.wait_ge(sems["vv"][0], cnt("vv"))
                    for tt in range(4):
                        a = tg * 4 + tt
                        for hh in range(2):
                            mm = tensor.matmul(
                                pbt[tt * 2 + hh][:, :],
                                xln_ap(c, a * 128, 128),
                                regb[:, 9216 + (c % 2) * 1024 + hh * 512:
                                     9216 + (c % 2) * 1024 + hh * 512 + 512],
                                start=(c == 0), stop=(c == 7))
                    emit_qkv_l1.vt[tg * 8 + c] = inc(mm, "tt")
            tv = cnt("tt")
            for tt in range(4):
                a = tg * 4 + tt

                @block.vector
                def _(vector, tt=tt, tv=tv, tg=tg):
                    vector.wait_ge(sems["tt"][0], tv)
                    vector.wait_ge(sems["ld"][0], lbv1)
                    if tg > 0:
                        vector.wait_ge(sems["sv%d" % tt][0],
                                       emit_qkv_l1.vdm.get((tg - 1, tt), 0))
                    stg = regb[:, tt * 1024: tt * 1024 + 1024]
                    inc(vector.tensor_tensor(out=stg[:, 0:512], in0=pbt[tt * 2][:, :],
                                             in1=bv1t[:, 0:512], op=OP.add), "vv")
                    inc(vector.tensor_tensor(out=stg[:, 512:1024],
                                             in0=pbt[tt * 2 + 1][:, :],
                                             in1=bv1t[:, 512:1024], op=OP.add), "vv")
                vva = cnt("vv")

                @block.sync
                def _(sync, tt=tt, a=a, vva=vva, tg=tg):
                    sync.wait_ge(sems["vv"][0], vva)
                    for h in range(16):
                        dm = sync.dma_start(
                            out=av[h, :, a, :],
                            in_=regb[:, tt * 1024 + 64 * h: tt * 1024 + 64 * h + 64])
                        inc(dm, "sv%d" % tt, 16)
                    emit_qkv_l1.vdm[(tg, tt)] = cnt("sv%d" % tt)
        barrier()
        emit_plucker()
        barrier()
    emit_qkv_l1.ev = {}
    emit_qkv_l1.dm = {}
    emit_qkv_l1.vt = {}
    emit_qkv_l1.vdm = {}

    def emit_plucker():
        w1p_sb = regb[:, 4096:4608]
        w2p_sb = regb[:, 4608:5120]

        @block.sync
        def _(sync):
            for c in range(8):
                inc(sync.dma_start(out=w1p_sb[:, c * 64:(c + 1) * 64], in_=l1_w1p[c]),
                    "ld", 16)
                inc(sync.dma_start(out=w2p_sb[:, c * 64:(c + 1) * 64], in_=l1_w2p[c]),
                    "ld", 16)
        lwp = cnt("ld")
        pev = {}
        for n in range(4):
            t0 = n * 512

            @block.tensor
            def _(tensor, n=n, t0=t0):
                if n == 0:
                    tensor.wait_ge(sems["ld"][0], lwp)
                else:
                    tensor.wait_ge(sems["ss"][0], pev[n - 1])
                for c in range(8):
                    tensor.matmul(pbt[0][0:64, :], w1p_sb[:, c * 64:(c + 1) * 64],
                                  xln_ap(c, t0), start=(c == 0), stop=(c == 7))
                    mm = tensor.matmul(pbt[1][0:64, :], w2p_sb[:, c * 64:(c + 1) * 64],
                                       xln_ap(c, t0), start=(c == 0), stop=(c == 7))
                inc(mm, "tt")
            tp = cnt("tt")

            @block.scalar
            def _(scalar, n=n, t0=t0, tp=tp):
                scalar.wait_ge(sems["tt"][0], tp)
                inc(scalar.activation(out=regb[0:64, t0:t0 + 512], in_=pbt[0][0:64, :],
                                      func=AF.Copy), "ss")
                inc(scalar.activation(out=regb[0:64, 2048 + t0:2048 + t0 + 512],
                                      in_=pbt[1][0:64, :], func=AF.Copy), "ss")
                pev[n] = cnt("ss")
        sp = cnt("ss")

        shev = {}
        for n in range(4):
            t0 = n * 512

            @block.tensor
            def _(tensor, n=n, t0=t0, sp=sp):
                tensor.wait_ge(sems["ss"][0], sp if n == 0 else shev[n - 1])
                tensor.matmul(pbt[2][0:96, :], spi[:], regb[0:64, t0:t0 + 512],
                              start=True, stop=True)
                tensor.matmul(pbt[3][0:96, :], spj[:],
                              regb[0:64, 2048 + t0:2048 + t0 + 512],
                              start=True, stop=True)
                tensor.matmul(pbt[4][0:96, :], spj[:], regb[0:64, t0:t0 + 512],
                              start=True, stop=True)
                inc(tensor.matmul(pbt[5][0:96, :], spi[:],
                                  regb[0:64, 2048 + t0:2048 + t0 + 512],
                                  start=True, stop=True), "tt")
            ts = cnt("tt")

            @block.scalar
            def _(scalar, n=n, t0=t0, ts=ts):
                scalar.wait_ge(sems["tt"][0], ts)
                inc(scalar.activation(out=regb[0:96, 4096 + t0:4096 + t0 + 512],
                                      in_=pbt[2][0:96, :], func=AF.Copy), "ss")
                inc(scalar.activation(out=regb[0:96, 6144 + t0:6144 + t0 + 512],
                                      in_=pbt[3][0:96, :], func=AF.Copy), "ss")
                inc(scalar.activation(out=regb[0:96, 8192 + t0:8192 + t0 + 512],
                                      in_=pbt[4][0:96, :], func=AF.Copy), "ss")
                inc(scalar.activation(out=regb[0:96, 10240 + t0:10240 + t0 + 512],
                                      in_=pbt[5][0:96, :], func=AF.Copy), "ss")
                shev[n] = cnt("ss")
        ssh = cnt("ss")

        @block.vector
        def _(vector):
            vector.wait_ge(sems["ss"][0], ssh)
            A1 = regb[0:96, 4096:6144]
            B2 = regb[0:96, 6144:8192]
            A2 = regb[0:96, 8192:10240]
            B1 = regb[0:96, 10240:12288]
            inc(vector.tensor_tensor(out=A1, in0=A1, in1=B2, op=OP.mult), "vv")
            inc(vector.tensor_tensor(out=A2, in0=A2, in1=B1, op=OP.mult), "vv")
            vector.drain()
            inc(vector.tensor_tensor(out=A1, in0=A1, in1=A2, op=OP.subtract), "vv")
            vector.drain()
            inc(vector.tensor_tensor(out=B2, in0=A1, in1=A1, op=OP.mult), "vv")
        vl = cnt("vv")

        nev = {}
        for n in range(4):
            t0 = n * 512

            @block.tensor
            def _(tensor, n=n, t0=t0, vl=vl):
                tensor.wait_ge(sems["vv"][0], vl)
                if n > 0:
                    tensor.wait_ge(sems["ss"][0], nev[n - 1])
                inc(tensor.matmul(pbt[0][0:16, :], g96[:],
                                  regb[0:96, 6144 + t0:6144 + t0 + 512],
                                  start=True, stop=True), "tt")
            tn = cnt("tt")

            @block.scalar
            def _(scalar, n=n, t0=t0, tn=tn):
                scalar.wait_ge(sems["tt"][0], tn)
                inc(scalar.activation(out=regb[0:16, 8192 + t0:8192 + t0 + 512],
                                      in_=pbt[0][0:16, :], func=AF.Sqrt), "ss")
                nev[n] = cnt("ss")
        sn = cnt("ss")

        @block.vector
        def _(vector):
            vector.wait_ge(sems["ss"][0], sn)
            nr = regb[0:16, 8192:10240]
            vector.tensor_scalar(out=nr, in0=nr, scalar1=1e-12, scalar2=None,
                                 op0=OP.max)
            vector.drain()
            inc(vector.reciprocal(out=regb[0:16, 10240:12288], in_=nr), "vv")
        vrn = cnt("vv")

        rnv = {}
        jlv = {}
        for n in range(4):
            t0 = n * 512

            @block.tensor
            def _(tensor, n=n, t0=t0, vrn=vrn):
                tensor.wait_ge(sems["vv"][0], vrn if n == 0 else rnv[n - 1])
                inc(tensor.matmul(pbt[1][0:96, :], gt96[:],
                                  regb[0:16, 10240 + t0:10240 + t0 + 512],
                                  start=True, stop=True), "tt")
            tb = cnt("tt")

            @block.vector
            def _(vector, n=n, t0=t0, tb=tb):
                vector.wait_ge(sems["tt"][0], tb)
                inc(vector.tensor_tensor(out=regb[0:96, 4096 + t0:4096 + t0 + 512],
                                         in0=regb[0:96, 4096 + t0:4096 + t0 + 512],
                                         in1=pbt[1][0:96, :], op=OP.mult), "vv")
            vno = cnt("vv")
            rnv[n] = vno

            @block.tensor
            def _(tensor, n=n, t0=t0, vno=vno):
                tensor.wait_ge(sems["vv"][0], vno)
                if n > 0:
                    tensor.wait_ge(sems["ss"][0], jlv[n - 1])
                inc(tensor.matmul(pbt[2][0:96, :], bjc[:],
                                  regb[0:96, 4096 + t0:4096 + t0 + 512],
                                  start=True, stop=True), "tt")
            tj = cnt("tt")

            @block.scalar
            def _(scalar, n=n, t0=t0, tj=tj):
                scalar.wait_ge(sems["tt"][0], tj)
                inc(scalar.activation(out=regb[0:96, 6144 + t0:6144 + t0 + 512],
                                      in_=pbt[2][0:96, :], func=AF.Copy), "ss")
                jlv[n] = cnt("ss")

            @block.vector
            def _(vector, n=n, t0=t0):
                vector.drain()
                inc(vector.tensor_scalar(out=regb[0:96, 8192 + t0:8192 + t0 + 512],
                                         in0=regb[0:96, 4096 + t0:4096 + t0 + 512],
                                         scalar1=ps96[:], scalar2=None,
                                         op0=OP.mult), "vv")
        sJ = cnt("ss")
        vL = cnt("vv")

        @block.sync
        def _(sync):
            sync.wait_ge(sems["ss"][0], sJ)
            sync.wait_ge(sems["vv"][0], vL)
            inc(sync.dma_start(out=aq1[2144:2240, :], in_=regb[0:96, 6144:8192]),
                "st", 16)
            inc(sync.dma_start(out=aq1[2048:2144, :], in_=regb[0:96, 8192:10240]),
                "st", 16)

    def emit_gather(layer):
        aq_src = aq0 if layer == 0 else aq1
        aq_dst = aq0g if layer == 0 else aq1g

        @block.gpsimd
        def _(gpsimd):
            for s in ("st", "sq0", "sq1", "sv0", "sv1", "sv2", "sv3"):
                if cnt(s):
                    gpsimd.wait_ge(sems[s][0], cnt(s))
            inc(gpsimd.collective_compute(
                "AllGather", OP.bypass, replica_groups=GROUPS,
                ins=[aq_src.opt()], outs=[aq_dst.opt()]), "gg")
            inc(gpsimd.collective_compute(
                "AllGather", OP.bypass, replica_groups=GROUPS,
                ins=[av.opt()], outs=[avg.opt()]), "gg")
        barrier()

    def emit_attention(layer):
        aqg = aq0g if layer == 0 else aq1g
        KR = 64 if layer == 0 else 70
        qis = qi0s if layer == 0 else qi1s
        kis = ki0s if layer == 0 else ki1s
        aqg_flat = aqg.rearrange("s r c -> (s r) c")
        avg_flat = avg.rearrange("s h p a c -> (s h p) (a c)")
        rzb = regb[0:64, 8608:8736]
        for j in range(8):
            tprev = cnt("tt")
            sprev = cnt("ss")
            vprev = cnt("vv")

            @block.gpsimd
            def _(gpsimd, j=j, tprev=tprev, sprev=sprev, vprev=vprev,
                  qis=qis, kis=kis, aqg_flat=aqg_flat, avg_flat=avg_flat, KR=KR):
                gpsimd.wait_ge(sems["tt"][0], tprev)
                gpsimd.wait_ge(sems["ss"][0], sprev)
                gpsimd.wait_ge(sems["vv"][0], vprev)
                for s in range(2):
                    inc(gpsimd.indirect_dma_start(
                        out=rega[0:KR, s * 2048:(s + 1) * 2048], out_offset=None,
                        in_=aqg_flat,
                        in_offset=bass.IndirectOffsetOnAxis(
                            ap=qis[:, 2 * j + s:2 * j + s + 1], axis=0)), "lg", 16)
                    inc(gpsimd.indirect_dma_start(
                        out=rega[0:KR, 4096 + s * 2048:4096 + (s + 1) * 2048],
                        out_offset=None, in_=aqg_flat,
                        in_offset=bass.IndirectOffsetOnAxis(
                            ap=kis[:, 2 * j + s:2 * j + s + 1], axis=0)), "lg", 16)
                    inc(gpsimd.indirect_dma_start(
                        out=rega[:, 8192 + s * 1024:8192 + s * 1024 + 1024],
                        out_offset=None, in_=avg_flat,
                        in_offset=bass.IndirectOffsetOnAxis(
                            ap=vis[:, 2 * j + s:2 * j + s + 1], axis=0)), "lg", 16)
            lh = cnt("lg")

            if layer == 0:

                @block.tensor
                def _(tensor, lh=lh):
                    tensor.wait_ge(sems["lg"][0], lh)
                    for g in range(32):
                        mm = tensor.matmul(pbt[6][0:64, g:g + 1],
                                           rega[:, 8192 + g * 64:8192 + g * 64 + 64],
                                           onesd[:, 0:1], start=True, stop=True)
                    inc(mm, "tt")
                tbs = cnt("tt")

                @block.scalar
                def _(scalar, tbs=tbs):
                    scalar.wait_ge(sems["tt"][0], tbs)
                    inc(scalar.activation(out=bsum[:], in_=pbt[6][0:64, 0:32],
                                          func=AF.Copy), "ss")
                sbs = cnt("ss")

                @block.tensor
                def _(tensor, sbs=sbs):
                    tensor.wait_ge(sems["ss"][0], sbs)
                    inc(tensor.transpose(pbt[5][0:32, 0:64], bsum[:],
                                         ident[0:64, 0:64]), "tt")
                tbt = cnt("tt")

                @block.scalar
                def _(scalar, tbt=tbt):
                    scalar.wait_ge(sems["tt"][0], tbt)
                    inc(scalar.activation(out=bsumT[:], in_=pbt[5][0:32, 0:64],
                                          func=AF.Copy), "ss")
                sbt = cnt("ss")

                @block.tensor
                def _(tensor, sbt=sbt):
                    tensor.wait_ge(sems["ss"][0], sbt)
                    inc(tensor.matmul(pbt[7][0:32, 0:64], u32[:], bsumT[:],
                                      start=True, stop=True), "tt")
                tvs = cnt("tt")

                @block.scalar
                def _(scalar, tvs=tvs):
                    scalar.wait_ge(sems["tt"][0], tvs)
                    inc(scalar.activation(out=vsufT[:], in_=pbt[7][0:32, 0:64],
                                          func=AF.Copy), "ss")
                svs = cnt("ss")

                @block.sync
                def _(sync, svs=svs):
                    sync.wait_ge(sems["ss"][0], svs)
                    inc(sync.dma_start(out=rega[0:1, 14336:16384], in_=vsufT[:]),
                        "ld", 16)
                vsuf_ld = cnt("ld")
            else:
                vsuf_ld = 0

            copy_ms = {}
            exp_ms = {}
            rz_ms = {}
            tav_ms = {}
            oev_ms = {}

            def emit_qk_softmax(i, j=j, lh=lh):
                K_i = 128 * (i + 1)
                ncH = (K_i + 511) // 512
                nsuf = T - K_i
                buf = regb[:, (i % 2) * 4096:(i % 2) * 4096 + 4096]
                for kc in range(ncH):
                    w = min(512, K_i - kc * 512)
                    gidx = emit_attention.qk_idx
                    emit_attention.qk_idx += 1
                    bank = gidx % 2

                    @block.tensor
                    def _(tensor, i=i, kc=kc, w=w, gidx=gidx, bank=bank):
                        if i == 0 and kc == 0:
                            tensor.wait_ge(sems["lg"][0], lh)
                        prev = emit_attention.qk_hist.get(gidx - 2)
                        if prev is not None:
                            tensor.wait_ge(sems["vv"][0], prev)
                        inc(tensor.matmul(pbt[bank][:, 0:w],
                                          rega[0:KR, 128 * i:128 * i + 128],
                                          rega[0:KR, 4096 + kc * 512:4096 + kc * 512 + w],
                                          start=True, stop=True), "tt")
                    tq = cnt("tt")

                    @block.vector
                    def _(vector, i=i, kc=kc, w=w, tq=tq, bank=bank, gidx=gidx, buf=buf):
                        vector.wait_ge(sems["tt"][0], tq)
                        if kc == 0 and i >= 2:
                            vector.wait_ge(sems["tt"][0], tav_ms.get(i - 2, 0))
                        mm = vector.tensor_scalar(
                            out=buf[:, kc * 512:kc * 512 + w],
                            in0=pbt[bank][:, 0:w],
                            scalar1=0.0, scalar2=FMIN, op0=OP.add, op1=OP.max,
                            accum_out=stat[:, kc:kc + 1])
                        emit_attention.qk_hist[gidx] = inc(mm, "vv")

                @block.vector
                def _(vector, i=i, ncH=ncH, buf=buf):
                    vector.drain()
                    if layer == 0:
                        vector.tensor_tensor(out=buf[:, 128 * i:128 * i + 128],
                                             in0=buf[:, 128 * i:128 * i + 128],
                                             in1=tril[:], op=OP.mult)
                    else:
                        vector.tensor_tensor(out=buf[:, 128 * i:128 * i + 128],
                                             in0=buf[:, 128 * i:128 * i + 128],
                                             in1=triun[:], op=OP.add)
                    vector.reduce_max(out=stat[:, 8:9], in_=stat[:, 0:ncH],
                                      axis=AX.X)
                    vector.drain()
                    pq = i % 2
                    if layer == 0:
                        inc(vector.tensor_scalar(out=stat[:, 9 + pq:10 + pq],
                                                 in0=stat[:, 8:9],
                                                 scalar1=0.0, scalar2=-1.0,
                                                 op0=OP.max, op1=OP.mult), "vv")
                    else:
                        inc(vector.tensor_scalar(out=stat[:, 9 + pq:10 + pq],
                                                 in0=stat[:, 8:9],
                                                 scalar1=-1.0, scalar2=None, op0=OP.mult),
                            "vv")
                vmask = cnt("vv")

                @block.scalar
                def _(scalar, i=i, K_i=K_i, vmask=vmask, buf=buf):
                    scalar.wait_ge(sems["vv"][0], vmask)
                    pq = i % 2
                    ex = scalar.activation(out=buf[:, 0:K_i], in_=buf[:, 0:K_i],
                                           func=AF.Exp, bias=stat[:, 9 + pq:10 + pq],
                                           accum_out=stat[:, 15 + pq:16 + pq])
                    if layer == 0:
                        scalar.drain()
                        ex = scalar.activation(out=stat[:, 11 + pq:12 + pq],
                                               in_=stat[:, 9 + pq:10 + pq],
                                               func=AF.Exp)
                    exp_ms[i] = inc(ex, "ss")

                @block.vector
                def _(vector, i=i, nsuf=nsuf):
                    vector.wait_ge(sems["ss"][0], exp_ms[i])
                    pq = i % 2
                    if layer == 0 and nsuf > 0:
                        vector.scalar_tensor_tensor(out=stat[:, 15 + pq:16 + pq],
                                                    in0=stat[:, 11 + pq:12 + pq],
                                                    scalar=float(nsuf),
                                                    in1=stat[:, 15 + pq:16 + pq],
                                                    op0=OP.mult, op1=OP.add)
                        vector.drain()
                    inc(vector.reciprocal(out=stat[:, 13 + pq:14 + pq],
                                          in_=stat[:, 15 + pq:16 + pq]), "vv")
                rz_ms[i] = cnt("vv")

            def emit_tav(i, j=j, vsuf_ld=vsuf_ld):
                K_i = 128 * (i + 1)
                nsuf = T - K_i
                buf = regb[:, (i % 2) * 4096:(i % 2) * 4096 + 4096]
                if layer == 0 and nsuf > 0:

                    @block.tensor
                    def _(tensor, i=i):
                        tensor.wait_ge(sems["ss"][0], exp_ms[i])
                        prev = emit_attention.emn_ms.get(i - 1)
                        if prev:
                            tensor.wait_ge(sems["ss"][0], prev)
                        inc(tensor.transpose(pbt[5][0:1, 256:384],
                                             stat[:, 11 + i % 2:12 + i % 2],
                                             ident[:]), "tt")
                    tem = cnt("tt")

                    @block.scalar
                    def _(scalar, i=i, tem=tem):
                        scalar.wait_ge(sems["tt"][0], tem)
                        scalar.wait_ge(sems["tt"][0], tav_ms.get(i - 1, 0))
                        inc(scalar.activation(out=srow[0:1, 0:128],
                                              in_=pbt[5][0:1, 256:384],
                                              func=AF.Copy), "ss")
                    emit_attention.emn_ms[i] = cnt("ss")

                def emit_T(g, buf=buf, i=i):
                    ptg = emit_attention.pt_idx
                    emit_attention.pt_idx += 1

                    @block.tensor
                    def _(tensor, i=i, g=g, ptg=ptg, buf=buf):
                        if g == 0:
                            tensor.wait_ge(sems["ss"][0], exp_ms[i])
                        prev = emit_attention.pt_hist.get(ptg - 2)
                        if prev is not None:
                            tensor.wait_ge(sems["vv"][0], prev)
                        inc(tensor.transpose(pbt[2 + ptg % 2][:, 0:128],
                                             buf[:, 128 * g:128 * g + 128],
                                             ident[:]), "tt")
                    tpt = cnt("tt")

                    @block.vector
                    def _(vector, i=i, g=g, tpt=tpt, ptg=ptg):
                        vector.wait_ge(sems["tt"][0], tpt)
                        emit_attention.pt_hist[ptg] = inc(vector.tensor_copy(
                            out=regb[:, 8192 + (ptg % 2) * 128:8192 + (ptg % 2) * 128 + 128],
                            in_=pbt[2 + ptg % 2][:, 0:128]), "vv")
                    return ptg

                def emit_AV(g, ptg, i=i, nsuf=nsuf):
                    @block.tensor
                    def _(tensor, i=i, g=g, ptg=ptg):
                        tensor.wait_ge(sems["vv"][0], emit_attention.pt_hist[ptg])
                        if g == 0 and i >= 1:
                            tensor.wait_ge(sems["vv"][0], oev_ms.get(i - 1, 0))
                        last = (g == i) and not (layer == 0 and nsuf > 0)
                        mm = tensor.matmul(pbt[4][0:64, 0:128],
                                           rega[:, 8192 + g * 64:8192 + g * 64 + 64],
                                           regb[:, 8192 + (ptg % 2) * 128:8192 + (ptg % 2) * 128 + 128],
                                           start=(g == 0), stop=last)
                        if last:
                            inc(mm, "tt")

                ptgs = {}
                for g in range(i + 1):
                    ptgs[g] = emit_T(g)
                    if g >= 1:
                        emit_AV(g - 1, ptgs[g - 1])
                emit_AV(i, ptgs[i])

                if layer == 0 and nsuf > 0:

                    @block.tensor
                    def _(tensor, i=i, vsuf_ld=vsuf_ld):
                        tensor.wait_ge(sems["ss"][0], emit_attention.emn_ms[i])
                        tensor.wait_ge(sems["ld"][0], vsuf_ld)
                        inc(tensor.matmul(pbt[4][0:64, 0:128],
                                          rega[0:1, 14336 + 64 * i:14336 + 64 * i + 64],
                                          srow[0:1, 0:128],
                                          start=False, stop=True), "tt")

                @block.tensor
                def _(tensor, i=i):
                    tensor.wait_ge(sems["vv"][0], rz_ms[i])
                    tensor.wait_ge(sems["ss"][0], emit_attention.rzs_ms.get(i - 1, 0))
                    inc(tensor.transpose(pbt[5][0:1, 384:512],
                                         stat[:, 13 + i % 2:14 + i % 2],
                                         ident[:]), "tt")
                trz = cnt("tt")

                @block.scalar
                def _(scalar, i=i, trz=trz):
                    scalar.wait_ge(sems["tt"][0], trz)
                    scalar.wait_ge(sems["tt"][0], tav_ms.get(i - 1, 0))
                    inc(scalar.activation(out=srow[0:1, 128:256],
                                          in_=pbt[5][0:1, 384:512],
                                          func=AF.Copy), "ss")
                emit_attention.rzs_ms[i] = cnt("ss")
                srz = cnt("ss")

                @block.tensor
                def _(tensor, i=i, srz=srz):
                    tensor.wait_ge(sems["ss"][0], srz)
                    inc(tensor.matmul(pbt[5][0:64, 128:256], ones1[0:1, 0:64],
                                      srow[0:1, 128:256], start=True, stop=True),
                        "tt")
                tbc = cnt("tt")

                @block.scalar
                def _(scalar, i=i, tbc=tbc):
                    scalar.wait_ge(sems["tt"][0], tbc)
                    scalar.wait_ge(sems["vv"][0], oev_ms.get(i - 1, 0))
                    inc(scalar.activation(out=rzb[:], in_=pbt[5][0:64, 128:256],
                                          func=AF.Copy), "ss")
                srzb = cnt("ss")
                tav_ms[i] = cnt("tt")

                @block.vector
                def _(vector, i=i, srzb=srzb):
                    vector.wait_ge(sems["tt"][0], tav_ms[i])
                    vector.wait_ge(sems["ss"][0], srzb)
                    if i == 0:
                        vector.wait_ge(sems["sto"][0], cnt("sto"))
                    oev_ms[i] = inc(vector.tensor_tensor(
                        out=rega[0:64, 10240 + 128 * i:10240 + 128 * i + 128],
                        in0=pbt[4][0:64, 0:128], in1=rzb[:], op=OP.mult), "vv")

            for i in range(32):
                emit_qk_softmax(i)
                emit_tav(i)

            @block.sync
            def _(sync, j=j):
                sync.wait_ge(sems["vv"][0], oev_ms[31])
                inc(sync.dma_start(out=om[64 * j:64 * j + 64, :],
                                   in_=rega[0:64, 10240:14336]), "sto", 16)
        barrier()
    emit_attention.qk_hist = {}
    emit_attention.qk_idx = 0
    emit_attention.pt_hist = {}
    emit_attention.pt_idx = 0
    emit_attention.emn_ms = {}
    emit_attention.rzs_ms = {}

    def emit_oproj(layer):
        wsrc = l0_wr if layer == 0 else l1_wout

        @block.sync
        def _(sync):
            for c in range(4):
                inc(sync.dma_start(out=regb[:, 1024 + c * D:1024 + (c + 1) * D],
                                   in_=wsrc[c]), "ld", 16)
        lwr = cnt("ld")
        for tk in range(8):
            for c in range(4):
                idx = tk * 4 + c

                @block.sync
                def _(sync, tk=tk, c=c, idx=idx):
                    if idx >= 2:
                        sync.wait_ge(sems["tt"][0], emit_oproj.tms.get(idx - 2, 0))
                    inc(sync.dma_start(
                        out=regb[:, (idx % 2) * 512:(idx % 2) * 512 + 512],
                        in_=om[128 * c:128 * c + 128, 512 * tk:512 * tk + 512]),
                        "lv%d" % (idx % 2), 16)
                lom = cnt("lv%d" % (idx % 2))

                @block.tensor
                def _(tensor, tk=tk, c=c, idx=idx, lom=lom, lwr=lwr):
                    tensor.wait_ge(sems["lv%d" % (idx % 2)][0], lom)
                    tensor.wait_ge(sems["ld"][0], lwr)
                    if c == 0:
                        tensor.wait_ge(sems["ss"][0], emit_oproj.sms.get(tk - 1, 0))
                    for m in range(8):
                        mm = tensor.matmul(
                            pbt[m][:, :],
                            regb[:, 1024 + c * D + 128 * m:1024 + c * D + 128 * m + 128],
                            regb[:, (idx % 2) * 512:(idx % 2) * 512 + 512],
                            start=(c == 0), stop=(c == 3))
                    emit_oproj.tms[idx] = inc(mm, "tt")
            tpr = cnt("tt")

            @block.scalar
            def _(scalar, tk=tk, tpr=tpr):
                scalar.wait_ge(sems["tt"][0], tpr)
                for m in range(8):
                    su = emit_oproj.slot_use
                    emit_oproj.slot_use += 1
                    prev = emit_oproj.slot_dm.get(su - 4)
                    if prev is not None:
                        scalar.wait_ge(sems["sv%d" % (su % 4)][0], prev)
                    stg = regb[:, 5120 + (su % 4) * 512:5120 + (su % 4) * 512 + 512]
                    inc(scalar.activation(out=stg, in_=pbt[m][:, :], func=AF.Copy), "ss")
                    scalar.drain()
                    dm = scalar.dma_start(
                        out=rs_s[tk // 4, m, :, (tk % 4) * 512:(tk % 4) * 512 + 512],
                        in_=stg)
                    emit_oproj.slot_dm[su] = inc(dm, "sv%d" % (su % 4), 16)
                emit_oproj.sms[tk] = cnt("ss")

        @block.gpsimd
        def _(gpsimd):
            for s in ("st", "sv0", "sv1", "sv2", "sv3"):
                if cnt(s):
                    gpsimd.wait_ge(sems[s][0], cnt(s))
            inc(gpsimd.collective_compute(
                "ReduceScatter", OP.add, replica_groups=GROUPS,
                ins=[rs_s.opt()], outs=[rs_d.opt()]), "gg")
        barrier()
    emit_oproj.tms = {}
    emit_oproj.sms = {}
    emit_oproj.slot_use = 0
    emit_oproj.slot_dm = {}

    def emit_postln_attn(layer):
        bb = 0 if layer == 0 else 16
        for tc in range(4):
            t0 = tc * 512
            vprev = cnt("vv")

            @block.sync
            def _(sync, tc=tc, t0=t0, vprev=vprev):
                sync.wait_ge(sems["vv"][0], vprev)
                for c in range(8):
                    inc(sync.dma_start(out=regb[:, c * 512:(c + 1) * 512],
                                       in_=rs_d[c][:, t0:t0 + 512]), "ld", 16)
            lrs = cnt("ld")

            @block.scalar
            def _(scalar, tc=tc, lrs=lrs, bb=bb):
                scalar.wait_ge(sems["ld"][0], lrs)
                for c in range(8):
                    inc(scalar.activation(out=regb[:, 4096 + c * 512:4096 + (c + 1) * 512],
                                          in_=regb[:, c * 512:(c + 1) * 512],
                                          func=AF.Identity, bias=bcol(bb + c)), "ss")
            sbi = cnt("ss")

            @block.vector
            def _(vector, sbi=sbi):
                vector.wait_ge(sems["ss"][0], sbi)

            emit_ln(lambda c: regb[:, 4096 + c * 512:4096 + (c + 1) * 512],
                    None,
                    lambda c, t0=t0: x_ap(c, t0),
                    512, 8192, 8192 + 4096 - 512)

    def emit_binarize():
        for H in range(32):
            tprev = cnt("tt")

            @block.sync
            def _(sync, H=H, tprev=tprev):
                sync.wait_ge(sems["tt"][0], tprev)
                inc(sync.dma_start(out=regb[:, (H % 2) * 1024:(H % 2) * 1024 + 1024],
                                   in_=l0_w1[H]), "lv%d" % (H % 2), 16)
            lw = cnt("lv%d" % (H % 2))

            @block.scalar
            def _(scalar, H=H, lw=lw):
                scalar.wait_ge(sems["lv%d" % (H % 2)][0], lw)
                inc(scalar.activation(out=regb[:, 2048:3072],
                                      in_=regb[:, (H % 2) * 1024:(H % 2) * 1024 + 1024],
                                      func=AF.Abs), "ss")
                inc(scalar.activation(out=regb[:, 3072:4096],
                                      in_=regb[:, (H % 2) * 1024:(H % 2) * 1024 + 1024],
                                      func=AF.Sign), "ss")
            sab = cnt("ss")

            @block.vector
            def _(vector, H=H, sab=sab):
                vector.wait_ge(sems["ss"][0], sab)
                vector.reduce_sum(out=stat[:, 8:9], in_=regb[:, 2048:3072], axis=AX.X)
                vector.drain()
                inc(vector.tensor_scalar(out=regb[:, 4096:5120], in0=regb[:, 3072:4096],
                                         scalar1=stat[:, 8:9], scalar2=1.0 / 1024.0,
                                         op0=OP.mult, op1=OP.mult), "vv")
            vbw = cnt("vv")

            for c in range(8):

                @block.tensor
                def _(tensor, H=H, c=c, vbw=vbw):
                    if c == 0:
                        tensor.wait_ge(sems["vv"][0], vbw)
                    prevev = emit_binarize.ev.get(c % 2)
                    if prevev:
                        tensor.wait_ge(sems["ss"][0], prevev)
                    inc(tensor.transpose(pbt[c % 2][:, 0:128],
                                         regb[:, 4096 + 128 * c:4096 + 128 * c + 128],
                                         ident[:]), "tt")
                ttr = cnt("tt")

                @block.scalar
                def _(scalar, H=H, c=c, ttr=ttr):
                    scalar.wait_ge(sems["tt"][0], ttr)
                    prevdm = emit_binarize.dm.get(c % 4)
                    if prevdm:
                        scalar.wait_ge(sems["sv%d" % (c % 4)][0], prevdm)
                    stg = regb[:, 5120 + (c % 4) * 128:5120 + (c % 4) * 128 + 128]
                    emit_binarize.ev[c % 2] = inc(
                        scalar.activation(out=stg, in_=pbt[c % 2][:, 0:128],
                                          func=AF.Copy), "ss")
                    scalar.drain()
                    emit_binarize.dm[c % 4] = inc(
                        scalar.dma_start(out=bw1t[H][:, 128 * c:128 * c + 128], in_=stg),
                        "sv%d" % (c % 4), 16)
        barrier()
        for m in range(8):
            tprev = cnt("tt")

            @block.sync
            def _(sync, m=m, tprev=tprev):
                sync.wait_ge(sems["tt"][0], tprev)
                inc(sync.dma_start(out=regb[:, 0:4096], in_=l0_w2[m]), "ld", 16)
            lw = cnt("ld")

            @block.scalar
            def _(scalar, m=m, lw=lw):
                scalar.wait_ge(sems["ld"][0], lw)
                inc(scalar.activation(out=regb[:, 4096:8192], in_=regb[:, 0:4096],
                                      func=AF.Abs), "ss")
                scalar.drain()
                inc(scalar.activation(out=regb[:, 0:4096], in_=regb[:, 0:4096],
                                      func=AF.Sign), "ss")
            sab = cnt("ss")

            @block.vector
            def _(vector, m=m, sab=sab):
                vector.wait_ge(sems["ss"][0], sab)
                vector.reduce_sum(out=stat[:, 8:9], in_=regb[:, 4096:8192], axis=AX.X)
                vector.drain()
                inc(vector.tensor_scalar(out=regb[:, 4096:8192], in0=regb[:, 0:4096],
                                         scalar1=stat[:, 8:9], scalar2=1.0 / 4096.0,
                                         op0=OP.mult, op1=OP.mult), "vv")
            vbw = cnt("vv")

            for c in range(32):

                @block.tensor
                def _(tensor, m=m, c=c, vbw=vbw):
                    if c == 0:
                        tensor.wait_ge(sems["vv"][0], vbw)
                    prevev = emit_binarize.ev2.get(c % 2)
                    if prevev:
                        tensor.wait_ge(sems["ss"][0], prevev)
                    inc(tensor.transpose(pbt[c % 2][:, 0:128],
                                         regb[:, 4096 + 128 * c:4096 + 128 * c + 128],
                                         ident[:]), "tt")
                ttr = cnt("tt")

                @block.scalar
                def _(scalar, m=m, c=c, ttr=ttr):
                    scalar.wait_ge(sems["tt"][0], ttr)
                    prevdm = emit_binarize.dm2.get(c % 4)
                    if prevdm:
                        scalar.wait_ge(sems["sv%d" % (c % 4)][0], prevdm)
                    stg = regb[:, 8192 + (c % 4) * 128:8192 + (c % 4) * 128 + 128]
                    emit_binarize.ev2[c % 2] = inc(
                        scalar.activation(out=stg, in_=pbt[c % 2][:, 0:128],
                                          func=AF.Copy), "ss")
                    scalar.drain()
                    emit_binarize.dm2[c % 4] = inc(
                        scalar.dma_start(out=bw2t[c][:, 128 * m:128 * m + 128], in_=stg),
                        "sv%d" % (c % 4), 16)
        barrier()
    emit_binarize.ev = {}
    emit_binarize.dm = {}
    emit_binarize.ev2 = {}
    emit_binarize.dm2 = {}

    def emit_ffn0():
        for tc in range(8):
            t0 = tc * 256
            for Ht in range(32):
                idx = tc * 32 + Ht
                tpv = cnt("tt")

                @block.sync
                def _(sync, Ht=Ht, idx=idx):
                    if idx >= 2:
                        sync.wait_ge(sems["tt"][0], emit_ffn0.t1.get(idx - 2, 0))
                    inc(sync.dma_start(out=regb[:, 512 + (idx % 2) * 1024:
                                                512 + (idx % 2) * 1024 + 1024],
                                       in_=bw1t[Ht]), "lv%d" % (idx % 2), 16)
                l1_ = cnt("lv%d" % (idx % 2))

                @block.tensor
                def _(tensor, Ht=Ht, t0=t0, idx=idx, l1_=l1_):
                    tensor.wait_ge(sems["lv%d" % (idx % 2)][0], l1_)
                    if idx >= 2:
                        tensor.wait_ge(sems["ss"][0], emit_ffn0.sg.get(idx - 2, 0))
                    for c in range(8):
                        mm = tensor.matmul(
                            pbt[idx % 2][:, 0:256],
                            regb[:, 512 + (idx % 2) * 1024 + 128 * c:
                                 512 + (idx % 2) * 1024 + 128 * c + 128],
                            xln_ap(c, t0, 256), start=(c == 0), stop=(c == 7))
                    emit_ffn0.t1[idx] = inc(mm, "tt")
                th1 = cnt("tt")

                @block.scalar
                def _(scalar, Ht=Ht, idx=idx, th1=th1):
                    scalar.wait_ge(sems["tt"][0], th1)
                    if idx >= 2:
                        scalar.wait_ge(sems["tt"][0], emit_ffn0.t2.get(idx - 2, 0))
                    emit_ffn0.sg[idx] = inc(scalar.activation(
                        out=regb[:, (idx % 2) * 256:(idx % 2) * 256 + 256],
                        in_=pbt[idx % 2][:, 0:256], func=AF.Gelu,
                        bias=b1sb[:, Ht:Ht + 1]), "ss")

                @block.sync
                def _(sync, Ht=Ht, idx=idx):
                    if idx >= 2:
                        sync.wait_ge(sems["tt"][0], emit_ffn0.t2.get(idx - 2, 0))
                    inc(sync.dma_start(out=regb[:, 2560 + (idx % 2) * 1024:
                                                2560 + (idx % 2) * 1024 + 1024],
                                       in_=bw2t[Ht]), "lw%d" % (2 + idx % 2), 16)
                l2_ = cnt("lw%d" % (2 + idx % 2))

                @block.tensor
                def _(tensor, Ht=Ht, idx=idx, l2_=l2_):
                    tensor.wait_ge(sems["lw%d" % (2 + idx % 2)][0], l2_)
                    tensor.wait_ge(sems["ss"][0], emit_ffn0.sg[idx])
                    if idx >= 1:
                        tensor.wait_ge(sems["vv"][0], emit_ffn0.ya.get(idx - 1, 0))
                    for m in range(8):
                        mm = tensor.matmul(
                            pbt[2 + m // 2][:, (m % 2) * 256:(m % 2) * 256 + 256],
                            regb[:, 2560 + (idx % 2) * 1024 + 128 * m:
                                 2560 + (idx % 2) * 1024 + 128 * m + 128],
                            regb[:, (idx % 2) * 256:(idx % 2) * 256 + 256],
                            start=True, stop=True, skip_group_check=True)
                    emit_ffn0.t2[idx] = inc(mm, "tt")
                th2 = cnt("tt")

                @block.vector
                def _(vector, Ht=Ht, idx=idx, th2=th2):
                    vector.wait_ge(sems["tt"][0], th2)
                    vector.drain()
                    for m in range(8):
                        src = pbt[2 + m // 2][:, (m % 2) * 256:(m % 2) * 256 + 256]
                        ydst = regb[:, 4608 + m * 256:4608 + m * 256 + 256]
                        if Ht == 0:
                            inc(vector.tensor_copy(out=ydst, in_=src), "vv")
                        else:
                            inc(vector.tensor_tensor(out=ydst, in0=ydst, in1=src,
                                                     op=OP.add), "vv")
                    emit_ffn0.ya[idx] = cnt("vv")
            yb = cnt("vv")

            @block.vector
            def _(vector, tc=tc, yb=yb):
                vector.drain()
                for c in range(8):
                    inc(vector.tensor_scalar(
                        out=regb[:, 4608 + c * 256:4608 + c * 256 + 256],
                        in0=regb[:, 4608 + c * 256:4608 + c * 256 + 256],
                        scalar1=bcol(8 + c), scalar2=None, op0=OP.add), "vv")

            emit_ln(lambda c: regb[:, 4608 + c * 256:4608 + c * 256 + 256],
                    None,
                    lambda c, t0=t0: x_ap(c, t0, 256),
                    256, 6656, 8704)
        barrier()
    emit_ffn0.t1 = {}
    emit_ffn0.t2 = {}
    emit_ffn0.sg = {}
    emit_ffn0.ya = {}

    def emit_routing():
        @block.sync
        def _(sync):
            for c in range(8):
                inc(sync.dma_start(out=gw_sb[:, c * 16:(c + 1) * 16], in_=m_gatew[c]),
                    "ld", 16)
        lgw = cnt("ld")
        glev = {}
        for n in range(4):
            t0 = n * 512

            @block.tensor
            def _(tensor, n=n, t0=t0, lgw=lgw):
                if n == 0:
                    tensor.wait_ge(sems["ld"][0], lgw)
                else:
                    tensor.wait_ge(sems["ss"][0], glev[n - 1])
                for c in range(8):
                    mm = tensor.matmul(pbt[0][0:16, :], gw_sb[:, c * 16:(c + 1) * 16],
                                       xln_ap(c, t0), start=(c == 0), stop=(c == 7))
                inc(mm, "tt")
            tg = cnt("tt")

            @block.scalar
            def _(scalar, n=n, t0=t0, tg=tg):
                scalar.wait_ge(sems["tt"][0], tg)
                inc(scalar.activation(out=gcf[:, t0:t0 + 512], in_=pbt[0][0:16, :],
                                      func=AF.Copy), "ss")
                glev[n] = cnt("ss")
        sg = cnt("ss")
        for tt in range(16):

            @block.tensor
            def _(tensor, tt=tt, sg=sg):
                if tt == 0:
                    tensor.wait_ge(sems["ss"][0], sg)
                if tt >= 2:
                    tensor.wait_ge(sems["ss"][0], emit_routing.ev.get(tt - 2, 0))
                inc(tensor.transpose(pbt[tt % 2][0:128, 0:16],
                                     gcf[:, 128 * tt:128 * tt + 128],
                                     ident[0:16, 0:16]), "tt")
            ttr = cnt("tt")

            @block.scalar
            def _(scalar, tt=tt, ttr=ttr):
                scalar.wait_ge(sems["tt"][0], ttr)
                emit_routing.ev[tt] = inc(scalar.activation(
                    out=glt[:, 16 * tt:16 * tt + 16], in_=pbt[tt % 2][0:128, 0:16],
                    func=AF.Copy), "ss")
        sev = cnt("ss")
        for tt in range(16):
            gl = glt[:, 16 * tt:16 * tt + 16]
            sreg = glt[:, 256 + 16 * tt:256 + 16 * tt + 16]
            msk = glt[:, 512 + 16 * tt:512 + 16 * tt + 16]
            cmb = glt[:, 768 + 16 * tt:768 + 16 * tt + 16]

            @block.vector
            def _(vector, tt=tt, gl=gl, sev=sev):
                if tt == 0:
                    vector.wait_ge(sems["ss"][0], sev)
                vector.reduce_max(out=stat[:, 8:9], in_=gl, axis=AX.X)
                vector.drain()
                inc(vector.tensor_scalar(out=stat[:, 9:10], in0=stat[:, 8:9],
                                         scalar1=-1.0, scalar2=None, op0=OP.mult), "vv")
            v1 = cnt("vv")

            @block.scalar
            def _(scalar, tt=tt, gl=gl, sreg=sreg, v1=v1):
                scalar.wait_ge(sems["vv"][0], v1)
                inc(scalar.activation(out=sreg, in_=gl, func=AF.Exp,
                                      bias=stat[:, 9:10], accum_out=stat[:, 10:11]),
                    "ss")
            s1 = cnt("ss")

            @block.vector
            def _(vector, tt=tt, sreg=sreg, msk=msk, cmb=cmb, s1=s1):
                vector.wait_ge(sems["ss"][0], s1)
                vector.reciprocal(out=stat[:, 11:12], in_=stat[:, 10:11])
                vector.drain()
                vector.tensor_scalar(out=sreg, in0=sreg, scalar1=stat[:, 11:12],
                                     scalar2=None, op0=OP.mult)
                vector.drain()
                vector.tensor_tensor(out=msk, in0=sreg, in1=gateb[:], op=OP.add)
                vector.drain()
                for g in range(4):
                    mg = msk[:, 4 * g:4 * g + 4]
                    tmp = cmb[:, 4 * g:4 * g + 4]
                    vector.reduce_max(out=stat[:, 12:13], in_=mg, axis=AX.X)
                    vector.drain()
                    vector.tensor_scalar(out=tmp, in0=mg, scalar1=stat[:, 12:13],
                                         scalar2=None, op0=OP.is_equal)
                    vector.drain()
                    vector.scalar_tensor_tensor(out=tmp, in0=tmp, scalar=NEG,
                                                in1=mg, op0=OP.mult, op1=OP.add)
                    vector.drain()
                    vector.reduce_max(out=stat[:, 13:14], in_=tmp, axis=AX.X)
                    vector.drain()
                    vector.tensor_tensor(out=stat[:, 16 + g:17 + g],
                                         in0=stat[:, 12:13], in1=stat[:, 13:14],
                                         op=OP.add)
                    vector.drain()
                gsc = stat[:, 16:20]
                vector.reduce_max(out=stat[:, 12:13], in_=gsc, axis=AX.X)
                vector.drain()
                vector.tensor_scalar(out=stat[:, 20:24], in0=gsc,
                                     scalar1=stat[:, 12:13], scalar2=None,
                                     op0=OP.is_equal)
                vector.drain()
                vector.scalar_tensor_tensor(out=stat[:, 20:24], in0=stat[:, 20:24],
                                            scalar=NEG, in1=gsc, op0=OP.mult,
                                            op1=OP.add)
                vector.drain()
                vector.reduce_max(out=stat[:, 13:14], in_=stat[:, 20:24], axis=AX.X)
                vector.drain()
                for g in range(4):
                    vector.tensor_scalar(out=stat[:, 20 + g:21 + g],
                                         in0=stat[:, 16 + g:17 + g],
                                         scalar1=stat[:, 13:14], scalar2=1.0,
                                         op0=OP.is_ge, op1=OP.subtract)
                    vector.drain()
                    vector.tensor_scalar(out=stat[:, 20 + g:21 + g],
                                         in0=stat[:, 20 + g:21 + g],
                                         scalar1=-NEG, scalar2=None, op0=OP.mult)
                    vector.drain()
                    vector.tensor_scalar(out=msk[:, 4 * g:4 * g + 4],
                                         in0=msk[:, 4 * g:4 * g + 4],
                                         scalar1=stat[:, 20 + g:21 + g],
                                         scalar2=None, op0=OP.add)
                    vector.drain()
                vector.reduce_max(out=stat[:, 12:13], in_=msk, axis=AX.X)
                vector.drain()
                vector.tensor_scalar(out=cmb, in0=msk, scalar1=stat[:, 12:13],
                                     scalar2=None, op0=OP.is_equal)
                vector.drain()
                vector.scalar_tensor_tensor(out=cmb, in0=cmb, scalar=NEG, in1=msk,
                                            op0=OP.mult, op1=OP.add)
                vector.drain()
                vector.reduce_max(out=stat[:, 13:14], in_=cmb, axis=AX.X)
                vector.drain()
                vector.tensor_scalar(out=cmb, in0=msk, scalar1=stat[:, 13:14],
                                     scalar2=None, op0=OP.is_ge)
                vector.drain()
                inc(vector.tensor_tensor(out=cmb, in0=cmb, in1=sreg, op=OP.mult),
                    "vv")
        vcm = cnt("vv")
        for tt in range(16):

            @block.tensor
            def _(tensor, tt=tt, vcm=vcm):
                if tt == 0:
                    tensor.wait_ge(sems["vv"][0], vcm)
                if tt >= 2:
                    tensor.wait_ge(sems["ss"][0], emit_routing.ev2.get(tt - 2, 0))
                inc(tensor.transpose(pbt[tt % 2][0:16, 0:128],
                                     glt[:, 768 + 16 * tt:768 + 16 * tt + 16],
                                     ident[:]), "tt")
            ttr = cnt("tt")

            @block.scalar
            def _(scalar, tt=tt, ttr=ttr):
                scalar.wait_ge(sems["tt"][0], ttr)
                emit_routing.ev2[tt] = inc(scalar.activation(
                    out=gcf[:, 128 * tt:128 * tt + 128], in_=pbt[tt % 2][0:16, 0:128],
                    func=AF.Copy), "ss")
        barrier()
    emit_routing.ev = {}
    emit_routing.ev2 = {}

    def emit_moe():
        for tc in range(4):
            t0 = tc * 512

            @block.sync
            def _(sync, tc=tc):
                sync.wait_ge(sems["tt"][0], cnt("tt"))
                sync.wait_ge(sems["vv"][0], cnt("vv"))
                inc(sync.dma_start(out=regb[0:16, 4096:5120], in_=m_be2), "ld", 16)
            lbe2 = cnt("ld")

            @block.tensor
            def _(tensor, tc=tc, t0=t0, lbe2=lbe2):
                tensor.wait_ge(sems["ld"][0], lbe2)
                tensor.wait_ge(sems["vv"][0], cnt("vv"))
                for m in range(8):
                    mm = tensor.matmul(pbt[m][:, :],
                                       regb[0:16, 4096 + 128 * m:4096 + 128 * m + 128],
                                       gcf[:, t0:t0 + 512],
                                       start=True, stop=True)
                inc(mm, "tt")
            tbe = cnt("tt")

            @block.scalar
            def _(scalar, tc=tc, tbe=tbe):
                scalar.wait_ge(sems["tt"][0], tbe)
                for m in range(8):
                    inc(scalar.activation(out=regb[:, 8192 + m * 512:8192 + m * 512 + 512],
                                          in_=pbt[m][:, :], func=AF.Copy), "ss")
            barrier(engs=["tensor", "scalar", "vector", "sync"], on=("tt", "ss"))

            def shared_mm(wsrc, rhsf, twait_ss=0):
                shm = {}
                tentry = cnt("tt")
                for c in range(8):

                    @block.sync
                    def _(sync, c=c, wsrc=wsrc, tentry=tentry):
                        if c < 2:
                            sync.wait_ge(sems["tt"][0], tentry)
                        else:
                            sync.wait_ge(sems["tt"][0], shm[c - 2])
                        inc(sync.dma_start(out=regb[:, 4096 + (c % 2) * 1024:
                                                    4096 + (c % 2) * 1024 + 1024],
                                           in_=wsrc[c]), "lv%d" % (c % 2), 16)
                    lw = cnt("lv%d" % (c % 2))

                    @block.tensor
                    def _(tensor, c=c, lw=lw, twait_ss=twait_ss, rhsf=rhsf):
                        tensor.wait_ge(sems["lv%d" % (c % 2)][0], lw)
                        if c == 0 and twait_ss:
                            tensor.wait_ge(sems["ss"][0], twait_ss)
                        for m in range(8):
                            mm = tensor.matmul(
                                pbt[m][:, :],
                                regb[:, 4096 + (c % 2) * 1024 + 128 * m:
                                     4096 + (c % 2) * 1024 + 128 * m + 128],
                                rhsf(c), start=(c == 0), stop=(c == 7))
                        shm[c] = inc(mm, "tt")
                return cnt("tt")

            tsh1 = shared_mm(m_ws1, lambda c: xln_ap(c, t0))

            @block.scalar
            def _(scalar, tsh1=tsh1):
                scalar.wait_ge(sems["tt"][0], tsh1)
                for m in range(8):
                    inc(scalar.activation(
                        out=regb[:, m * 512:m * 512 + 512],
                        in_=pbt[m][:, :], func=AF.Silu, bias=bcol(48 + m)), "ss")
            s1done = cnt("ss")

            tsh3 = shared_mm(m_ws3, lambda c: xln_ap(c, t0), twait_ss=s1done)

            @block.vector
            def _(vector, tsh3=tsh3, s1done=s1done):
                vector.wait_ge(sems["tt"][0], tsh3)
                vector.wait_ge(sems["ss"][0], s1done)
                for m in range(8):
                    inc(vector.scalar_tensor_tensor(
                        out=regb[:, m * 512:m * 512 + 512],
                        in0=pbt[m][:, :], scalar=bcol(56 + m),
                        in1=regb[:, m * 512:m * 512 + 512],
                        op0=OP.add, op1=OP.mult), "vv")
            vh3 = cnt("vv")

            @block.tensor
            def _(tensor, vh3=vh3):
                tensor.wait_ge(sems["vv"][0], vh3)
            tz = shared_mm(m_ws2, lambda c: regb[:, c * 512:c * 512 + 512])

            @block.vector
            def _(vector, tz=tz):
                vector.wait_ge(sems["tt"][0], tz)
                vector.drain()
                for m in range(8):
                    inc(vector.scalar_tensor_tensor(
                        out=regb[:, 8192 + m * 512:8192 + m * 512 + 512],
                        in0=pbt[m][:, :], scalar=bcol(24 + m),
                        in1=regb[:, 8192 + m * 512:8192 + m * 512 + 512],
                        op0=OP.add, op1=OP.add), "vv")
            barrier(engs=["tensor", "scalar", "vector", "sync"], on=("tt", "vv", "ss"))

            for e in range(16):

                @block.sync
                def _(sync, e=e, tc=tc):
                    sync.wait_ge(sems["tt"][0], cnt("tt"))
                    for c in range(8):
                        inc(sync.dma_start(out=regb[:, 4096 + c * 512:4096 + c * 512 + 512],
                                           in_=m_we1[e, c]), "ld", 16)
                    inc(sync.dma_start(out=be1sb[:], in_=m_be1[e]), "ld", 16)
                    inc(sync.dma_start(out=be3sb[:], in_=m_be3[e]), "ld", 16)
                lw1 = cnt("ld")

                @block.tensor
                def _(tensor, e=e, t0=t0, lw1=lw1):
                    tensor.wait_ge(sems["ld"][0], lw1)
                    tensor.wait_ge(sems["vv"][0], cnt("vv"))
                    tensor.matmul(pbt[6][:, :], sel16[:, 128 * e:128 * e + 128],
                                  gcf[0:16, t0:t0 + 512], start=True, stop=True)
                    for mt in range(4):
                        for c in range(8):
                            mm = tensor.matmul(
                                pbt[mt][:, :],
                                regb[:, 4096 + c * 512 + mt * 128:
                                     4096 + c * 512 + mt * 128 + 128],
                                xln_ap(c, t0), start=(c == 0), stop=(c == 7))
                    inc(mm, "tt")
                t1 = cnt("tt")

                @block.scalar
                def _(scalar, e=e, t1=t1):
                    scalar.wait_ge(sems["tt"][0], t1)
                    for mt in range(4):
                        inc(scalar.activation(out=regb[:, mt * 512:mt * 512 + 512],
                                              in_=pbt[mt][:, :], func=AF.Silu,
                                              bias=be1sb[:, mt:mt + 1]), "ss")
                s1_ = cnt("ss")

                @block.sync
                def _(sync, e=e, s1_=s1_):
                    sync.wait_ge(sems["tt"][0], cnt("tt"))
                    for c in range(8):
                        inc(sync.dma_start(out=regb[:, 4096 + c * 512:4096 + c * 512 + 512],
                                           in_=m_we3[e, c]), "ld", 16)
                lw3 = cnt("ld")

                @block.tensor
                def _(tensor, e=e, t0=t0, lw3=lw3, s1_=s1_):
                    tensor.wait_ge(sems["ld"][0], lw3)
                    tensor.wait_ge(sems["ss"][0], s1_)
                    for mt in range(4):
                        for c in range(8):
                            mm = tensor.matmul(
                                pbt[mt][:, :],
                                regb[:, 4096 + c * 512 + mt * 128:
                                     4096 + c * 512 + mt * 128 + 128],
                                xln_ap(c, t0), start=(c == 0), stop=(c == 7))
                    inc(mm, "tt")
                t3 = cnt("tt")

                @block.vector
                def _(vector, e=e, t3=t3):
                    vector.wait_ge(sems["tt"][0], t3)
                    for mt in range(4):
                        inc(vector.scalar_tensor_tensor(
                            out=regb[:, 2048 + mt * 512:2048 + mt * 512 + 512],
                            in0=pbt[mt][:, :], scalar=be3sb[:, mt:mt + 1],
                            in1=regb[:, mt * 512:mt * 512 + 512],
                            op0=OP.add, op1=OP.mult), "vv")
                    vector.drain()
                    for mt in range(4):
                        inc(vector.tensor_tensor(
                            out=regb[:, mt * 512:mt * 512 + 512],
                            in0=regb[:, 2048 + mt * 512:2048 + mt * 512 + 512],
                            in1=pbt[6][:, :], op=OP.mult), "vv")
                vht = cnt("vv")

                @block.sync
                def _(sync, e=e, vht=vht):
                    sync.wait_ge(sems["tt"][0], cnt("tt"))
                    for hc in range(4):
                        inc(sync.dma_start(out=regb[:, 4096 + hc * 1024:4096 + hc * 1024 + 1024],
                                           in_=m_we2[e, hc]), "ld", 16)
                lw2 = cnt("ld")

                @block.tensor
                def _(tensor, e=e, lw2=lw2, vht=vht):
                    tensor.wait_ge(sems["ld"][0], lw2)
                    tensor.wait_ge(sems["vv"][0], vht)
                    for hc in range(4):
                        for m in range(8):
                            mm = tensor.matmul(
                                pbt[m][:, :],
                                regb[:, 4096 + hc * 1024 + 128 * m:
                                     4096 + hc * 1024 + 128 * m + 128],
                                regb[:, hc * 512:hc * 512 + 512],
                                start=(hc == 0), stop=(hc == 3))
                    inc(mm, "tt")
                t2 = cnt("tt")

                @block.vector
                def _(vector, e=e, t2=t2):
                    vector.wait_ge(sems["tt"][0], t2)
                    vector.drain()
                    for m in range(8):
                        inc(vector.tensor_tensor(
                            out=regb[:, 8192 + m * 512:8192 + m * 512 + 512],
                            in0=regb[:, 8192 + m * 512:8192 + m * 512 + 512],
                            in1=pbt[m][:, :], op=OP.add), "vv")

                @block.tensor
                def _(tensor, e=e):
                    tensor.wait_ge(sems["vv"][0], cnt("vv"))

            emit_ln(lambda c: regb[:, 8192 + c * 512:8192 + c * 512 + 512],
                    None,
                    lambda c, t0=t0: x_ap(c, t0),
                    512, 0, 4096)
        barrier()

    emit_preln([0, 512, 1024, 1536])
    barrier()
    emit_qkv_l0()
    emit_gather(0)
    emit_attention(0)
    emit_oproj(0)
    emit_postln_attn(0)
    if stage >= 2:
        emit_preln([0, 512, 1024, 1536])
        barrier()
        emit_binarize()
        emit_ffn0()
    if stage >= 3:
        emit_preln([0, 512, 1024, 1536])
        barrier()
        emit_qkv_l1()
        emit_gather(1)
        emit_attention(1)
        emit_oproj(1)
        emit_postln_attn(1)
    if stage >= 4:
        emit_preln([0, 512, 1024, 1536])
        barrier()
        emit_routing()
        emit_moe()
    barrier()

    @block.sync
    def _(sync):
        for c in range(8):
            inc(sync.dma_start(out=out[c], in_=x_sb[:, c * TH:(c + 1) * TH]), "oo", 16)
    OO = cnt("oo")

    @block.gpsimd
    def _(gpsimd):
        gpsimd.wait_ge(sems["oo"][0], OO)

    blkcm.__exit__(None, None, None)
    for cm in reversed(ctx):
        cm.__exit__(None, None, None)
    return nc


def _prep_inputs(inputs, core):
    b, r = core // 2, core % 2
    slot = list(range(16))
    pi = np.arange(1024)
    myrows = np.concatenate([np.arange(64 * h, 64 * h + 64)
                             for h in range(8 * r, 8 * r + 8)])
    C = np.ascontiguousarray

    d = {}
    x = np.asarray(inputs["L0_attn_x"][b, r * TH:(r + 1) * TH, :])
    d["x"] = C(x.T[pi].reshape(8, 128, TH))

    w0 = np.asarray(inputs["L0_attn_Wqkv"])
    bq0 = np.asarray(inputs["L0_attn_bqkv"])
    d["l0_wqk"] = C(np.concatenate([w0[:, 0:128], w0[:, 0:128]], axis=0))
    d["l0_wv"] = C(np.concatenate([w0[:, 128:192], w0[:, 128:192]], axis=0))
    d["l0_bqk"] = C(bq0[0:128].reshape(128, 1))
    d["l0_sqk"] = C(np.array([0.125] * 64 + [1.0] * 64, np.float32).reshape(128, 1))
    d["l0_bv"] = C(np.broadcast_to(np.tile(bq0[128:192], 16), (128, 1024)))
    wr = np.asarray(inputs["L0_attn_Wr"])
    d["l0_wr"] = C(wr[myrows][:, pi].reshape(4, 128, D))
    w1 = np.asarray(inputs["L0_ffn_w1"])
    d["l0_w1"] = C(w1[:, pi].reshape(32, 128, D))
    d["l0_b1"] = C(np.asarray(inputs["L0_ffn_b1"]).reshape(32, 128).T)
    w2 = np.asarray(inputs["L0_ffn_w2"])
    d["l0_w2"] = C(w2[pi, :].reshape(8, 128, 4096))

    wq1 = np.asarray(inputs["L1_attn_Wqkv"])
    colp = np.concatenate([np.concatenate([np.arange(64 * h, 64 * h + 64) + 1024 * s
                                           for h in slot]) for s in range(3)])
    d["l1_wqkv"] = C(wq1[pi][:, colp].reshape(8, 128, 3072))
    bq1 = np.asarray(inputs["L1_attn_bqkv"])[colp]
    d["l1_bv"] = C(np.broadcast_to(bq1[2048:3072], (128, 1024)))
    w1p = np.asarray(inputs["L1_attn_W1p"])
    w2p = np.asarray(inputs["L1_attn_W2p"])
    colp4 = np.concatenate([np.arange(4 * h, 4 * h + 4) for h in slot])
    d["l1_w1p"] = C(w1p[pi][:, colp4].reshape(8, 128, 64))
    d["l1_w2p"] = C(w2p[pi][:, colp4].reshape(8, 128, 64))
    d["l1_ps96"] = C(np.repeat(np.asarray(inputs["L1_attn_plucker_scale"])[slot], 6)
                     .reshape(96, 1))
    wout = np.asarray(inputs["L1_attn_Wout"])
    d["l1_wout"] = C(wout[myrows][:, pi].reshape(4, 128, D))

    d["m_gatew"] = C(np.asarray(inputs["L1_ffn_gate_w"]).T[pi].reshape(8, 128, 16))
    d["m_gateb"] = C(np.broadcast_to(np.asarray(inputs["L1_ffn_gate_b"]), (128, 16)))
    d["m_we1"] = C(np.asarray(inputs["L1_ffn_We1"])[:, pi, :].reshape(16, 8, 128, 512))
    d["m_be1"] = C(np.asarray(inputs["L1_ffn_be1"]).reshape(16, 4, 128)
                   .transpose(0, 2, 1))
    d["m_we3"] = C(np.asarray(inputs["L1_ffn_We3"])[:, pi, :].reshape(16, 8, 128, 512))
    d["m_be3"] = C(np.asarray(inputs["L1_ffn_be3"]).reshape(16, 4, 128)
                   .transpose(0, 2, 1))
    d["m_we2"] = C(np.asarray(inputs["L1_ffn_We2"])[:, :, pi].reshape(16, 4, 128, D))
    d["m_be2"] = C(np.asarray(inputs["L1_ffn_be2"])[:, pi])
    d["m_ws1"] = C(np.asarray(inputs["L1_ffn_ws1"])[pi].reshape(8, 128, 1024))
    d["m_ws3"] = C(np.asarray(inputs["L1_ffn_ws3"])[pi].reshape(8, 128, 1024))
    d["m_ws2"] = C(np.asarray(inputs["L1_ffn_ws2"])[:, pi].reshape(8, 128, D))

    bp = np.zeros((128, 64), np.float32)
    bp[:, 0:8] = np.asarray(inputs["L0_attn_br"])[pi].reshape(8, 128).T
    bp[:, 8:16] = np.asarray(inputs["L0_ffn_b2"])[pi].reshape(8, 128).T
    bp[:, 16:24] = np.asarray(inputs["L1_attn_bout"])[pi].reshape(8, 128).T
    bp[:, 24:32] = np.asarray(inputs["L1_ffn_bs2"])[pi].reshape(8, 128).T
    bp[:, 32:48] = bq1[0:2048].reshape(16, 128).T
    bp[:, 48:56] = np.asarray(inputs["L1_ffn_bs1"]).reshape(8, 128).T
    bp[:, 56:64] = np.asarray(inputs["L1_ffn_bs3"]).reshape(8, 128).T
    d["b_pack"] = C(bp)

    d["c_ident"] = np.eye(128, dtype=np.float32)
    jj = np.arange(128)
    d["c_tril"] = C((jj[None, :] < jj[:, None]).astype(np.float32))
    d["c_triuneg"] = C(np.where(jj[None, :] > jj[:, None], NEG, 0.0)
                       .astype(np.float32))
    PI_ = np.array([0, 0, 0, 1, 1, 2])
    PJ_ = np.array([1, 2, 3, 2, 3, 3])
    spi_ = np.zeros((64, 96), np.float32)
    spj_ = np.zeros((64, 96), np.float32)
    for h in range(16):
        for mth in range(6):
            spi_[4 * h + PI_[mth], 6 * h + mth] = 1.0
            spj_[4 * h + PJ_[mth], 6 * h + mth] = 1.0
    d["c_spi"] = spi_
    d["c_spj"] = spj_
    g_ = np.zeros((96, 16), np.float32)
    gt_ = np.zeros((16, 96), np.float32)
    for h in range(16):
        g_[6 * h:6 * h + 6, h] = 1.0
        gt_[h, 6 * h:6 * h + 6] = 1.0
    d["c_g96"] = g_
    d["c_gt96"] = gt_
    J6 = np.array([[0, 0, 0, 0, 0, 1], [0, 0, 0, 0, -1, 0], [0, 0, 0, 1, 0, 0],
                   [0, 0, 1, 0, 0, 0], [0, -1, 0, 0, 0, 0], [1, 0, 0, 0, 0, 0]],
                  np.float32)
    bj_ = np.zeros((96, 96), np.float32)
    for h in range(16):
        bj_[6 * h:6 * h + 6, 6 * h:6 * h + 6] = J6.T
    d["c_bj"] = bj_
    d["c_ones1"] = np.ones((1, 128), np.float32)
    d["c_onesd"] = np.ones((128, 1), np.float32)
    jj32 = np.arange(32)
    d["c_u32"] = C((jj32[:, None] > jj32[None, :]).astype(np.float32))
    d["c_eps"] = np.array([[0.0, 1e-5]], np.float32)
    dd = np.arange(64)
    jj8 = np.arange(8)
    qi0 = np.zeros((64, 32), np.uint32)
    ki0 = np.zeros((64, 32), np.uint32)
    qi1 = np.zeros((70, 32), np.uint32)
    ki1 = np.zeros((70, 32), np.uint32)
    for j in range(8):
        h = 8 * r + j
        for s in range(2):
            qi0[:, 2 * j + s] = s * 2048 + 64 * h + dd
            ki0[:, 2 * j + s] = s * 2048 + 1024 + 64 * h + dd
            qi1[0:64, 2 * j + s] = s * 2240 + 64 * h + dd
            qi1[64:70, 2 * j + s] = s * 2240 + 2048 + 6 * h + np.arange(6)
            ki1[0:64, 2 * j + s] = s * 2240 + 1024 + 64 * h + dd
            ki1[64:70, 2 * j + s] = s * 2240 + 2144 + 6 * h + np.arange(6)
    d["c_qi0"] = qi0
    d["c_ki0"] = ki0
    d["c_qi1"] = qi1
    d["c_ki1"] = ki1
    vi = np.zeros((128, 16), np.uint32)
    pp = np.arange(128)
    for j in range(8):
        h = 8 * r + j
        for s in range(2):
            vi[:, j * 2 + s] = (s * 16 + h) * 128 + pp
    d["c_vi"] = vi
    sel = np.zeros((16, 16, 128), np.float32)
    for e_ in range(16):
        sel[e_, e_, :] = 1.0
    d["c_sel16"] = C(sel.transpose(1, 0, 2).reshape(16, 2048))
    d["_pi"] = pi
    return d


def kernel(**inputs):
    nc = build_nc()
    in_maps = []
    pis = []
    for core in range(8):
        d = _prep_inputs(inputs, core)
        pis.append(d.pop("_pi"))
        in_maps.append(d)
    res = _bu.run_bass_kernel_spmd(nc, in_maps, core_ids=list(range(8)))
    y = np.empty((4, T, D), np.float32)
    for core in range(8):
        b, r = core // 2, core % 2
        o = np.asarray(res.results[core]["out"]).reshape(1024, TH)
        yo = np.empty_like(o)
        yo[pis[core], :] = o
        y[b, r * TH:(r + 1) * TH, :] = yo.T
    return y
